# version 21
# baseline (speedup 1.0000x reference)
import numpy as np
import ml_dtypes
from types import SimpleNamespace
import concourse.bass as bass
import concourse.mybir as mybir
from concourse.bass_utils import run_bass_kernel_spmd

F32 = mybir.dt.float32
BF16 = mybir.dt.bfloat16
AF = mybir.ActivationFunctionType
ALU = mybir.AluOpType
AX = mybir.AxisListType
BF = ml_dtypes.bfloat16

LAM = 0.125
EPS = 1e-5

KINDS = ("sd", "gd", "g", "t", "v", "s")
ENGINE_OF = {"sd": "sync", "gd": "gpsimd", "g": "gpsimd",
             "t": "tensor", "v": "vector", "s": "scalar"}
ENGINES = ("sync", "gpsimd", "tensor", "vector", "scalar")


def dims_real():
    return SimpleNamespace(C=1024, H=16, TPC=2048, G=4, NB=2, E=8, R=64)


def dd(d):
    d.D = 64
    d.HP = d.H // 2
    d.CH = d.C // 128
    d.NT = d.TPC // 128
    d.NTC = d.TPC // 512
    d.L = d.G * d.TPC
    d.KT = d.L // 128
    d.NTB = d.L // 128
    d.GB = d.L // 512
    d.F3 = 3 * d.C
    d.FC = min(512, d.C)
    d.NFC = d.C // d.FC
    d.ER = d.E * d.R
    d.ERCH = d.ER // 128
    d.EPC = 128 // d.R
    assert d.CH == d.HP, "cls q0 block-diag assumes CH == HP"
    return d


class Prog:

    want_oplog = False

    def __init__(self):
        self.ops = []
        self.cum = {}
        self.last_stream = {}
        self.oplog = []

    def add(self, kind, fn, deps=(), stream=None):
        if kind in ("sd", "gd"):
            assert stream is not None
            semkey = "D_" + stream
            inc = 16
            extra = (self.last_stream.get(stream),)
        else:
            semkey = kind
            inc = 1
            extra = ()
        self.cum[semkey] = self.cum.get(semkey, 0) + inc
        op = SimpleNamespace(kind=kind, fn=fn, inc=inc, semkey=semkey,
                             deps=tuple(x for x in (tuple(deps) + extra)
                                        if x is not None),
                             scum=self.cum[semkey],
                             line=fn.__code__.co_firstlineno)
        self.ops.append(op)
        if stream is not None:
            self.last_stream[stream] = op
        return op

    def emit(self, nc):
        Prog.last = self
        import contextlib
        with contextlib.ExitStack() as st:
            sems = {k: st.enter_context(nc.semaphore(name=f"sm_{k}"))
                    for k in self.cum}
            blk = st.enter_context(nc.Block())
            for ename in ENGINES:
                kinds_here = tuple(k for k in KINDS if ENGINE_OF[k] == ename)

                def emit_engine(eng, ename=ename, kinds_here=kinds_here):
                    hw = {}
                    for op in self.ops:
                        if op.kind not in kinds_here:
                            continue
                        need = {}
                        for dp in op.deps:
                            if dp.semkey == op.semkey and op.kind == "t" \
                                    and dp.kind == "t":
                                continue
                            need[dp.semkey] = max(need.get(dp.semkey, 0),
                                                  dp.scum)
                        for k, v in need.items():
                            if v > 0 and v > hw.get(k, 0):
                                eng.wait_ge(sems[k], v)
                                hw[k] = v
                        _inst = op.fn(eng)
                        _inst.then_inc(sems[op.semkey], op.inc)
                        if Prog.want_oplog:
                            self.oplog.append(
                                (nc.next_id(), op.line, op.kind))
                    if ename == "gpsimd":
                        for k, v in self.cum.items():
                            eng.wait_ge(sems[k], v)
                getattr(blk, ename)(emit_engine)
        return nc


def ln_pass1(P, d, src_ap, mean_all, var_all, t, dep_src, sc):
    if t % 2 == 0:
        o1 = P.add("s", lambda e, a=src_ap: e.activation(
            out=sc["jnk"][:], in_=a, func=AF.Copy, bias=0.0, scale=1.0,
            accum_out=sc["s1"][:, 0:1]),
            deps=list(dep_src) + [sc.get("prev_o2"), sc.get("prev_o3")])
        o3 = P.add("s", lambda e, a=src_ap: e.activation(
            out=sc["jnk"][:], in_=a, func=AF.Square, bias=sc["z"], scale=1.0,
            accum_out=sc["s2"][:, 0:1]),
            deps=list(dep_src) + [sc["oz"], sc.get("prev_o5"), o1])
        sc["prev_o3"] = o3
        c1, c2 = sc["s1"][:, 0:1], sc["s2"][:, 0:1]
    else:
        o1 = P.add("v", lambda e, a=src_ap: e.reduce_sum(
            sc["s1"][:, 1:2], a, axis=AX.X),
            deps=list(dep_src) + [sc.get("prev_o2v")])
        osq = P.add("v", lambda e, a=src_ap: e.tensor_tensor(
            out=sc["sqv"][:], in0=a, in1=a, op=ALU.mult),
            deps=list(dep_src) + [sc.get("prev_o5v")])
        o3 = P.add("v", lambda e: e.reduce_sum(
            sc["s2"][:, 2:3], sc["sqv"][:], axis=AX.X), deps=[osq])
        c1, c2 = sc["s1"][:, 1:2], sc["s2"][:, 2:3]
    o2 = P.add("v", lambda e, t=t, c1=c1: e.tensor_scalar(
        out=mean_all[:, t:t + 1], in0=c1, scalar1=1.0 / d.C,
        scalar2=None, op0=ALU.mult), deps=[o1])
    o5 = P.add("v", lambda e, c2=c2: e.tensor_scalar(
        out=sc["s2"][:, 1:2], in0=c2, scalar1=1.0 / d.C,
        scalar2=None, op0=ALU.mult), deps=[o3, sc.get("prev_o7")])
    o6 = P.add("v", lambda e, t=t: e.tensor_tensor(
        out=sc["s6"][:, 0:1], in0=mean_all[:, t:t + 1],
        in1=mean_all[:, t:t + 1], op=ALU.mult),
        deps=[o2, sc.get("prev_o7")])
    o7 = P.add("v", lambda e, t=t: e.tensor_tensor(
        out=var_all[:, t:t + 1], in0=sc["s2"][:, 1:2], in1=sc["s6"][:, 0:1],
        op=ALU.subtract), deps=[o5, o6])
    if t % 2 == 0:
        sc["prev_o2"] = o2
        sc["prev_o5"] = o5
    else:
        sc["prev_o2v"] = o2
        sc["prev_o5v"] = o3
    sc["prev_o7"] = o7
    return o7


def ln_finish(P, d, mean_all, var_all, rstd_all, mur_all, dep, cst, ocst):
    oL = P.add("s", lambda e: e.activation(
        out=var_all[:], in_=var_all[:], func=AF.Ln, bias=cst[:, 0:1],
        scale=1.0), deps=list(dep) + [ocst])
    oE = P.add("s", lambda e: e.activation(
        out=rstd_all[:], in_=var_all[:], func=AF.Exp, bias=cst[:, 1:2],
        scale=-0.5), deps=[oL])
    oM = P.add("v", lambda e: e.tensor_tensor(
        out=mur_all[:], in0=mean_all[:], in1=rstd_all[:], op=ALU.mult),
        deps=[oE])
    return oE, oM


def build_fused(d):
    nc = bass.Bass("TRN2", target_bir_lowering=False, debug=False)
    P = Prog()

    xbt = nc.dram_tensor("xb", [d.L, d.C], F32, kind="ExternalInput").ap()
    xst = nc.dram_tensor("xs", [d.TPC, d.C], F32, kind="ExternalInput").ap()
    wqkv = nc.dram_tensor("wqkv", [d.C, d.F3], F32, kind="ExternalInput").ap()
    cosK = nc.dram_tensor("cosK", [128, d.L], F32, kind="ExternalInput").ap()
    sinK = nc.dram_tensor("sinK", [128, d.L], F32, kind="ExternalInput").ap()
    cosQ = nc.dram_tensor("cosQ", [128, d.TPC], F32, kind="ExternalInput").ap()
    sinQ = nc.dram_tensor("sinQ", [128, d.TPC], F32, kind="ExternalInput").ap()
    wproj = nc.dram_tensor("wproj", [d.C, d.C], F32, kind="ExternalInput").ap()
    bproj = nc.dram_tensor("bproj", [1, d.C], F32, kind="ExternalInput").ap()
    wr = nc.dram_tensor("wr", [d.E, d.C], F32, kind="ExternalInput").ap()
    aw = nc.dram_tensor("aw", [d.E, d.R, d.C], F32, kind="ExternalInput").ap()
    bw = nc.dram_tensor("bw", [d.E, d.C, d.R], F32, kind="ExternalInput").ap()
    sel = nc.dram_tensor("sel", [d.E, d.ERCH * 128], F32,
                         kind="ExternalInput").ap()
    idn = nc.dram_tensor("idn", [128, 128], BF16, kind="ExternalInput").ap()
    idf = nc.dram_tensor("idf", [128, 128], F32, kind="ExternalInput").ap()
    out = nc.dram_tensor("out", [d.TPC, d.C], F32, kind="ExternalOutput").ap()

    kscr = nc.dram_tensor("kscr", [d.HP, 128, d.L], BF16,
                          kind="Internal").ap()
    vscr = nc.dram_tensor("vscr", [d.HP, d.KT, 128, 128], BF16,
                          kind="Internal").ap()

    import contextlib
    with contextlib.ExitStack() as top:
        T = lambda *a: top.enter_context(nc.sbuf_tensor(*a))
        _qfm_cm = nc.sbuf_tensor("Q_fm", [128, d.HP * d.TPC], BF16,
                                 side="right")
        Q_fm = _qfm_cm.__enter__()
        wp_sb = T("wp_sb", [128, d.CH * d.C], BF16)
        bp_s = T("bp_s", [1, d.C], F32)
        idn_s = T("idn_s", [128, 128], BF16)
        idf_s = T("idf_s", [128, 128], F32)
        ones_bf = T("ones_bf", [128, 1], BF16)
        ones1f = T("ones1f", [1, 128], F32)
        cst = T("cst", [128, 2], F32)
        kn_s = T("kn_s", [1, d.H * d.GB], F32)
        qn_s = T("qn_s", [1, d.H * d.NTC], F32)
        Mb_s = T("Mb_s", [128, d.HP * d.NTC], F32)
        G_s = T("G_s", [128, d.ERCH], F32)
        xln0 = T("xln0", [128, d.CH], BF16)
        q0f = T("q0f", [128, d.HP], BF16)

        ld_wp = P.add("gd", lambda e: e.dma_start(
            out=wp_sb.rearrange("p (c f) -> p c f", c=d.CH),
            in_=wproj.rearrange("(c p) f -> p c f", p=128)), stream="wonce")
        ld_bp = P.add("sd", lambda e: e.dma_start(out=bp_s[:], in_=bproj),
                      stream="sonce")
        ld_id = P.add("sd", lambda e: e.dma_start(out=idn_s[:], in_=idn),
                      stream="sonce")
        ld_if = P.add("sd", lambda e: e.dma_start(out=idf_s[:], in_=idf),
                      stream="sonce")
        o_ones = P.add("g", lambda e: e.memset(ones_bf[:], 1.0))
        o_ones1 = P.add("g", lambda e: e.memset(ones1f[:], 1.0))
        oc1 = P.add("g", lambda e: e.memset(cst[:, 0:1], EPS))
        oc2 = P.add("g", lambda e: e.memset(cst[:, 1:2], 0.0))

        with contextlib.ExitStack() as p1:
            T1 = lambda *a: p1.enter_context(nc.sbuf_tensor(*a))
            wsb = T1("wsb", [128, d.CH * d.F3], BF16)
            x_t = [T1(f"x{i}", [128, d.C], F32) for i in range(3)]
            sc = {"s1": T1("s1", [128, 4], F32),
                  "jnk": T1("jnk", [128, d.C], BF16),
                  "s2": T1("s2", [128, 4], F32),
                  "sqv": T1("sqv", [128, d.C], F32),
                  "s6": T1("s6", [128, 4], F32),
                  "z": cst[0:128, 1:2], "oz": oc2}
            meanb = T1("meanb", [128, d.NTB], F32)
            varb = T1("varb", [128, d.NTB], F32)
            rstdb = T1("rstdb", [128, d.NTB], F32)
            murb = T1("murb", [128, d.NTB], F32)
            mean_o = T1("mean_o", [128, d.NT], F32)
            var_o = T1("var_o", [128, d.NT], F32)
            rstd_o = T1("rstd_o", [128, d.NT], F32)
            mur_o = T1("mur_o", [128, d.NT], F32)
            xlnb = [T1(f"xlnb{i}", [128, d.C], BF16) for i in range(4)]
            xfg = [T1(f"xfg{i}", [128, d.CH * 512], BF16) for i in range(3)]
            ctb = [T1(f"ctb{i}", [128, 512], F32) for i in range(4)]
            stb = [T1(f"stb{i}", [128, 512], F32) for i in range(4)]
            sh_b = [T1(f"sh_b{i}", [128, 512], F32) for i in range(4)]
            t1_b = [T1(f"t1_b{i}", [128, 512], F32) for i in range(4)]
            t2_b = [T1(f"t2_b{i}", [128, 512], F32) for i in range(4)]
            stage = [T1(f"stage{i}", [128, 512], BF16) for i in range(4)]
            qsq_b = [T1(f"qsq_b{i}", [128, 512], BF16) for i in range(5)]
            vstage = [T1(f"vstage{i}", [128, d.FC], BF16) for i in range(4)]

            SG0 = p1.enter_context(nc.psum_tensor("SG0", [128, 1024], F32))
            SG1 = p1.enter_context(nc.psum_tensor("SG1", [128, 1024], F32))
            PO = p1.enter_context(nc.psum_tensor("PO", [128, 512], F32))
            PD = p1.enter_context(nc.psum_tensor("PD", [64, 512], F32))
            PTX = [p1.enter_context(nc.psum_tensor(f"PTX{i}",
                                                   [128, 512], BF16))
                   for i in range(2)]

            ld_w = P.add("gd", lambda e: e.dma_start(
                out=wsb.rearrange("p (c f) -> p c f", c=d.CH),
                in_=wqkv.rearrange("(c p) f -> p c f", p=128)),
                stream="wonce")

            lnb_done = []
            for t in range(d.NTB):
                lx = P.add("sd", lambda e, t=t: e.dma_start(
                    out=x_t[t % 2][:], in_=xbt[t * 128:(t + 1) * 128, :]),
                    deps=[lnb_done[t - 2]] if t >= 2 else [],
                    stream=f"x1{t % 2}")
                lnb_done.append(ln_pass1(P, d, x_t[t % 2][:], meanb, varb,
                                         t, [lx], sc))
            oEb, oMb_ = ln_finish(P, d, meanb, varb, rstdb, murb,
                                  lnb_done, cst, oc2)

            lno_done = []
            for t in range(d.NT):
                lx = P.add("sd", lambda e, t=t: e.dma_start(
                    out=x_t[t % 2][:], in_=xst[t * 128:(t + 1) * 128, :]),
                    deps=[lno_done[t - 2] if t >= 2 else lnb_done[-1]],
                    stream=f"x1{t % 2}")
                lno_done.append(ln_pass1(P, d, x_t[t % 2][:], mean_o, var_o,
                                         t, [lx], sc))
            oEo, oMo = ln_finish(P, d, mean_o, var_o, rstd_o, mur_o,
                                 lno_done, cst, oc2)

            st8 = dict(pslot=[None] * 4, psi=0,
                       t12_reader=[None] * 4, t2_free=[None] * 4,
                       qsq_reader=[None] * 5,
                       normPD=None, normPO=None,
                       stage_readers=[[] for _ in range(4)],
                       vst_free=[[] for _ in range(4)],
                       xfg_users=[[], [], []], norm_pending=[],
                       xln_rd=[lno_done[-1]] * 4,
                       tp_evac=[None, None], tpb=0, si=0, vi=0,
                       gseq=0, ct_user=[None] * 4)

            def take_pslot():
                i = st8["psi"] % 4
                st8["psi"] += 1
                ten = (SG0, SG1)[i // 2]
                return i, ten[:, (i % 2) * 512:(i % 2) * 512 + 512]
            x0ops = []

            def emit_group(src_x, g, rstd_a, mur_a, oE_, oM_, save0):
                last = None
                gp = st8["gseq"] % 3
                users = st8["xfg_users"][gp]
                for tl in range(4):
                    t = g * 4 + tl
                    ti = (g * 4 + tl) % 3
                    lx = P.add("sd", lambda e, t=t, ti=ti, src_x=src_x:
                               e.dma_start(
                        out=x_t[ti][:],
                        in_=src_x[t * 128:(t + 1) * 128, :]),
                        deps=[st8["xln_rd"][ti]], stream=f"x2{ti % 2}")
                    oxl = P.add("v", lambda e, t=t, ti=ti, rstd_a=rstd_a,
                                mur_a=mur_a: e.tensor_scalar(
                        out=xlnb[ti][:], in0=x_t[ti][:],
                        scalar1=rstd_a[:, t:t + 1],
                        scalar2=mur_a[:, t:t + 1],
                        op0=ALU.mult, op1=ALU.subtract),
                        deps=[lx, oE_, oM_])
                    for b4 in range((d.CH + 3) // 4):
                        bank = st8["tpb"] % 2
                        st8["tpb"] += 1
                        otps = []
                        for j4 in range(min(4, d.CH - b4 * 4)):
                            c = b4 * 4 + j4
                            pt = PTX[bank][:, j4 * 128:(j4 + 1) * 128]
                            otp = P.add("t", lambda e, ti=ti, c=c, pt=pt:
                                        e.transpose(
                                pt, xlnb[ti][:, c * 128:(c + 1) * 128],
                                idn_s[:]),
                                deps=[oxl, ld_id,
                                      st8["tp_evac"][bank]] + users)
                            otps.append(otp)
                            st8["xln_rd"][ti] = otp
                        for j4 in range(min(4, d.CH - b4 * 4)):
                            c = b4 * 4 + j4
                            pt = PTX[bank][:, j4 * 128:(j4 + 1) * 128]
                            oev = P.add("s", lambda e, tl=tl, c=c, pt=pt,
                                        gp=gp: e.copy(
                                xfg[gp][:, c * 512 + tl * 128:
                                        c * 512 + (tl + 1) * 128], pt),
                                deps=[otps[-1]])
                            last = oev
                        st8["tp_evac"][bank] = last
                        if save0 and tl == 0:
                            for c in range(b4 * 4,
                                           min(d.CH, b4 * 4 + 4)):
                                osv = P.add("v", lambda e, c=c:
                                            e.tensor_copy(
                                    xln0[:, c:c + 1],
                                    xfg[0][:, c * 512:c * 512 + 1]),
                                    deps=[last])
                                x0ops.append(osv)
                                last = osv
                            st8["tp_evac"][bank] = last
                st8["xfg_users"][gp] = []
                st8["gseq"] += 1
                return last, gp

            def rope_block(ps, mm, ct, st_, outsink, nbuf, ncol):
                si = st8["si"]
                pz = si % 4
                osh = P.add("v", lambda e, ps=ps, pz=pz: e.stream_shuffle(
                    sh_b[pz][:], ps, [i ^ 1 for i in range(32)]),
                    deps=[mm, st8["t2_free"][pz]])
                ot1 = P.add("v", lambda e, ps=ps, ct=ct, pz=pz:
                            e.tensor_tensor(
                    out=t1_b[pz][:], in0=ps, in1=ct, op=ALU.mult),
                    deps=[mm, st8["t12_reader"][pz]])
                ot2 = P.add("g", lambda e, st_=st_, pz=pz: e.tensor_tensor(
                    out=t2_b[pz][:], in0=sh_b[pz][:], in1=st_, op=ALU.mult),
                    deps=[osh, st8["t12_reader"][pz]])
                st8["t2_free"][pz] = ot2
                ost = P.add("v", lambda e, pz=pz: e.tensor_tensor(
                    out=stage[pz][:], in0=t1_b[pz][:], in1=t2_b[pz][:],
                    op=ALU.add),
                    deps=[ot1, ot2] + st8["stage_readers"][pz])
                st8["t12_reader"][pz] = ost
                osink = outsink(stage[pz][:], [ost])
                qz = si % 5
                osq = P.add("v", lambda e, qz=qz, pz=pz: e.tensor_tensor(
                    out=qsq_b[qz][:], in0=stage[pz][:],
                    in1=stage[pz][:], op=ALU.mult),
                    deps=[ost, st8["qsq_reader"][qz]])
                st8["stage_readers"][pz] = [osq, osink]
                st8["norm_pending"].append((qz, osq, nbuf, ncol))
                flush_norm(4)
                st8["si"] = si + 1
                return osink

            def flush_norm(n_keep):
                while len(st8["norm_pending"]) > n_keep:
                    pz, osq, nbuf, ncol = st8["norm_pending"].pop(0)
                    on1 = P.add("t", lambda e, pz=pz: e.matmul(
                        PD[0:1, :], ones_bf[0:64, :], qsq_b[pz][0:64, :],
                        start=True, stop=True, tile_position=(0, 0)),
                        deps=[osq, o_ones, st8["normPD"]])
                    on2 = P.add("t", lambda e, pz=pz: e.matmul(
                        PO[0:1, :], ones_bf[64:128, :], qsq_b[pz][64:128, :],
                        start=True, stop=True, tile_position=(64, 0)),
                        deps=[osq, o_ones, st8["normPO"]])
                    st8["qsq_reader"][pz] = on2
                    or1 = P.add("v", lambda e, nbuf=nbuf, n0=ncol[0]:
                                e.reduce_max(nbuf[0:1, n0:n0 + 1],
                                             PD[0:1, :], axis=AX.X),
                                deps=[on1])
                    or2 = P.add("v", lambda e, nbuf=nbuf, n1=ncol[1]:
                                e.reduce_max(nbuf[0:1, n1:n1 + 1],
                                             PO[0:1, :], axis=AX.X),
                                deps=[on2])
                    st8["normPD"] = or1
                    st8["normPO"] = or2

            ko_last = vo_last = None
            vo_lasts = [None, None]
            ko_lasts = [None, None]
            for g in range(d.GB):
                ogrp, gp = emit_group(xbt, g, rstdb, murb, oEb, oMb_, g == 0)
                lct = P.add("sd", lambda e, g=g: e.dma_start(
                    out=ctb[g % 4][:], in_=cosK[:, g * 512:(g + 1) * 512]),
                    deps=[st8["ct_user"][g % 4]], stream="ct")
                lst = P.add("sd", lambda e, g=g: e.dma_start(
                    out=stb[g % 4][:], in_=sinK[:, g * 512:(g + 1) * 512]),
                    deps=[st8["ct_user"][g % 4]], stream="st")
                for hp in range(d.HP):
                    slot, ps = take_pslot()
                    mm = None
                    for c in range(d.CH):
                        mm = P.add("t", lambda e, c=c, hp=hp, g=g, ps=ps,
                                   gp=gp: e.matmul(
                            ps, wsb[:, c * d.F3 + d.C + hp * 128:
                                    c * d.F3 + d.C + (hp + 1) * 128],
                            xfg[gp][:, c * 512:(c + 1) * 512],
                            start=(c == 0), stop=(c == d.CH - 1)),
                            deps=[ogrp, ld_w, st8["pslot"][slot], lct, lst])
                    st8["xfg_users"][gp].append(mm)

                    def ksink(stage_ap, dps, hp=hp, g=g,
                              kp=st8["si"] % 2):
                        return P.add("gd", lambda e: e.dma_start(
                            out=kscr[hp, :, g * 512:(g + 1) * 512],
                            in_=stage_ap), deps=dps, stream=f"ko{kp}")
                    ncol = ((hp * 2 + 0) * d.GB + g, (hp * 2 + 1) * d.GB + g)
                    osink = rope_block(ps, mm, ctb[g % 4][:], stb[g % 4][:],
                                       ksink, kn_s, ncol)
                    ko_lasts[(st8["si"] - 1) % 2] = osink
                    ko_last = osink
                    st8["pslot"][slot] = st8["t12_reader"][(st8["si"] - 1) % 4]
                    st8["ct_user"][g % 4] = st8["t12_reader"][(st8["si"] - 1) % 4]
                for tl in range(4):
                    vsl = [take_pslot() for _ in range(d.NFC)]
                    mmvs = [None] * d.NFC
                    for c in range(d.CH):
                        for fc in range(d.NFC):
                            vslot, psv0 = vsl[fc]
                            psv = psv0[:, 0:d.FC]
                            mmvs[fc] = P.add("t", lambda e, c=c, tl=tl,
                                             fc=fc, g=g, psv=psv, gp=gp:
                                             e.matmul(
                                psv,
                                xfg[gp][:, c * 512 + tl * 128:
                                        c * 512 + (tl + 1) * 128],
                                wsb[:, c * d.F3 + 2 * d.C + fc * d.FC:
                                    c * d.F3 + 2 * d.C + (fc + 1) * d.FC],
                                start=(c == 0), stop=(c == d.CH - 1)),
                                deps=[ogrp, ld_w,
                                      st8["pslot"][vsl[fc][0]]]
                                if c == 0 else [ogrp, ld_w])
                    for fc in range(d.NFC):
                        vslot, psv0 = vsl[fc]
                        psv = psv0[:, 0:d.FC]
                        mmv = mmvs[fc]
                        st8["xfg_users"][gp].append(mmv)
                        vi = st8["vi"]
                        oev = P.add("s", lambda e, psv=psv, vi=vi:
                                    e.copy(vstage[vi % 4][:], psv),
                                    deps=[mmv] + st8["vst_free"][vi % 4])
                        st8["pslot"][vslot] = oev
                        hp0 = (fc * d.FC) // 128
                        nh = d.FC // 128
                        kt = g * 4 + tl
                        od = P.add("gd", lambda e, hp0=hp0, nh=nh, kt=kt,
                                   vi=vi: e.dma_start(
                            out=vscr[hp0:hp0 + nh, kt, :, :].rearrange(
                                "h p dd -> p h dd"),
                            in_=vstage[vi % 4].rearrange(
                                "p (h dd) -> p h dd", h=nh)),
                            deps=[oev], stream=f"vo{vi % 2}")
                        st8["vst_free"][vi % 4] = [od]
                        vo_lasts[vi % 2] = od
                        vo_last = od
                        st8["vi"] += 1

            q_done = []
            for qg in range(d.NTC):
                ogrp, gp = emit_group(xst, qg, rstd_o, mur_o, oEo, oMo, False)
                gpar = (d.GB + qg) % 4
                lct = P.add("sd", lambda e, qg=qg, gpar=gpar: e.dma_start(
                    out=ctb[gpar][:], in_=cosQ[:, qg * 512:(qg + 1) * 512]),
                    deps=[st8["ct_user"][gpar]], stream="ct")
                lst = P.add("sd", lambda e, qg=qg, gpar=gpar: e.dma_start(
                    out=stb[gpar][:], in_=sinQ[:, qg * 512:(qg + 1) * 512]),
                    deps=[st8["ct_user"][gpar]], stream="st")
                for hp in range(d.HP):
                    slot, ps = take_pslot()
                    mm = None
                    for c in range(d.CH):
                        mm = P.add("t", lambda e, c=c, hp=hp, gpar=gpar,
                                   ps=ps, gp=gp: e.matmul(
                            ps, wsb[:, c * d.F3 + hp * 128:
                                    c * d.F3 + (hp + 1) * 128],
                            xfg[gp][:, c * 512:(c + 1) * 512],
                            start=(c == 0), stop=(c == d.CH - 1)),
                            deps=[ogrp, ld_w, st8["pslot"][slot], lct, lst])
                    st8["xfg_users"][gp].append(mm)

                    def qsink(stage_ap, dps, hp=hp, qg=qg):
                        return P.add("s", lambda e: e.copy(
                            Q_fm[:, hp * d.TPC + qg * 512:
                                 hp * d.TPC + (qg + 1) * 512], stage_ap),
                            deps=dps)
                    ncol = ((2 * hp) * d.NTC + qg, (2 * hp + 1) * d.NTC + qg)
                    osink = rope_block(ps, mm, ctb[gpar][:], stb[gpar][:],
                                       qsink, qn_s, ncol)
                    q_done.append(osink)
                    st8["pslot"][slot] = st8["t12_reader"][(st8["si"] - 1) % 4]
                    st8["ct_user"][gpar] = st8["t12_reader"][(st8["si"] - 1) % 4]

            flush_norm(0)
            oq0m = None
            for hp in range(d.HP):
                for c in range(d.CH):
                    oq0m = P.add("t", lambda e, hp=hp, c=c: e.matmul(
                        SG1[:, hp:hp + 1],
                        wsb[:, c * d.F3 + hp * 128:c * d.F3 + (hp + 1) * 128],
                        xln0[:, c:c + 1],
                        start=(c == 0), stop=(c == d.CH - 1)),
                        deps=[x0ops[-1], ld_w] + list(st8["pslot"]))
            oq0c = P.add("v", lambda e: e.tensor_copy(
                q0f[:], SG1[:, 0:d.HP]), deps=[oq0m])
            p1_t_tail = oq0m
            p1_v_tail = oq0c
            or_last = st8["normPO"]

        _ofm_cm = nc.sbuf_tensor("ofm", [128, d.HP * d.TPC], BF16)
        ofm = _ofm_cm.__enter__()
        with contextlib.ExitStack() as p2:
            T2 = lambda *a: p2.enter_context(nc.sbuf_tensor(*a))
            kbuf = T2("kbuf", [128, d.L], BF16)
            vbuf = T2("vbuf", [128, d.KT * 128], BF16)
            PTb = [T2(f"PTb{i}", [128, 1024], BF16) for i in range(4)]
            rd_s = T2("rd_s", [1, 1024], F32)
            rb_s = T2("rb_s", [128, 1024], F32)
            kn8 = T2("kn8", [1, d.HP], F32)
            qn32 = T2("qn32", [1, d.HP * d.NTC], F32)
            t32 = T2("t32", [1, d.HP * d.NTC], F32)

            ASG = [p2.enter_context(nc.psum_tensor(f"ASG{i}", [128, 1024],
                                                   F32)) for i in range(3)]
            PA = p2.enter_context(nc.psum_tensor("PA", [128, 512], F32))
            PB = p2.enter_context(nc.psum_tensor("PB", [128, 512], F32))

            okn = P.add("v", lambda e: e.reduce_max(
                kn8.rearrange("p (a b) -> p a b", b=1),
                kn_s.rearrange("p (a b) -> p a b", b=2 * d.GB),
                axis=AX.X), deps=[or_last, p1_t_tail])
            oqn = None
            for hp in range(d.HP):
                for qc in range(d.NTC):
                    oqn = P.add("v", lambda e, hp=hp, qc=qc: e.tensor_tensor(
                        out=qn32[0:1, hp * d.NTC + qc:hp * d.NTC + qc + 1],
                        in0=qn_s[0:1, (2 * hp) * d.NTC + qc:
                                 (2 * hp) * d.NTC + qc + 1],
                        in1=qn_s[0:1, (2 * hp + 1) * d.NTC + qc:
                                 (2 * hp + 1) * d.NTC + qc + 1],
                        op=ALU.max), deps=[or_last, oqn])
            ot32 = None
            for hp in range(d.HP):
                ot32 = P.add("v", lambda e, hp=hp: e.tensor_scalar(
                    out=t32[0:1, hp * d.NTC:(hp + 1) * d.NTC],
                    in0=qn32[0:1, hp * d.NTC:(hp + 1) * d.NTC],
                    scalar1=kn8[0:1, hp:hp + 1], scalar2=None,
                    op0=ALU.mult), deps=[okn, oqn, ot32])
            oln = P.add("s", lambda e: e.activation(
                out=t32[:], in_=t32[:], func=AF.Ln, bias=cst[0:1, 1:2],
                scale=1.0), deps=[ot32, oc2])
            oex = P.add("s", lambda e: e.activation(
                out=t32[:], in_=t32[:], func=AF.Exp, bias=cst[0:1, 1:2],
                scale=0.5), deps=[oln])
            omu = P.add("s", lambda e: e.mul(t32[:], t32[:], -LAM),
                        deps=[oex])
            obc = P.add("t", lambda e: e.matmul(
                ASG[0][:, 0:d.HP * d.NTC], ones1f[:], t32[:],
                start=True, stop=True),
                deps=[omu, o_ones1, p1_t_tail, p1_v_tail])
            oMb = P.add("v", lambda e: e.tensor_copy(
                Mb_s[:], ASG[0][:, 0:d.HP * d.NTC]), deps=[obc])

            exp_ops = {}
            od_last = {}
            vb_loads = {}
            bc_par = [0]
            ev_mults = None
            ev_recipB = None
            bc_copies = oMb
            att_last_t = None
            pend_ev = [None]

            def flush_evac():
                nonlocal ev_mults, bc_copies, att_last_t
                if pend_ev[0] is None:
                    return
                p, odl, expl, orA, orB, col = pend_ev[0]
                pend_ev[0] = None
                obA = P.add("t", lambda e, p=p: e.matmul(
                    ASG[p][:, 0:512], ones1f[:], rd_s[0:1, 0:512],
                    start=True, stop=True),
                    deps=[orA, expl, o_ones1])
                obB = P.add("t", lambda e, p=p: e.matmul(
                    ASG[p][:, 512:1024], ones1f[:],
                    rd_s[0:1, 512:1024],
                    start=True, stop=True),
                    deps=[orB, expl])
                ocA = P.add("v", lambda e, p=p: e.tensor_copy(
                    rb_s[:, 0:512], ASG[p][:, 0:512]),
                    deps=[obA, ev_mults])
                ocB = P.add("v", lambda e, p=p: e.tensor_copy(
                    rb_s[:, 512:1024], ASG[p][:, 512:1024]),
                    deps=[obB, ev_mults])
                bc_par[0] = p
                bc_copies = ocB
                omA = P.add("v", lambda e, col=col: e.tensor_tensor(
                    out=ofm[0:64, col:col + 512], in0=PA[0:64, :],
                    in1=rb_s[0:64, 0:512], op=ALU.mult),
                    deps=[ocA, odl])
                omB = P.add("v", lambda e, col=col: e.tensor_tensor(
                    out=ofm[64:128, col:col + 512], in0=PB[64:128, :],
                    in1=rb_s[64:128, 512:1024], op=ALU.mult),
                    deps=[ocB])
                ev_mults = omB
                att_last_t = obB
            last_S = obc
            last_O = None
            it = 0
            for hp in range(d.HP):
                kload = P.add("sd", lambda e, hp=hp: e.dma_start(
                    out=kbuf[:], in_=kscr[hp]),
                    deps=[last_S, ko_lasts[0], ko_lasts[1], p1_v_tail],
                    stream="kb")
                vls = []
                for j8 in range(max(1, d.KT // 8)):
                    n8 = min(8, d.KT - j8 * 8)
                    vls.append(P.add("sd", lambda e, hp=hp, j8=j8, n8=n8:
                                     e.dma_start(
                        out=vbuf[:, j8 * 1024:j8 * 1024 + n8 * 128]
                        .rearrange("p (k dd) -> p k dd", k=n8),
                        in_=vscr[hp, j8 * 8:j8 * 8 + n8].rearrange(
                            "k p dd -> p k dd")),
                        deps=[last_O, vo_lasts[0], vo_lasts[1], obc,
                              p1_v_tail], stream="vb"))
                vb_loads[hp] = vls[-1]
                def emit_od(jt, hp2, qc2):
                    kt2 = jt % d.KT
                    oex2 = exp_ops[jt]
                    first, last = kt2 == 0, kt2 == d.KT - 1
                    ops = []
                    if first:
                        ops = ["oA", "oB", "dA", "dB"]
                    else:
                        ops = ["dA", "dB", "oA", "oB"]
                    res = {}
                    for nm in ops:
                        if nm == "oA":
                            res[nm] = P.add("t", lambda e, jt=jt, kt2=kt2:
                                            e.matmul(
                                PA[0:64, :],
                                vbuf[:, kt2 * 128:kt2 * 128 + 64],
                                PTb[jt % 4][:, 0:512],
                                start=first, stop=last,
                                tile_position=(0, 0)),
                                deps=[oex2, vb_loads[hp2],
                                      ev_mults if first else None])
                        elif nm == "oB":
                            res[nm] = P.add("t", lambda e, jt=jt, kt2=kt2:
                                            e.matmul(
                                PB[64:128, :],
                                vbuf[:, kt2 * 128 + 64:(kt2 + 1) * 128],
                                PTb[jt % 4][:, 512:1024],
                                start=first, stop=last,
                                tile_position=(0, 64)),
                                deps=[oex2, vb_loads[hp2],
                                      ev_mults if first else None])
                        elif nm == "dA":
                            res[nm] = P.add("t", lambda e, jt=jt, kt2=kt2:
                                            e.matmul(
                                PA[64:65, :], ones_bf[:],
                                PTb[jt % 4][:, 0:512],
                                start=first, stop=False,
                                tile_position=(0, 64),
                                skip_group_check=True),
                                deps=[oex2, o_ones,
                                      ev_recipB if first else None])
                        else:
                            res[nm] = P.add("t", lambda e, jt=jt, kt2=kt2:
                                            e.matmul(
                                PB[0:1, :], ones_bf[:],
                                PTb[jt % 4][:, 512:1024],
                                start=first, stop=False,
                                tile_position=(0, 0),
                                skip_group_check=True),
                                deps=[oex2, o_ones,
                                      ev_recipB if first else None])
                    od_last[jt] = res[ops[-1]]
                    return res["oB"] if not first else res["dB"]

                for qc in range(d.NTC):
                    for kt in range(d.KT):
                        sg = ASG[it % 3]
                        extra = bc_copies if (
                            kt < 3 and (it % 3) == bc_par[0]) or it == 0 \
                            else None
                        sA = P.add("t", lambda e, hp=hp, qc=qc, kt=kt, sg=sg:
                                   e.matmul(
                            sg[:, 0:512],
                            kbuf[0:64, kt * 128:(kt + 1) * 128],
                            Q_fm[0:64, hp * d.TPC + qc * 512:
                                 hp * d.TPC + (qc + 1) * 512],
                            start=True, stop=True, tile_position=(0, 0)),
                            deps=[kload, exp_ops.get(it - 3), extra]
                            + ([q_done[-1], p1_v_tail] if it == 0 else []))
                        sB = P.add("t", lambda e, hp=hp, qc=qc, kt=kt, sg=sg:
                                   e.matmul(
                            sg[:, 512:1024],
                            kbuf[64:128, kt * 128:(kt + 1) * 128],
                            Q_fm[64:128, hp * d.TPC + qc * 512:
                                 hp * d.TPC + (qc + 1) * 512],
                            start=True, stop=True, tile_position=(64, 0)),
                            deps=[kload, exp_ops.get(it - 3), extra])
                        last_S = sB
                        midx = hp * d.NTC + qc
                        oex2 = P.add("s", lambda e, it=it, midx=midx, sg=sg:
                                     e.activation(
                            out=PTb[it % 4][:], in_=sg[:], func=AF.Exp,
                            bias=Mb_s[:, midx:midx + 1], scale=LAM),
                            deps=[sB, oMb, od_last.get(it - 4)]
                            + ([ko_lasts[0], ko_lasts[1], vo_lasts[0],
                                vo_lasts[1]] if it == 0 else []))
                        exp_ops[it] = oex2
                        if kt == 0:
                            flush_evac()
                        if kt > 2:
                            last_O = emit_od(it - 3, hp, qc)
                        it += 1
                    last_O = emit_od(it - 3, hp, qc)
                    last_O = emit_od(it - 2, hp, qc)
                    last_O = emit_od(it - 1, hp, qc)
                    p = (it - 1) % 3
                    orA = P.add("v", lambda e: e.reciprocal(
                        rd_s[0:1, 0:512], PA[64:65, :]),
                        deps=[od_last[it - 1], bc_copies])
                    orB = P.add("v", lambda e: e.reciprocal(
                        rd_s[0:1, 512:1024], PB[0:1, :]),
                        deps=[od_last[it - 1]])
                    ev_recipB = orB
                    pend_ev[0] = (p, od_last[it - 1], exp_ops[it - 1],
                                  orA, orB, hp * d.TPC + qc * 512)
            flush_evac()
            att_t = att_last_t
            att_v = ev_mults
            att_s = exp_ops[it - 1]
        _qfm_cm.__exit__(None, None, None)

        with contextlib.ExitStack() as p3:
            T3 = lambda *a: p3.enter_context(nc.sbuf_tensor(*a))
            q0bd = T3("q0bd", [128, d.CH * d.H], BF16)
            kcb = [T3(f"kcb{i}", [128, d.CH * 512], BF16)
                   for i in range(2)]
            s0_s = T3("s0_s", [d.H, d.L], F32)
            p0_s = T3("p0_s", [d.H, d.L], BF16)
            p0T = T3("p0T", [128, d.KT * d.H], BF16)
            vb3 = [T3(f"vb3{i}", [128, d.KT * 128], BF16)
                   for i in range(2)]
            o0f = T3("o0f", [128, d.CH], BF16)
            m0 = T3("m0", [d.H, 4], F32)
            xp0_s = T3("xp0_s", [1, d.C], F32)
            g_scr = T3("g_scr", [1, d.C], F32)
            g_sc2 = T3("g_sc2", [1, 64], F32)
            cls_s = T3("cls_s", [128, d.CH], F32)
            wrT_s = T3("wrT_s", [128, d.CH * d.E], F32)
            gfm_s = T3("gfm_s", [d.E, 1], F32)
            wr_s = T3("wr_s", [d.E, d.C], F32)
            sel_s = T3("sel_s", [d.E, d.ERCH * 128], F32)

            CP = [p3.enter_context(nc.psum_tensor(f"CP{i}", [128, 512], F32))
                  for i in range(2)]
            CO = [p3.enter_context(nc.psum_tensor(f"CO{i}", [128, 16], F32))
                  for i in range(2)]
            CT = [p3.enter_context(nc.psum_tensor(f"CT{i}", [128, 16], BF16))
                  for i in range(2)]

            ld_wr = P.add("sd", lambda e: e.dma_start(out=wr_s[:], in_=wr),
                          deps=[att_t, att_v, att_s], stream="sonce")
            ld_sel = P.add("sd", lambda e: e.dma_start(
                out=sel_s[:], in_=sel), deps=[att_t, att_v, att_s],
                stream="sonce")
            lx0 = P.add("sd", lambda e: e.dma_start(
                out=xp0_s[:], in_=xbt[0:1, :]),
                deps=[att_t, att_v, att_s], stream="x0")
            o0prev = [None, None]
            lv3s = [None, None]
            for hp in range(2):
                for j8 in range(max(1, d.KT // 8)):
                    n8 = min(8, d.KT - j8 * 8)
                    lv3s[hp % 2] = P.add(
                        "sd", lambda e, hp=hp, j8=j8, n8=n8:
                        e.dma_start(
                            out=vb3[hp % 2][:, j8 * 1024:
                                            j8 * 1024 + n8 * 128]
                            .rearrange("p (k dd) -> p k dd", k=n8),
                            in_=vscr[hp, j8 * 8:j8 * 8 + n8].rearrange(
                                "k p dd -> p k dd")),
                        deps=[vo_lasts[0], vo_lasts[1],
                              att_t, att_v, att_s],
                        stream=f"vs{j8 % 4}")

            oz = P.add("g", lambda e: e.memset(q0bd[:], 0.0),
                       deps=[att_t, att_v, att_s])
            oqb = None
            for c in range(d.CH):
                oqb = P.add("v", lambda e, c=c: e.tensor_copy(
                    q0bd[0:64, c * d.H + 2 * c:c * d.H + 2 * c + 1],
                    q0f[0:64, c:c + 1]), deps=[oz, p1_v_tail, oqb])
                oqb = P.add("v", lambda e, c=c: e.tensor_copy(
                    q0bd[64:128, c * d.H + 2 * c + 1:c * d.H + 2 * c + 2],
                    q0f[64:128, c:c + 1]), deps=[oqb])
            os0 = []
            for ktc in range(d.GB):
                lk = P.add("sd", lambda e, ktc=ktc: e.dma_start(
                    out=kcb[ktc % 2].rearrange("p (h t) -> p h t", h=d.CH),
                    in_=kscr[:, :, ktc * 512:(ktc + 1) * 512].rearrange(
                        "h p t -> p h t")),
                    deps=[ko_lasts[0], ko_lasts[1], att_t, att_v, att_s]
                    + (os0[-2:-1] or []), stream=f"kc{ktc % 2}")
                mm = None
                for c in range(d.CH):
                    mm = P.add("t", lambda e, c=c, ktc=ktc: e.matmul(
                        CP[ktc % 2][0:d.H, :],
                        q0bd[:, c * d.H:(c + 1) * d.H],
                        kcb[ktc % 2][:, c * 512:(c + 1) * 512],
                        start=(c == 0), stop=(c == d.CH - 1)),
                        deps=[oqb, lk, att_v, os0[-2] if len(os0) >= 2
                              else None])
                oc_ = P.add("v", lambda e, ktc=ktc: e.tensor_copy(
                    s0_s[:, ktc * 512:(ktc + 1) * 512], CP[ktc % 2][0:d.H, :]),
                    deps=[mm, att_v])
                os0.append(oc_)
            om0 = P.add("v", lambda e: e.reduce_max(
                m0[:, 0:1], s0_s[:], axis=AX.X), deps=[os0[-1]])
            om0n = P.add("v", lambda e: e.tensor_scalar(
                out=m0[:, 1:2], in0=m0[:, 0:1], scalar1=-LAM, scalar2=None,
                op0=ALU.mult), deps=[om0])
            oe0 = P.add("s", lambda e: e.activation(
                out=p0_s[:], in_=s0_s[:], func=AF.Exp, bias=m0[:, 1:2],
                scale=LAM, accum_out=m0[:, 2:3]), deps=[om0n, os0[-1]])
            or0 = P.add("v", lambda e: e.reciprocal(m0[:, 3:4], m0[:, 2:3]),
                        deps=[oe0])
            op0n = P.add("v", lambda e: e.tensor_scalar(
                out=p0_s[:], in0=p0_s[:], scalar1=m0[:, 3:4], scalar2=None,
                op0=ALU.mult), deps=[or0, oe0])
            otp_prev = [None, None]
            op0T = None
            for kt in range(d.KT):
                ct_ = CT[kt % 2]
                otp = P.add("t", lambda e, kt=kt, ct_=ct_: e.transpose(
                    ct_[:, 0:d.H],
                    p0_s[:, kt * 128:(kt + 1) * 128], idn_s[0:d.H, 0:d.H]),
                    deps=[op0n, ld_id, otp_prev[kt % 2]])
                oev = P.add("v", lambda e, kt=kt, ct_=ct_: e.tensor_copy(
                    p0T[:, kt * d.H:(kt + 1) * d.H], ct_[:, 0:d.H]),
                    deps=[otp])
                otp_prev[kt % 2] = oev
                op0T = oev
            o0evs = []
            for hp in range(d.HP):
                if hp >= 2:
                    for j8 in range(max(1, d.KT // 8)):
                        n8 = min(8, d.KT - j8 * 8)
                        lv3s[hp % 2] = P.add(
                            "sd", lambda e, hp=hp, j8=j8, n8=n8:
                            e.dma_start(
                                out=vb3[hp % 2][:, j8 * 1024:
                                                j8 * 1024 + n8 * 128]
                                .rearrange("p (k dd) -> p k dd", k=n8),
                                in_=vscr[hp, j8 * 8:j8 * 8 + n8].rearrange(
                                    "k p dd -> p k dd")),
                            deps=[o0prev[hp % 2], vo_lasts[0], vo_lasts[1]],
                            stream=f"vs{j8 % 4}")
                co = CO[hp % 2]
                mm = None
                for kt in range(d.KT):
                    mm = P.add("t", lambda e, kt=kt, co=co, hp=hp: e.matmul(
                        co[:, 0:d.H], vb3[hp % 2][:, kt * 128:(kt + 1) * 128],
                        p0T[:, kt * d.H:(kt + 1) * d.H],
                        start=(kt == 0), stop=(kt == d.KT - 1)),
                        deps=[lv3s[hp % 2], op0T,
                              o0evs[-2] if len(o0evs) >= 2 else None])
                oe1 = P.add("v", lambda e, hp=hp, co=co: e.tensor_copy(
                    o0f[0:64, hp:hp + 1], co[0:64, 2 * hp:2 * hp + 1]),
                    deps=[mm])
                oe2 = P.add("v", lambda e, hp=hp, co=co: e.tensor_copy(
                    o0f[64:128, hp:hp + 1],
                    co[64:128, 2 * hp + 1:2 * hp + 2]), deps=[oe1])
                o0evs.append(oe2)
                o0prev[hp % 2] = mm
            oxp0 = []
            for fc in range(d.NFC):
                mm = None
                for c in range(d.CH):
                    mm = P.add("t", lambda e, c=c, fc=fc: e.matmul(
                        CP[0][0:1, 0:d.FC],
                        o0f[:, c:c + 1],
                        wp_sb[:, c * d.C + fc * d.FC:c * d.C + (fc + 1) * d.FC],
                        start=(c == 0), stop=False),
                        deps=[o0evs[-1], ld_wp,
                              oxp0[-1] if oxp0 else os0[-1]])
                mb = P.add("t", lambda e, fc=fc: e.matmul(
                    CP[0][0:1, 0:d.FC], ones1f[0:1, 0:1],
                    bp_s[0:1, fc * d.FC:(fc + 1) * d.FC],
                    start=False, stop=True), deps=[ld_bp, o_ones1])
                oadd = P.add("v", lambda e, fc=fc: e.tensor_tensor(
                    out=xp0_s[0:1, fc * d.FC:(fc + 1) * d.FC],
                    in0=CP[0][0:1, 0:d.FC],
                    in1=xp0_s[0:1, fc * d.FC:(fc + 1) * d.FC],
                    op=ALU.add), deps=[mb, lx0])
                oxp0.append(oadd)
            og1 = P.add("v", lambda e: e.reduce_sum(
                g_sc2[0:1, 0:1], xp0_s[:], axis=AX.X), deps=[oxp0[-1]])
            og2 = P.add("v", lambda e: e.tensor_scalar(
                out=g_sc2[0:1, 1:2], in0=g_sc2[0:1, 0:1],
                scalar1=1.0 / d.C, scalar2=None, op0=ALU.mult), deps=[og1])
            og3 = P.add("v", lambda e: e.tensor_scalar(
                out=g_scr[:], in0=xp0_s[:], scalar1=g_sc2[0:1, 1:2],
                scalar2=None, op0=ALU.subtract), deps=[og2])
            og4 = P.add("v", lambda e: e.tensor_tensor(
                out=g_scr[:], in0=g_scr[:], in1=g_scr[:], op=ALU.mult),
                deps=[og3])
            og5 = P.add("v", lambda e: e.reduce_sum(
                g_sc2[0:1, 2:3], g_scr[:], axis=AX.X), deps=[og4])
            og6 = P.add("v", lambda e: e.tensor_scalar(
                out=g_sc2[0:1, 3:4], in0=g_sc2[0:1, 2:3],
                scalar1=1.0 / d.C, scalar2=None, op0=ALU.mult), deps=[og5])
            og7 = P.add("s", lambda e: e.activation(
                out=g_sc2[0:1, 4:5], in_=g_sc2[0:1, 3:4], func=AF.Ln,
                bias=cst[0:1, 0:1], scale=1.0), deps=[og6, oc1])
            og8 = P.add("s", lambda e: e.activation(
                out=g_sc2[0:1, 5:6], in_=g_sc2[0:1, 4:5], func=AF.Exp,
                bias=cst[0:1, 1:2], scale=-0.5), deps=[og7])
            og9 = P.add("v", lambda e: e.tensor_scalar(
                out=g_scr[:], in0=xp0_s[:], scalar1=g_sc2[0:1, 1:2],
                scalar2=g_sc2[0:1, 5:6], op0=ALU.subtract, op1=ALU.mult),
                deps=[og8, og5])
            ocs = []
            for c in range(d.CH):
                ocs.append(P.add("t", lambda e, c=c: e.transpose(
                    CP[1][:, c:c + 1], g_scr[0:1, c * 128:(c + 1) * 128],
                    idf_s[0:1, 0:1]), deps=[og9, ld_if, oxp0[-1]]))
            occ = P.add("v", lambda e: e.tensor_copy(
                cls_s[:], CP[1][:, 0:d.CH]), deps=ocs)
            ows = []
            for c in range(d.CH):
                ows.append(P.add("t", lambda e, c=c: e.transpose(
                    CP[1][:, 16 + c * d.E:16 + (c + 1) * d.E],
                    wr_s[:, c * 128:(c + 1) * 128], idf_s[0:d.E, 0:d.E]),
                    deps=[ld_wr, ld_if, occ if c == 0 else None]))
            owc = P.add("v", lambda e: e.tensor_copy(
                wrT_s[:], CP[1][:, 16:16 + d.CH * d.E]), deps=ows)
            olg = None
            for c in range(d.CH):
                olg = P.add("t", lambda e, c=c: e.matmul(
                    CP[0][0:1, 16:16 + d.E], cls_s[:, c:c + 1],
                    wrT_s[:, c * d.E:(c + 1) * d.E],
                    start=(c == 0), stop=(c == d.CH - 1)),
                    deps=[occ, owc, oxp0[-1]])
            olgc = P.add("v", lambda e: e.tensor_copy(
                g_sc2[0:1, 8:8 + d.E], CP[0][0:1, 16:16 + d.E]),
                deps=[olg])
            om8 = P.add("v", lambda e: e.max(
                g_sc2[0:1, 16:24], g_sc2[0:1, 8:8 + d.E]), deps=[olgc])
            okp = P.add("v", lambda e: e.tensor_scalar(
                out=g_sc2[0:1, 24:24 + d.E], in0=g_sc2[0:1, 8:8 + d.E],
                scalar1=g_sc2[0:1, 17:18], scalar2=None, op0=ALU.is_ge),
                deps=[om8])
            om1n = P.add("v", lambda e: e.tensor_scalar(
                out=g_sc2[0:1, 32:33], in0=g_sc2[0:1, 16:17], scalar1=-1.0,
                scalar2=None, op0=ALU.mult), deps=[om8])
            oen = P.add("s", lambda e: e.activation(
                out=g_sc2[0:1, 40:40 + d.E], in_=g_sc2[0:1, 8:8 + d.E],
                func=AF.Exp, bias=g_sc2[0:1, 32:33], scale=1.0),
                deps=[om1n, okp])
            ogu = P.add("v", lambda e: e.tensor_tensor(
                out=g_sc2[0:1, 48:48 + d.E], in0=g_sc2[0:1, 24:24 + d.E],
                in1=g_sc2[0:1, 40:40 + d.E], op=ALU.mult), deps=[oen])
            odn = P.add("v", lambda e: e.reduce_sum(
                g_sc2[0:1, 56:57], g_sc2[0:1, 48:48 + d.E], axis=AX.X),
                deps=[ogu])
            ord_ = P.add("v", lambda e: e.reciprocal(
                g_sc2[0:1, 57:58], g_sc2[0:1, 56:57]), deps=[odn])
            ogt = P.add("v", lambda e: e.tensor_scalar(
                out=g_sc2[0:1, 48:48 + d.E], in0=g_sc2[0:1, 48:48 + d.E],
                scalar1=g_sc2[0:1, 57:58], scalar2=None, op0=ALU.mult),
                deps=[ord_])
            ogf = P.add("t", lambda e: e.transpose(
                CP[0][0:d.E, 32:33], g_sc2[0:1, 48:48 + d.E],
                idf_s[0:1, 0:1]), deps=[ogt, olg])
            ogfc = P.add("v", lambda e: e.tensor_copy(
                gfm_s[:], CP[0][0:d.E, 32:33]), deps=[ogf])
            oGs = []
            for ec in range(d.ERCH):
                oGs.append(P.add("t", lambda e, ec=ec: e.matmul(
                    CP[1][:, 48 + ec:48 + ec + 1],
                    sel_s[:, ec * 128:(ec + 1) * 128],
                    gfm_s[:], start=True, stop=True),
                    deps=[ogfc, ld_sel, owc]))
            oGc = P.add("v", lambda e: e.tensor_copy(
                G_s[:], CP[1][:, 48:48 + d.ERCH]), deps=oGs)
            p3_t_tail = oGs[-1]
            p3_v_tail = oGc

        _xpost_cm = nc.sbuf_tensor("xpost", [128, d.NT * d.C], F32,
                                   side="right")
        xpost = _xpost_cm.__enter__()
        with contextlib.ExitStack() as p4:
            T4 = lambda *a: p4.enter_context(nc.sbuf_tensor(*a))
            xs_t = [T4(f"xs4{i}", [128, d.C], F32) for i in range(4)]
            PP = [p4.enter_context(nc.psum_tensor(f"PP{i}", [128, 512],
                                                  F32)) for i in range(4)]
            xs_free = [None] * 4
            pslot_free = [p3_v_tail] * 4
            xpw = []
            for tt in range(d.NT):
                lx = P.add("sd", lambda e, tt=tt: e.dma_start(
                    out=xs_t[tt % 4][:], in_=xst[tt * 128:(tt + 1) * 128, :]),
                    deps=[xs_free[tt % 4], p3_t_tail, p3_v_tail],
                    stream=f"xp{tt % 2}")
                evs = []
                for fc in range(d.NFC):
                    slot = (tt * d.NFC + fc) % 4
                    ps = PP[slot]
                    mm = None
                    for c in range(d.CH):
                        mm = P.add("t", lambda e, c=c, tt=tt, fc=fc, ps=ps:
                                   e.matmul(
                            ps[:, 0:d.FC],
                            ofm[:, c * d.TPC + tt * 128:
                                c * d.TPC + (tt + 1) * 128],
                            wp_sb[:, c * d.C + fc * d.FC:
                                  c * d.C + (fc + 1) * d.FC],
                            start=(c == 0), stop=False),
                            deps=[att_v, ld_wp, pslot_free[slot],
                                  p3_t_tail])
                    mb = P.add("t", lambda e, fc=fc, ps=ps: e.matmul(
                        ps[:, 0:d.FC], ones1f[:],
                        bp_s[0:1, fc * d.FC:(fc + 1) * d.FC],
                        start=False, stop=True), deps=[ld_bp, o_ones1])
                    oev = P.add("v", lambda e, tt=tt, fc=fc, ps=ps:
                                e.tensor_tensor(
                        out=xpost[:, tt * d.C + fc * d.FC:
                                  tt * d.C + (fc + 1) * d.FC],
                        in0=ps[:, 0:d.FC],
                        in1=xs_t[tt % 4][:, fc * d.FC:(fc + 1) * d.FC],
                        op=ALU.add), deps=[mb, lx])
                    pslot_free[slot] = oev
                    evs.append(oev)
                xs_free[tt % 4] = evs[-1]
                xpw.append(evs[-1])

        _ofm_cm.__exit__(None, None, None)

        with contextlib.ExitStack() as p5:
            T5 = lambda *a: p5.enter_context(nc.sbuf_tensor(*a))
            xfm2 = T5("xfm2", [128, d.CH * d.TPC], BF16)
            xlnb2 = [T5(f"xlnb2{i}", [128, d.C], BF16) for i in range(2)]
            sc5 = {"s1": T5("s51", [128, 4], F32),
                   "jnk": T5("s5jnk", [128, d.C], BF16),
                   "s2": T5("s52", [128, 4], F32),
                   "sqv": T5("s5sqv", [128, d.C], F32),
                   "s6": T5("s56", [128, 4], F32),
                   "z": cst[0:128, 1:2], "oz": oc2}
            mean2 = T5("mean2", [128, d.NT], F32)
            var2 = T5("var2", [128, d.NT], F32)
            rstd2 = T5("rstd2", [128, d.NT], F32)
            mur2 = T5("mur2", [128, d.NT], F32)
            r1_2 = T5("r1_2", [128, d.NT], F32)
            _ab_cm = nc.sbuf_tensor("a_sb", [d.R, d.E * d.C], BF16,
                                    side="right")
            a_sb = _ab_cm.__enter__()
            _bb_cm = nc.sbuf_tensor("b_sb", [128, d.E * d.CH * d.R],
                                    BF16, side="right")
            b_sb = _bb_cm.__enter__()
            aT_s = T5("aT_s", [128, d.CH * d.ERCH * 128], BF16)
            bT_s = T5("bT_s", [128, d.ERCH * d.C], BF16)
            mid_s = T5("mid_s", [128, d.ERCH * d.TPC], BF16)
            xo = [T5(f"xo5{i}", [128, d.C], F32) for i in range(2)]
            sc1b = [T5(f"sc1b{i}", [128, d.FC], F32) for i in range(2)]
            sc2b = [T5(f"sc2b{i}", [128, d.FC], F32) for i in range(2)]

            FSG = [p5.enter_context(nc.psum_tensor(f"FSG{i}", [128, 1024],
                                                   F32)) for i in range(2)]
            PT5 = [p5.enter_context(nc.psum_tensor(f"PT5{i}", [128, 512],
                                                   BF16)) for i in range(2)]

            ld_a = P.add("gd", lambda e: e.dma_start(
                out=a_sb.rearrange("r (e h) -> r e h", e=d.E),
                in_=aw.rearrange("e r h -> r e h")),
                deps=[p3_t_tail, p3_v_tail, xpw[-1]], stream="ab")
            ld_bs = []
            for e_ in range(d.E):
                ld_bs.append(P.add("gd", lambda e, e_=e_: e.dma_start(
                    out=b_sb[:, e_ * d.CH * d.R:(e_ + 1) * d.CH * d.R]
                    .rearrange("p (c r) -> p c r", c=d.CH),
                    in_=bw[e_].rearrange("(c p) r -> p c r", p=128)),
                    stream="ab"))

            ln_done = []
            for t in range(d.NT):
                ln_done.append(ln_pass1(
                    P, d, xpost[:, t * d.C:(t + 1) * d.C], mean2, var2, t,
                    [xpw[t]] + ([p3_t_tail, xpw[-1]] if t == 0 else []),
                    sc5))
            oE2, oM2 = ln_finish(P, d, mean2, var2, rstd2, mur2, ln_done,
                                 cst, oc2)
            oR1 = P.add("v", lambda e: e.tensor_scalar(
                out=r1_2[:], in0=rstd2[:], scalar1=1.0, scalar2=None,
                op0=ALU.add), deps=[oE2])

            bank_ev = [None, None]
            xln_rd = [None, None]
            xfm_done_sv = [None, None]
            aT_done_sv = [None, None]
            bT_done_sv = [None, None]
            bsz = min(4, d.CH)
            nblk = d.CH // bsz
            bi = 0
            for t in range(d.NT):
                oxl = P.add("v", lambda e, t=t: e.tensor_scalar(
                    out=xlnb2[t % 2][:], in0=xpost[:, t * d.C:(t + 1) * d.C],
                    scalar1=rstd2[:, t:t + 1], scalar2=mur2[:, t:t + 1],
                    op0=ALU.mult, op1=ALU.subtract),
                    deps=[oE2, oM2, xln_rd[t % 2]])
                for blk in range(nblk):
                    hb = bi % 2
                    pt = PT5[hb]
                    last_tp = None
                    for j in range(bsz):
                        c = blk * bsz + j
                        last_tp = P.add("t", lambda e, t=t, c=c, j=j, pt=pt:
                                        e.transpose(
                            pt[:, j * 128:(j + 1) * 128],
                            xlnb2[t % 2][:, c * 128:(c + 1) * 128],
                            idn_s[:]),
                            deps=([oxl, ld_id, bank_ev[hb]] if j == 0
                                  else [oxl]))
                    dst = lambda t=t, blk=blk: xfm2.rearrange(
                        "p (c x) -> p c x", c=d.CH)[
                        :, blk * bsz:(blk + 1) * bsz,
                        t * 128:(t + 1) * 128]
                    src = lambda pt=pt: pt[:, 0:bsz * 128].rearrange(
                        "p (c x) -> p c x", x=128)
                    if bi % 2 == 0:
                        ev = P.add("s", lambda e, dst=dst, src=src:
                                   e.activation(
                            out=dst(), in_=src(), func=AF.Copy),
                            deps=[last_tp])
                    else:
                        ev = P.add("v", lambda e, dst=dst, src=src:
                                   e.tensor_copy(dst(), src()),
                            deps=[last_tp])
                    bank_ev[hb] = ev
                    xfm_done_sv[bi % 2] = ev
                    bi += 1
                xln_rd[t % 2] = last_tp

            for e_ in range(d.E):
                hb = bi % 2
                pt = PT5[hb]
                last_tp = None
                for c in range(d.CH):
                    last_tp = P.add("t", lambda e, e_=e_, c=c, pt=pt:
                                    e.transpose(
                        pt[:, c * d.R:(c + 1) * d.R],
                        a_sb[:, e_ * d.C + c * 128:e_ * d.C + (c + 1) * 128],
                        idn_s[0:d.R, 0:d.R]),
                        deps=[ld_a, ld_id] + ([bank_ev[hb]] if c == 0
                                              else []))
                off = (e_ // d.EPC) * 128 + (e_ % d.EPC) * d.R
                dst = lambda off=off: aT_s.rearrange(
                    "p (c x) -> p c x", x=d.ERCH * 128)[:, :, off:off + d.R]
                src = lambda pt=pt: pt[:, 0:d.CH * d.R].rearrange(
                    "p (c x) -> p c x", x=d.R)
                if bi % 2 == 0:
                    ev = P.add("s", lambda e, dst=dst, src=src: e.activation(
                        out=dst(), in_=src(), func=AF.Copy), deps=[last_tp])
                else:
                    ev = P.add("v", lambda e, dst=dst, src=src:
                               e.tensor_copy(dst(), src()), deps=[last_tp])
                bank_ev[hb] = ev
                aT_done_sv[bi % 2] = ev
                bi += 1
            for e_ in range(d.E):
                for blk in range(nblk):
                    hb = bi % 2
                    pt = PT5[hb]
                    last_tp = None
                    for j in range(bsz):
                        c = blk * bsz + j
                        last_tp = P.add("t", lambda e, e_=e_, c=c, j=j,
                                        pt=pt: e.transpose(
                            pt[0:d.R, j * 128:(j + 1) * 128],
                            b_sb[:, e_ * d.CH * d.R + c * d.R:
                                 e_ * d.CH * d.R + (c + 1) * d.R],
                            idn_s[:]),
                            deps=[ld_bs[e_], ld_id]
                            + ([bank_ev[hb]] if j == 0 else []))
                    row = (e_ % d.EPC) * d.R
                    col = (e_ // d.EPC) * d.C + blk * bsz * 128
                    w_ = bsz * 128
                    if bi % 2 == 0:
                        ev = P.add("s", lambda e, row=row, col=col, w_=w_,
                                   pt=pt: e.activation(
                            out=bT_s[row:row + d.R, col:col + w_],
                            in_=pt[0:d.R, 0:w_], func=AF.Copy),
                            deps=[last_tp])
                    else:
                        ev = P.add("v", lambda e, row=row, col=col, w_=w_,
                                   pt=pt: e.tensor_copy(
                            bT_s[row:row + d.R, col:col + w_],
                            pt[0:d.R, 0:w_]), deps=[last_tp])
                    bank_ev[hb] = ev
                    bT_done_sv[bi % 2] = ev
                    bi += 1

            _bb_cm.__exit__(None, None, None)
            _ab_cm.__exit__(None, None, None)
            slot_free = [bT_done_sv[0], aT_done_sv[0],
                         bT_done_sv[1], aT_done_sv[1]]
            mi = 0
            mid_done = None
            for ec in range(d.ERCH):
                for tc in range(d.NTC):
                    slot = mi % 4
                    ps = FSG[slot // 2][:, (slot % 2) * 512:
                                        (slot % 2) * 512 + 512]
                    mm = None
                    for c in range(d.CH):
                        mm = P.add("t", lambda e, ec=ec, tc=tc, c=c, ps=ps:
                                   e.matmul(
                            ps, aT_s[:, (c * d.ERCH + ec) * 128:
                                     (c * d.ERCH + ec + 1) * 128],
                            xfm2[:, c * d.TPC + tc * 512:
                                 c * d.TPC + (tc + 1) * 512],
                            start=(c == 0), stop=(c == d.CH - 1)),
                            deps=xfm_done_sv + aT_done_sv
                            + [slot_free[slot]])
                    oev = P.add("v", lambda e, ec=ec, tc=tc, ps=ps:
                                e.tensor_scalar(
                        out=mid_s[:, ec * d.TPC + tc * 512:
                                  ec * d.TPC + (tc + 1) * 512],
                        in0=ps, scalar1=G_s[:, ec:ec + 1], scalar2=None,
                        op0=ALU.mult), deps=[mm, p3_v_tail])
                    slot_free[slot] = oev
                    mid_done = oev
                    mi += 1

            xo_free = [None, None]
            ps_free = [mid_done] * 4
            prev_oo = [None, None]
            ei = 0
            for tt in range(d.NT):
                evs = []
                for fc in range(d.NFC):
                    slot = ei % 4
                    ps = FSG[slot // 2][:, (slot % 2) * 512:
                                        (slot % 2) * 512 + d.FC]
                    mm = None
                    for ec in range(d.ERCH):
                        mm = P.add("t", lambda e, tt=tt, fc=fc, ec=ec,
                                   ps=ps: e.matmul(
                            ps, mid_s[:, ec * d.TPC + tt * 128:
                                      ec * d.TPC + (tt + 1) * 128],
                            bT_s[:, ec * d.C + fc * d.FC:
                                 ec * d.C + (fc + 1) * d.FC],
                            start=(ec == 0), stop=(ec == d.ERCH - 1)),
                            deps=[mid_done] + bT_done_sv
                            + [ps_free[slot]])
                    oa = P.add("v", lambda e, tt=tt, ps=ps, slot=slot:
                               e.tensor_scalar(
                        out=sc1b[slot % 2][:, 0:d.FC], in0=ps,
                        scalar1=mur2[:, tt:tt + 1], scalar2=None,
                        op0=ALU.subtract),
                        deps=[mm, oM2, prev_oo[slot % 2]])
                    ps_free[slot] = oa
                    ob = P.add("v", lambda e, tt=tt, fc=fc, slot=slot:
                               e.tensor_scalar(
                        out=sc2b[slot % 2][:, 0:d.FC],
                        in0=xpost[:, tt * d.C + fc * d.FC:
                                  tt * d.C + (fc + 1) * d.FC],
                        scalar1=r1_2[:, tt:tt + 1], scalar2=None,
                        op0=ALU.mult), deps=[oR1, prev_oo[slot % 2]])
                    oo = P.add("v", lambda e, tt=tt, fc=fc, slot=slot:
                               e.tensor_tensor(
                        out=xo[tt % 2][:, fc * d.FC:(fc + 1) * d.FC],
                        in0=sc1b[slot % 2][:, 0:d.FC],
                        in1=sc2b[slot % 2][:, 0:d.FC],
                        op=ALU.add), deps=[oa, ob, xo_free[tt % 2]])
                    prev_oo[slot % 2] = oo
                    evs.append(oo)
                    ei += 1
                od = P.add("gd", lambda e, tt=tt: e.dma_start(
                    out=out[tt * 128:(tt + 1) * 128, :],
                    in_=xo[tt % 2][:]), deps=evs, stream="out")
                xo_free[tt % 2] = od

        _xpost_cm.__exit__(None, None, None)
        P.emit(nc)
    return nc


def host_consts(d):
    i32 = np.arange(0, d.D, 2) / d.D
    theta = (1.0 / (10000.0 ** i32)) * 3.0
    pos = np.arange(d.L, dtype=np.float64)
    ang = pos[:, None] * theta[None, :]
    cos = np.cos(ang)
    sin = np.sin(ang)
    p = np.arange(128)
    pair = (p % d.D) // 2
    sgn = np.where(p % 2 == 0, -1.0, 1.0)
    cosF = np.ascontiguousarray(cos[:, pair].T.astype(np.float32))
    sinF = np.ascontiguousarray((sin[:, pair] * sgn[None, :]).T
                                .astype(np.float32))
    idn = np.eye(128, dtype=BF)
    idf = np.eye(128, dtype=np.float32)
    sel = np.zeros((d.E, d.ERCH * 128), np.float32)
    for e_ in range(d.E):
        ec = e_ // d.EPC
        j0 = (e_ % d.EPC) * d.R
        sel[e_, ec * 128 + j0: ec * 128 + j0 + d.R] = 1.0
    return cosF, sinF, idn, idf, sel


def run_fused(d, x, wqkv, wproj, bproj, wr, aw, bw):
    ncores = d.NB * d.G
    cores = list(range(ncores))
    cosF, sinF, idn, idf, sel = host_consts(d)
    nc = build_fused(d)
    im = []
    for c in cores:
        b, r = c // d.G, c % d.G
        off = r * d.TPC
        im.append({
            "xb": np.ascontiguousarray(x[b]),
            "xs": np.ascontiguousarray(x[b, off:off + d.TPC]),
            "wqkv": wqkv, "wproj": wproj, "bproj": bproj,
            "cosK": cosF, "sinK": sinF,
            "cosQ": np.ascontiguousarray(cosF[:, off:off + d.TPC]),
            "sinQ": np.ascontiguousarray(sinF[:, off:off + d.TPC]),
            "wr": wr, "aw": aw, "bw": bw, "sel": sel,
            "idn": idn, "idf": idf,
        })
    r_ = run_bass_kernel_spmd(nc, im, core_ids=cores)
    outp = np.empty((d.NB, d.L, d.C), np.float32)
    for c in cores:
        outp[c // d.G, (c % d.G) * d.TPC:(c % d.G + 1) * d.TPC] = \
            np.asarray(r_.results[c]["out"], np.float32)
    return outp


def kernel(L0_attn_x, L0_attn_w_qkv, L0_attn_w_proj, L0_attn_b_proj,
           L0_ffn_Wr, L0_ffn_A, L0_ffn_Bw):
    d = dd(dims_real())
    x = np.ascontiguousarray(np.asarray(L0_attn_x, np.float32))
    wqkv = np.ascontiguousarray(np.asarray(L0_attn_w_qkv, np.float32))
    wproj = np.ascontiguousarray(np.asarray(L0_attn_w_proj, np.float32))
    bproj = np.ascontiguousarray(
        np.asarray(L0_attn_b_proj, np.float32).reshape(1, d.C))
    wr = np.ascontiguousarray(np.asarray(L0_ffn_Wr, np.float32))
    aw = np.ascontiguousarray(np.asarray(L0_ffn_A, np.float32))
    bw = np.ascontiguousarray(np.asarray(L0_ffn_Bw, np.float32))
    return run_fused(d, x, wqkv, wproj, bproj, wr, aw, bw)


# version 22
# speedup vs baseline: 1.1865x; 1.1865x over previous
import numpy as np
import ml_dtypes
from types import SimpleNamespace
import concourse.bass as bass
import concourse.mybir as mybir
from concourse.bass_utils import run_bass_kernel_spmd

F32 = mybir.dt.float32
BF16 = mybir.dt.bfloat16
AF = mybir.ActivationFunctionType
ALU = mybir.AluOpType
AX = mybir.AxisListType
BF = ml_dtypes.bfloat16

LAM = 0.125
EPS = 1e-5

KINDS = ("sd", "gd", "g", "t", "v", "s")
ENGINE_OF = {"sd": "sync", "gd": "gpsimd", "g": "gpsimd",
             "t": "tensor", "v": "vector", "s": "scalar"}
ENGINES = ("sync", "gpsimd", "tensor", "vector", "scalar")


def dims_real():
    return SimpleNamespace(C=1024, H=16, TPC=2048, G=4, NB=2, E=8, R=64)


def dd(d):
    d.D = 64
    d.HP = d.H // 2
    d.CH = d.C // 128
    d.NT = d.TPC // 128
    d.NTC = d.TPC // 512
    d.L = d.G * d.TPC
    d.KT = d.L // 128
    d.NTB = d.L // 128
    d.GB = d.L // 512
    d.F3 = 3 * d.C
    d.FC = min(512, d.C)
    d.NFC = d.C // d.FC
    d.ER = d.E * d.R
    d.ERCH = d.ER // 128
    d.EPC = 128 // d.R
    assert d.CH == d.HP, "cls q0 block-diag assumes CH == HP"
    return d


class Prog:

    want_oplog = False

    def __init__(self):
        self.ops = []
        self.cum = {}
        self.last_stream = {}
        self.oplog = []

    def add(self, kind, fn, deps=(), stream=None):
        if kind in ("sd", "gd"):
            assert stream is not None
            semkey = "D_" + stream
            inc = 16
            extra = (self.last_stream.get(stream),)
        else:
            semkey = kind
            inc = 1
            extra = ()
        self.cum[semkey] = self.cum.get(semkey, 0) + inc
        op = SimpleNamespace(kind=kind, fn=fn, inc=inc, semkey=semkey,
                             deps=tuple(x for x in (tuple(deps) + extra)
                                        if x is not None),
                             scum=self.cum[semkey],
                             line=fn.__code__.co_firstlineno)
        self.ops.append(op)
        if stream is not None:
            self.last_stream[stream] = op
        return op

    def emit(self, nc):
        Prog.last = self
        import contextlib
        with contextlib.ExitStack() as st:
            sems = {k: st.enter_context(nc.semaphore(name=f"sm_{k}"))
                    for k in self.cum}
            blk = st.enter_context(nc.Block())
            for ename in ENGINES:
                kinds_here = tuple(k for k in KINDS if ENGINE_OF[k] == ename)

                def emit_engine(eng, ename=ename, kinds_here=kinds_here):
                    hw = {}
                    for op in self.ops:
                        if op.kind not in kinds_here:
                            continue
                        need = {}
                        for dp in op.deps:
                            if dp.semkey == op.semkey and op.kind == "t" \
                                    and dp.kind == "t":
                                continue
                            need[dp.semkey] = max(need.get(dp.semkey, 0),
                                                  dp.scum)
                        for k, v in need.items():
                            if v > 0 and v > hw.get(k, 0):
                                eng.wait_ge(sems[k], v)
                                hw[k] = v
                        _inst = op.fn(eng)
                        _inst.then_inc(sems[op.semkey], op.inc)
                        if Prog.want_oplog:
                            self.oplog.append(
                                (nc.next_id(), op.line, op.kind))
                    if ename == "gpsimd":
                        for k, v in self.cum.items():
                            eng.wait_ge(sems[k], v)
                getattr(blk, ename)(emit_engine)
        return nc


def ln_pass1(P, d, src_ap, mean_all, var_all, t, dep_src, sc):
    if t % 2 == 0:
        o1 = P.add("s", lambda e, a=src_ap: e.activation(
            out=sc["jnk"][:], in_=a, func=AF.Copy, bias=0.0, scale=1.0,
            accum_out=sc["s1"][:, 0:1]),
            deps=list(dep_src) + [sc.get("prev_o2"), sc.get("prev_o3")])
        o3 = P.add("s", lambda e, a=src_ap: e.activation(
            out=sc["jnk"][:], in_=a, func=AF.Square, bias=sc["z"], scale=1.0,
            accum_out=sc["s2"][:, 0:1]),
            deps=list(dep_src) + [sc["oz"], sc.get("prev_o5"), o1])
        sc["prev_o3"] = o3
        c1, c2 = sc["s1"][:, 0:1], sc["s2"][:, 0:1]
    else:
        o1 = P.add("v", lambda e, a=src_ap: e.reduce_sum(
            sc["s1"][:, 1:2], a, axis=AX.X),
            deps=list(dep_src) + [sc.get("prev_o2v")])
        osq = P.add("v", lambda e, a=src_ap: e.tensor_tensor(
            out=sc["sqv"][:], in0=a, in1=a, op=ALU.mult),
            deps=list(dep_src) + [sc.get("prev_o5v")])
        o3 = P.add("v", lambda e: e.reduce_sum(
            sc["s2"][:, 2:3], sc["sqv"][:], axis=AX.X), deps=[osq])
        c1, c2 = sc["s1"][:, 1:2], sc["s2"][:, 2:3]
    o2 = P.add("v", lambda e, t=t, c1=c1: e.tensor_scalar(
        out=mean_all[:, t:t + 1], in0=c1, scalar1=1.0 / d.C,
        scalar2=None, op0=ALU.mult), deps=[o1])
    o5 = P.add("v", lambda e, c2=c2: e.tensor_scalar(
        out=sc["s2"][:, 1:2], in0=c2, scalar1=1.0 / d.C,
        scalar2=None, op0=ALU.mult), deps=[o3, sc.get("prev_o7")])
    o6 = P.add("v", lambda e, t=t: e.tensor_tensor(
        out=sc["s6"][:, 0:1], in0=mean_all[:, t:t + 1],
        in1=mean_all[:, t:t + 1], op=ALU.mult),
        deps=[o2, sc.get("prev_o7")])
    o7 = P.add("v", lambda e, t=t: e.tensor_tensor(
        out=var_all[:, t:t + 1], in0=sc["s2"][:, 1:2], in1=sc["s6"][:, 0:1],
        op=ALU.subtract), deps=[o5, o6])
    if t % 2 == 0:
        sc["prev_o2"] = o2
        sc["prev_o5"] = o5
    else:
        sc["prev_o2v"] = o2
        sc["prev_o5v"] = o3
    sc["prev_o7"] = o7
    return o7


def ln_finish(P, d, mean_all, var_all, rstd_all, mur_all, dep, cst, ocst):
    oL = P.add("s", lambda e: e.activation(
        out=var_all[:], in_=var_all[:], func=AF.Ln, bias=cst[:, 0:1],
        scale=1.0), deps=list(dep) + [ocst])
    oE = P.add("s", lambda e: e.activation(
        out=rstd_all[:], in_=var_all[:], func=AF.Exp, bias=cst[:, 1:2],
        scale=-0.5), deps=[oL])
    oM = P.add("v", lambda e: e.tensor_tensor(
        out=mur_all[:], in0=mean_all[:], in1=rstd_all[:], op=ALU.mult),
        deps=[oE])
    return oE, oM


def build_fused(d):
    nc = bass.Bass("TRN2", target_bir_lowering=False, debug=False)
    P = Prog()

    xbt = nc.dram_tensor("xb", [d.L, d.C], F32, kind="ExternalInput").ap()
    xst = nc.dram_tensor("xs", [d.TPC, d.C], F32, kind="ExternalInput").ap()
    wqkv = nc.dram_tensor("wqkv", [d.C, d.F3], F32, kind="ExternalInput").ap()
    cosK = nc.dram_tensor("cosK", [128, d.L], F32, kind="ExternalInput").ap()
    sinK = nc.dram_tensor("sinK", [128, d.L], F32, kind="ExternalInput").ap()
    cosQ = nc.dram_tensor("cosQ", [128, d.TPC], F32, kind="ExternalInput").ap()
    sinQ = nc.dram_tensor("sinQ", [128, d.TPC], F32, kind="ExternalInput").ap()
    wproj = nc.dram_tensor("wproj", [d.C, d.C], F32, kind="ExternalInput").ap()
    bproj = nc.dram_tensor("bproj", [1, d.C], F32, kind="ExternalInput").ap()
    wr = nc.dram_tensor("wr", [d.E, d.C], F32, kind="ExternalInput").ap()
    aw = nc.dram_tensor("aw", [d.E, d.R, d.C], F32, kind="ExternalInput").ap()
    bw = nc.dram_tensor("bw", [d.E, d.C, d.R], F32, kind="ExternalInput").ap()
    sel = nc.dram_tensor("sel", [d.E, d.ERCH * 128], F32,
                         kind="ExternalInput").ap()
    idn = nc.dram_tensor("idn", [128, 128], BF16, kind="ExternalInput").ap()
    idf = nc.dram_tensor("idf", [128, 128], F32, kind="ExternalInput").ap()
    out = nc.dram_tensor("out", [d.TPC, d.C], F32, kind="ExternalOutput").ap()

    kscr = nc.dram_tensor("kscr", [d.HP, 128, d.L], BF16,
                          kind="Internal").ap()
    vscr = nc.dram_tensor("vscr", [d.HP, d.KT, 128, 128], BF16,
                          kind="Internal").ap()

    import contextlib
    with contextlib.ExitStack() as top:
        T = lambda *a: top.enter_context(nc.sbuf_tensor(*a))
        _qfm_cm = nc.sbuf_tensor("Q_fm", [128, d.HP * d.TPC], BF16,
                                 side="right")
        Q_fm = _qfm_cm.__enter__()
        wp_sb = T("wp_sb", [128, d.CH * d.C], BF16)
        bp_s = T("bp_s", [1, d.C], F32)
        idn_s = T("idn_s", [128, 128], BF16)
        idf_s = T("idf_s", [128, 128], F32)
        ones_bf = T("ones_bf", [128, 1], BF16)
        ones1f = T("ones1f", [1, 128], F32)
        cst = T("cst", [128, 2], F32)
        kn_s = T("kn_s", [1, d.H * d.GB], F32)
        qn_s = T("qn_s", [1, d.H * d.NTC], F32)
        Mb_s = T("Mb_s", [128, d.HP * d.NTC], F32)
        G_s = T("G_s", [128, d.ERCH], F32)
        xln0 = T("xln0", [128, d.CH], BF16)
        q0f = T("q0f", [128, d.HP], BF16)

        ld_wp = P.add("gd", lambda e: e.dma_start(
            out=wp_sb.rearrange("p (c f) -> p c f", c=d.CH),
            in_=wproj.rearrange("(c p) f -> p c f", p=128)), stream="wonce")
        ld_bp = P.add("sd", lambda e: e.dma_start(out=bp_s[:], in_=bproj),
                      stream="sonce")
        ld_id = P.add("sd", lambda e: e.dma_start(out=idn_s[:], in_=idn),
                      stream="sonce")
        ld_if = P.add("sd", lambda e: e.dma_start(out=idf_s[:], in_=idf),
                      stream="sonce")
        o_ones = P.add("g", lambda e: e.memset(ones_bf[:], 1.0))
        o_ones1 = P.add("g", lambda e: e.memset(ones1f[:], 1.0))
        oc1 = P.add("g", lambda e: e.memset(cst[:, 0:1], EPS))
        oc2 = P.add("g", lambda e: e.memset(cst[:, 1:2], 0.0))

        with contextlib.ExitStack() as p1:
            T1 = lambda *a: p1.enter_context(nc.sbuf_tensor(*a))
            wsb = T1("wsb", [128, d.CH * d.F3], BF16)
            x_t = [T1(f"x{i}", [128, d.C], F32) for i in range(3)]
            sc = {"s1": T1("s1", [128, 4], F32),
                  "jnk": T1("jnk", [128, d.C], BF16),
                  "s2": T1("s2", [128, 4], F32),
                  "sqv": T1("sqv", [128, d.C], F32),
                  "s6": T1("s6", [128, 4], F32),
                  "z": cst[0:128, 1:2], "oz": oc2}
            meanb = T1("meanb", [128, d.NTB], F32)
            varb = T1("varb", [128, d.NTB], F32)
            rstdb = T1("rstdb", [128, d.NTB], F32)
            murb = T1("murb", [128, d.NTB], F32)
            mean_o = T1("mean_o", [128, d.NT], F32)
            var_o = T1("var_o", [128, d.NT], F32)
            rstd_o = T1("rstd_o", [128, d.NT], F32)
            mur_o = T1("mur_o", [128, d.NT], F32)
            xlnb = [T1(f"xlnb{i}", [128, d.C], BF16) for i in range(4)]
            xfg = [T1(f"xfg{i}", [128, d.CH * 512], BF16) for i in range(3)]
            ctb = [T1(f"ctb{i}", [128, 512], F32) for i in range(4)]
            stb = [T1(f"stb{i}", [128, 512], F32) for i in range(4)]
            sh_b = [T1(f"sh_b{i}", [128, 512], F32) for i in range(4)]
            t1_b = [T1(f"t1_b{i}", [128, 512], F32) for i in range(4)]
            t2_b = [T1(f"t2_b{i}", [128, 512], F32) for i in range(4)]
            stage = [T1(f"stage{i}", [128, 512], BF16) for i in range(4)]
            qsq_b = [T1(f"qsq_b{i}", [128, 512], BF16) for i in range(4)]
            vstage = [T1(f"vstage{i}", [128, d.FC], BF16) for i in range(4)]

            SG0 = p1.enter_context(nc.psum_tensor("SG0", [128, 1024], F32))
            SG1 = p1.enter_context(nc.psum_tensor("SG1", [128, 1024], F32))
            PO = p1.enter_context(nc.psum_tensor("PO", [128, 512], F32))
            PD = p1.enter_context(nc.psum_tensor("PD", [64, 512], F32))
            PTX = [p1.enter_context(nc.psum_tensor(f"PTX{i}",
                                                   [128, 512], BF16))
                   for i in range(2)]

            ld_w = P.add("gd", lambda e: e.dma_start(
                out=wsb.rearrange("p (c f) -> p c f", c=d.CH),
                in_=wqkv.rearrange("(c p) f -> p c f", p=128)),
                stream="wonce")

            lnb_done = []
            for t in range(d.NTB):
                lx = P.add("sd", lambda e, t=t: e.dma_start(
                    out=x_t[t % 2][:], in_=xbt[t * 128:(t + 1) * 128, :]),
                    deps=[lnb_done[t - 2]] if t >= 2 else [],
                    stream=f"x1{t % 2}")
                lnb_done.append(ln_pass1(P, d, x_t[t % 2][:], meanb, varb,
                                         t, [lx], sc))
            oEb, oMb_ = ln_finish(P, d, meanb, varb, rstdb, murb,
                                  lnb_done, cst, oc2)

            lno_done = []
            for t in range(d.NT):
                lx = P.add("sd", lambda e, t=t: e.dma_start(
                    out=x_t[t % 2][:], in_=xst[t * 128:(t + 1) * 128, :]),
                    deps=[lno_done[t - 2] if t >= 2 else lnb_done[-1]],
                    stream=f"x1{t % 2}")
                lno_done.append(ln_pass1(P, d, x_t[t % 2][:], mean_o, var_o,
                                         t, [lx], sc))
            oEo, oMo = ln_finish(P, d, mean_o, var_o, rstd_o, mur_o,
                                 lno_done, cst, oc2)

            st8 = dict(pslot=[None] * 4, psi=0,
                       t12_reader=[None] * 4, t2_free=[None] * 4,
                       qsq_reader=[None] * 4,
                       normPD=None, normPO=None,
                       stage_readers=[[] for _ in range(4)],
                       vst_free=[[] for _ in range(4)],
                       xfg_users=[[], [], []], norm_pending=[],
                       xln_rd=[lno_done[-1]] * 4,
                       tp_evac=[None, None], tpb=0, si=0, vi=0,
                       gseq=0, ct_user=[None] * 4)

            def take_pslot():
                i = st8["psi"] % 4
                st8["psi"] += 1
                ten = (SG0, SG1)[i // 2]
                return i, ten[:, (i % 2) * 512:(i % 2) * 512 + 512]
            x0ops = []

            def emit_group(src_x, g, rstd_a, mur_a, oE_, oM_, save0):
                last = None
                gp = st8["gseq"] % 3
                users = st8["xfg_users"][gp]
                for tl in range(4):
                    t = g * 4 + tl
                    ti = (g * 4 + tl) % 3
                    lx = P.add("sd", lambda e, t=t, ti=ti, src_x=src_x:
                               e.dma_start(
                        out=x_t[ti][:],
                        in_=src_x[t * 128:(t + 1) * 128, :]),
                        deps=[st8["xln_rd"][ti]], stream=f"x2{ti % 2}")
                    oxl = P.add("v", lambda e, t=t, ti=ti, rstd_a=rstd_a,
                                mur_a=mur_a: e.tensor_scalar(
                        out=xlnb[ti][:], in0=x_t[ti][:],
                        scalar1=rstd_a[:, t:t + 1],
                        scalar2=mur_a[:, t:t + 1],
                        op0=ALU.mult, op1=ALU.subtract),
                        deps=[lx, oE_, oM_])
                    for b4 in range((d.CH + 3) // 4):
                        bank = st8["tpb"] % 2
                        st8["tpb"] += 1
                        otps = []
                        for j4 in range(min(4, d.CH - b4 * 4)):
                            c = b4 * 4 + j4
                            pt = PTX[bank][:, j4 * 128:(j4 + 1) * 128]
                            otp = P.add("t", lambda e, ti=ti, c=c, pt=pt:
                                        e.transpose(
                                pt, xlnb[ti][:, c * 128:(c + 1) * 128],
                                idn_s[:]),
                                deps=[oxl, ld_id,
                                      st8["tp_evac"][bank]] + users)
                            otps.append(otp)
                            st8["xln_rd"][ti] = otp
                        for j4 in range(min(4, d.CH - b4 * 4)):
                            c = b4 * 4 + j4
                            pt = PTX[bank][:, j4 * 128:(j4 + 1) * 128]
                            oev = P.add("s", lambda e, tl=tl, c=c, pt=pt,
                                        gp=gp: e.copy(
                                xfg[gp][:, c * 512 + tl * 128:
                                        c * 512 + (tl + 1) * 128], pt),
                                deps=[otps[-1]])
                            last = oev
                        st8["tp_evac"][bank] = last
                        if save0 and tl == 0:
                            for c in range(b4 * 4,
                                           min(d.CH, b4 * 4 + 4)):
                                osv = P.add("v", lambda e, c=c:
                                            e.tensor_copy(
                                    xln0[:, c:c + 1],
                                    xfg[0][:, c * 512:c * 512 + 1]),
                                    deps=[last])
                                x0ops.append(osv)
                                last = osv
                            st8["tp_evac"][bank] = last
                st8["xfg_users"][gp] = []
                st8["gseq"] += 1
                return last, gp

            def rope_block(ps, mm, ct, st_, outsink, nbuf, ncol):
                si = st8["si"]
                pz = si % 4
                osh = P.add("v", lambda e, ps=ps, pz=pz: e.stream_shuffle(
                    sh_b[pz][:], ps, [i ^ 1 for i in range(32)]),
                    deps=[mm, st8["t2_free"][pz]])
                ot1 = P.add("v", lambda e, ps=ps, ct=ct, pz=pz:
                            e.tensor_tensor(
                    out=t1_b[pz][:], in0=ps, in1=ct, op=ALU.mult),
                    deps=[mm, st8["t12_reader"][pz]])
                ot2 = P.add("g", lambda e, st_=st_, pz=pz: e.tensor_tensor(
                    out=t2_b[pz][:], in0=sh_b[pz][:], in1=st_, op=ALU.mult),
                    deps=[osh, st8["t12_reader"][pz]])
                st8["t2_free"][pz] = ot2
                ost = P.add("v", lambda e, pz=pz: e.tensor_tensor(
                    out=stage[pz][:], in0=t1_b[pz][:], in1=t2_b[pz][:],
                    op=ALU.add),
                    deps=[ot1, ot2] + st8["stage_readers"][pz])
                st8["t12_reader"][pz] = ost
                osink = outsink(stage[pz][:], [ost])
                osq = P.add("v", lambda e, pz=pz: e.tensor_tensor(
                    out=qsq_b[pz][:], in0=stage[pz][:],
                    in1=stage[pz][:], op=ALU.mult),
                    deps=[ost, st8["qsq_reader"][pz]])
                st8["stage_readers"][pz] = [osq, osink]
                st8["norm_pending"].append((pz, osq, nbuf, ncol))
                flush_norm(3)
                st8["si"] = si + 1
                return osink

            def flush_norm(n_keep):
                while len(st8["norm_pending"]) > n_keep:
                    pz, osq, nbuf, ncol = st8["norm_pending"].pop(0)
                    on1 = P.add("t", lambda e, pz=pz: e.matmul(
                        PD[0:1, :], ones_bf[0:64, :], qsq_b[pz][0:64, :],
                        start=True, stop=True, tile_position=(0, 0)),
                        deps=[osq, o_ones, st8["normPD"]])
                    on2 = P.add("t", lambda e, pz=pz: e.matmul(
                        PO[0:1, :], ones_bf[64:128, :], qsq_b[pz][64:128, :],
                        start=True, stop=True, tile_position=(64, 0)),
                        deps=[osq, o_ones, st8["normPO"]])
                    st8["qsq_reader"][pz] = on2
                    or1 = P.add("v", lambda e, nbuf=nbuf, n0=ncol[0]:
                                e.reduce_max(nbuf[0:1, n0:n0 + 1],
                                             PD[0:1, :], axis=AX.X),
                                deps=[on1])
                    or2 = P.add("v", lambda e, nbuf=nbuf, n1=ncol[1]:
                                e.reduce_max(nbuf[0:1, n1:n1 + 1],
                                             PO[0:1, :], axis=AX.X),
                                deps=[on2])
                    st8["normPD"] = or1
                    st8["normPO"] = or2

            ko_last = vo_last = None
            vo_lasts = [None, None]
            ko_lasts = [None, None]
            for g in range(d.GB):
                ogrp, gp = emit_group(xbt, g, rstdb, murb, oEb, oMb_, g == 0)
                lct = P.add("sd", lambda e, g=g: e.dma_start(
                    out=ctb[g % 4][:], in_=cosK[:, g * 512:(g + 1) * 512]),
                    deps=[st8["ct_user"][g % 4]], stream="ct")
                lst = P.add("sd", lambda e, g=g: e.dma_start(
                    out=stb[g % 4][:], in_=sinK[:, g * 512:(g + 1) * 512]),
                    deps=[st8["ct_user"][g % 4]], stream="st")
                for hp in range(d.HP):
                    slot, ps = take_pslot()
                    mm = None
                    for c in range(d.CH):
                        mm = P.add("t", lambda e, c=c, hp=hp, g=g, ps=ps,
                                   gp=gp: e.matmul(
                            ps, wsb[:, c * d.F3 + d.C + hp * 128:
                                    c * d.F3 + d.C + (hp + 1) * 128],
                            xfg[gp][:, c * 512:(c + 1) * 512],
                            start=(c == 0), stop=(c == d.CH - 1)),
                            deps=[ogrp, ld_w, st8["pslot"][slot], lct, lst])
                    st8["xfg_users"][gp].append(mm)

                    def ksink(stage_ap, dps, hp=hp, g=g,
                              kp=st8["si"] % 2):
                        return P.add("gd", lambda e: e.dma_start(
                            out=kscr[hp, :, g * 512:(g + 1) * 512],
                            in_=stage_ap), deps=dps, stream=f"ko{kp}")
                    ncol = ((hp * 2 + 0) * d.GB + g, (hp * 2 + 1) * d.GB + g)
                    osink = rope_block(ps, mm, ctb[g % 4][:], stb[g % 4][:],
                                       ksink, kn_s, ncol)
                    ko_lasts[(st8["si"] - 1) % 2] = osink
                    ko_last = osink
                    st8["pslot"][slot] = st8["t12_reader"][(st8["si"] - 1) % 4]
                    st8["ct_user"][g % 4] = st8["t12_reader"][(st8["si"] - 1) % 4]
                for tl in range(4):
                    vsl = [take_pslot() for _ in range(d.NFC)]
                    mmvs = [None] * d.NFC
                    for c in range(d.CH):
                        for fc in range(d.NFC):
                            vslot, psv0 = vsl[fc]
                            psv = psv0[:, 0:d.FC]
                            mmvs[fc] = P.add("t", lambda e, c=c, tl=tl,
                                             fc=fc, g=g, psv=psv, gp=gp:
                                             e.matmul(
                                psv,
                                xfg[gp][:, c * 512 + tl * 128:
                                        c * 512 + (tl + 1) * 128],
                                wsb[:, c * d.F3 + 2 * d.C + fc * d.FC:
                                    c * d.F3 + 2 * d.C + (fc + 1) * d.FC],
                                start=(c == 0), stop=(c == d.CH - 1)),
                                deps=[ogrp, ld_w,
                                      st8["pslot"][vsl[fc][0]]]
                                if c == 0 else [ogrp, ld_w])
                    for fc in range(d.NFC):
                        vslot, psv0 = vsl[fc]
                        psv = psv0[:, 0:d.FC]
                        mmv = mmvs[fc]
                        st8["xfg_users"][gp].append(mmv)
                        vi = st8["vi"]
                        oev = P.add("s", lambda e, psv=psv, vi=vi:
                                    e.copy(vstage[vi % 4][:], psv),
                                    deps=[mmv] + st8["vst_free"][vi % 4])
                        st8["pslot"][vslot] = oev
                        hp0 = (fc * d.FC) // 128
                        nh = d.FC // 128
                        kt = g * 4 + tl
                        od = P.add("gd", lambda e, hp0=hp0, nh=nh, kt=kt,
                                   vi=vi: e.dma_start(
                            out=vscr[hp0:hp0 + nh, kt, :, :].rearrange(
                                "h p dd -> p h dd"),
                            in_=vstage[vi % 4].rearrange(
                                "p (h dd) -> p h dd", h=nh)),
                            deps=[oev], stream=f"vo{vi % 2}")
                        st8["vst_free"][vi % 4] = [od]
                        vo_lasts[vi % 2] = od
                        vo_last = od
                        st8["vi"] += 1

            q_done = []
            for qg in range(d.NTC):
                ogrp, gp = emit_group(xst, qg, rstd_o, mur_o, oEo, oMo, False)
                gpar = (d.GB + qg) % 4
                lct = P.add("sd", lambda e, qg=qg, gpar=gpar: e.dma_start(
                    out=ctb[gpar][:], in_=cosQ[:, qg * 512:(qg + 1) * 512]),
                    deps=[st8["ct_user"][gpar]], stream="ct")
                lst = P.add("sd", lambda e, qg=qg, gpar=gpar: e.dma_start(
                    out=stb[gpar][:], in_=sinQ[:, qg * 512:(qg + 1) * 512]),
                    deps=[st8["ct_user"][gpar]], stream="st")
                for hp in range(d.HP):
                    slot, ps = take_pslot()
                    mm = None
                    for c in range(d.CH):
                        mm = P.add("t", lambda e, c=c, hp=hp, gpar=gpar,
                                   ps=ps, gp=gp: e.matmul(
                            ps, wsb[:, c * d.F3 + hp * 128:
                                    c * d.F3 + (hp + 1) * 128],
                            xfg[gp][:, c * 512:(c + 1) * 512],
                            start=(c == 0), stop=(c == d.CH - 1)),
                            deps=[ogrp, ld_w, st8["pslot"][slot], lct, lst])
                    st8["xfg_users"][gp].append(mm)

                    def qsink(stage_ap, dps, hp=hp, qg=qg):
                        return P.add("s", lambda e: e.copy(
                            Q_fm[:, hp * d.TPC + qg * 512:
                                 hp * d.TPC + (qg + 1) * 512], stage_ap),
                            deps=dps)
                    ncol = ((2 * hp) * d.NTC + qg, (2 * hp + 1) * d.NTC + qg)
                    osink = rope_block(ps, mm, ctb[gpar][:], stb[gpar][:],
                                       qsink, qn_s, ncol)
                    q_done.append(osink)
                    st8["pslot"][slot] = st8["t12_reader"][(st8["si"] - 1) % 4]
                    st8["ct_user"][gpar] = st8["t12_reader"][(st8["si"] - 1) % 4]

            flush_norm(0)
            oq0m = None
            for hp in range(d.HP):
                for c in range(d.CH):
                    oq0m = P.add("t", lambda e, hp=hp, c=c: e.matmul(
                        SG1[:, hp:hp + 1],
                        wsb[:, c * d.F3 + hp * 128:c * d.F3 + (hp + 1) * 128],
                        xln0[:, c:c + 1],
                        start=(c == 0), stop=(c == d.CH - 1)),
                        deps=[x0ops[-1], ld_w] + list(st8["pslot"]))
            oq0c = P.add("v", lambda e: e.tensor_copy(
                q0f[:], SG1[:, 0:d.HP]), deps=[oq0m])
            p1_t_tail = oq0m
            p1_v_tail = oq0c
            or_last = st8["normPO"]

        _ofm_cm = nc.sbuf_tensor("ofm", [128, d.HP * d.TPC], BF16)
        ofm = _ofm_cm.__enter__()
        with contextlib.ExitStack() as p2:
            T2 = lambda *a: p2.enter_context(nc.sbuf_tensor(*a))
            kbuf = T2("kbuf", [128, d.L], BF16)
            vbuf = T2("vbuf", [128, d.KT * 128], BF16)
            PTb = [T2(f"PTb{i}", [128, 1024], BF16) for i in range(4)]
            rd_s = T2("rd_s", [1, 1024], F32)
            rb_s = T2("rb_s", [128, 1024], F32)
            kn8 = T2("kn8", [1, d.HP], F32)
            qn32 = T2("qn32", [1, d.HP * d.NTC], F32)
            t32 = T2("t32", [1, d.HP * d.NTC], F32)

            ASG = [p2.enter_context(nc.psum_tensor(f"ASG{i}", [128, 1024],
                                                   F32)) for i in range(3)]
            PA = p2.enter_context(nc.psum_tensor("PA", [128, 512], F32))
            PB = p2.enter_context(nc.psum_tensor("PB", [128, 512], F32))

            okn = P.add("v", lambda e: e.reduce_max(
                kn8.rearrange("p (a b) -> p a b", b=1),
                kn_s.rearrange("p (a b) -> p a b", b=2 * d.GB),
                axis=AX.X), deps=[or_last, p1_t_tail])
            oqn = None
            for hp in range(d.HP):
                for qc in range(d.NTC):
                    oqn = P.add("v", lambda e, hp=hp, qc=qc: e.tensor_tensor(
                        out=qn32[0:1, hp * d.NTC + qc:hp * d.NTC + qc + 1],
                        in0=qn_s[0:1, (2 * hp) * d.NTC + qc:
                                 (2 * hp) * d.NTC + qc + 1],
                        in1=qn_s[0:1, (2 * hp + 1) * d.NTC + qc:
                                 (2 * hp + 1) * d.NTC + qc + 1],
                        op=ALU.max), deps=[or_last, oqn])
            ot32 = None
            for hp in range(d.HP):
                ot32 = P.add("v", lambda e, hp=hp: e.tensor_scalar(
                    out=t32[0:1, hp * d.NTC:(hp + 1) * d.NTC],
                    in0=qn32[0:1, hp * d.NTC:(hp + 1) * d.NTC],
                    scalar1=kn8[0:1, hp:hp + 1], scalar2=None,
                    op0=ALU.mult), deps=[okn, oqn, ot32])
            oln = P.add("s", lambda e: e.activation(
                out=t32[:], in_=t32[:], func=AF.Ln, bias=cst[0:1, 1:2],
                scale=1.0), deps=[ot32, oc2])
            oex = P.add("s", lambda e: e.activation(
                out=t32[:], in_=t32[:], func=AF.Exp, bias=cst[0:1, 1:2],
                scale=0.5), deps=[oln])
            omu = P.add("s", lambda e: e.mul(t32[:], t32[:], -LAM),
                        deps=[oex])
            obc = P.add("t", lambda e: e.matmul(
                ASG[0][:, 0:d.HP * d.NTC], ones1f[:], t32[:],
                start=True, stop=True),
                deps=[omu, o_ones1, p1_t_tail, p1_v_tail])
            oMb = P.add("v", lambda e: e.tensor_copy(
                Mb_s[:], ASG[0][:, 0:d.HP * d.NTC]), deps=[obc])

            exp_ops = {}
            od_last = {}
            vb_loads = {}
            bc_par = [0]
            ev_mults = None
            ev_recipB = None
            bc_copies = oMb
            att_last_t = None
            pend_ev = [None]

            def flush_evac():
                nonlocal ev_mults, bc_copies, att_last_t
                if pend_ev[0] is None:
                    return
                p, odl, expl, orA, orB, col = pend_ev[0]
                pend_ev[0] = None
                obA = P.add("t", lambda e, p=p: e.matmul(
                    ASG[p][:, 0:512], ones1f[:], rd_s[0:1, 0:512],
                    start=True, stop=True),
                    deps=[orA, expl, o_ones1])
                obB = P.add("t", lambda e, p=p: e.matmul(
                    ASG[p][:, 512:1024], ones1f[:],
                    rd_s[0:1, 512:1024],
                    start=True, stop=True),
                    deps=[orB, expl])
                ocA = P.add("v", lambda e, p=p: e.tensor_copy(
                    rb_s[:, 0:512], ASG[p][:, 0:512]),
                    deps=[obA, ev_mults])
                ocB = P.add("v", lambda e, p=p: e.tensor_copy(
                    rb_s[:, 512:1024], ASG[p][:, 512:1024]),
                    deps=[obB, ev_mults])
                bc_par[0] = p
                bc_copies = ocB
                omA = P.add("v", lambda e, col=col: e.tensor_tensor(
                    out=ofm[0:64, col:col + 512], in0=PA[0:64, :],
                    in1=rb_s[0:64, 0:512], op=ALU.mult),
                    deps=[ocA, odl])
                omB = P.add("v", lambda e, col=col: e.tensor_tensor(
                    out=ofm[64:128, col:col + 512], in0=PB[64:128, :],
                    in1=rb_s[64:128, 512:1024], op=ALU.mult),
                    deps=[ocB])
                ev_mults = omB
                att_last_t = obB
            last_S = obc
            last_O = None
            it = 0
            for hp in range(d.HP):
                kload = P.add("sd", lambda e, hp=hp: e.dma_start(
                    out=kbuf[:], in_=kscr[hp]),
                    deps=[last_S, ko_lasts[0], ko_lasts[1], p1_v_tail],
                    stream="kb")
                vls = []
                for j8 in range(max(1, d.KT // 8)):
                    n8 = min(8, d.KT - j8 * 8)
                    vls.append(P.add("sd", lambda e, hp=hp, j8=j8, n8=n8:
                                     e.dma_start(
                        out=vbuf[:, j8 * 1024:j8 * 1024 + n8 * 128]
                        .rearrange("p (k dd) -> p k dd", k=n8),
                        in_=vscr[hp, j8 * 8:j8 * 8 + n8].rearrange(
                            "k p dd -> p k dd")),
                        deps=[last_O, vo_lasts[0], vo_lasts[1], obc,
                              p1_v_tail], stream="vb"))
                vb_loads[hp] = vls[-1]
                def emit_od(jt, hp2, qc2):
                    kt2 = jt % d.KT
                    oex2 = exp_ops[jt]
                    first, last = kt2 == 0, kt2 == d.KT - 1
                    ops = []
                    if first:
                        ops = ["oA", "oB", "dA", "dB"]
                    else:
                        ops = ["dA", "dB", "oA", "oB"]
                    res = {}
                    for nm in ops:
                        if nm == "oA":
                            res[nm] = P.add("t", lambda e, jt=jt, kt2=kt2:
                                            e.matmul(
                                PA[0:64, :],
                                vbuf[:, kt2 * 128:kt2 * 128 + 64],
                                PTb[jt % 4][:, 0:512],
                                start=first, stop=last,
                                tile_position=(0, 0)),
                                deps=[oex2, vb_loads[hp2],
                                      ev_mults if first else None])
                        elif nm == "oB":
                            res[nm] = P.add("t", lambda e, jt=jt, kt2=kt2:
                                            e.matmul(
                                PB[64:128, :],
                                vbuf[:, kt2 * 128 + 64:(kt2 + 1) * 128],
                                PTb[jt % 4][:, 512:1024],
                                start=first, stop=last,
                                tile_position=(0, 64)),
                                deps=[oex2, vb_loads[hp2],
                                      ev_mults if first else None])
                        elif nm == "dA":
                            res[nm] = P.add("t", lambda e, jt=jt, kt2=kt2:
                                            e.matmul(
                                PA[64:65, :], ones_bf[:],
                                PTb[jt % 4][:, 0:512],
                                start=first, stop=False,
                                tile_position=(0, 64),
                                skip_group_check=True),
                                deps=[oex2, o_ones,
                                      ev_recipB if first else None])
                        else:
                            res[nm] = P.add("t", lambda e, jt=jt, kt2=kt2:
                                            e.matmul(
                                PB[0:1, :], ones_bf[:],
                                PTb[jt % 4][:, 512:1024],
                                start=first, stop=False,
                                tile_position=(0, 0),
                                skip_group_check=True),
                                deps=[oex2, o_ones,
                                      ev_recipB if first else None])
                    od_last[jt] = res[ops[-1]]
                    return res["oB"] if not first else res["dB"]

                for qc in range(d.NTC):
                    for kt in range(d.KT):
                        sg = ASG[it % 3]
                        extra = bc_copies if (
                            kt < 3 and (it % 3) == bc_par[0]) or it == 0 \
                            else None
                        sA = P.add("t", lambda e, hp=hp, qc=qc, kt=kt, sg=sg:
                                   e.matmul(
                            sg[:, 0:512],
                            kbuf[0:64, kt * 128:(kt + 1) * 128],
                            Q_fm[0:64, hp * d.TPC + qc * 512:
                                 hp * d.TPC + (qc + 1) * 512],
                            start=True, stop=True, tile_position=(0, 0)),
                            deps=[kload, exp_ops.get(it - 3), extra]
                            + ([q_done[-1], p1_v_tail] if it == 0 else []))
                        sB = P.add("t", lambda e, hp=hp, qc=qc, kt=kt, sg=sg:
                                   e.matmul(
                            sg[:, 512:1024],
                            kbuf[64:128, kt * 128:(kt + 1) * 128],
                            Q_fm[64:128, hp * d.TPC + qc * 512:
                                 hp * d.TPC + (qc + 1) * 512],
                            start=True, stop=True, tile_position=(64, 0)),
                            deps=[kload, exp_ops.get(it - 3), extra])
                        last_S = sB
                        midx = hp * d.NTC + qc
                        oex2 = P.add("s", lambda e, it=it, midx=midx, sg=sg:
                                     e.activation(
                            out=PTb[it % 4][:], in_=sg[:], func=AF.Exp,
                            bias=Mb_s[:, midx:midx + 1], scale=LAM),
                            deps=[sB, oMb, od_last.get(it - 4)]
                            + ([ko_lasts[0], ko_lasts[1], vo_lasts[0],
                                vo_lasts[1]] if it == 0 else []))
                        exp_ops[it] = oex2
                        if kt == 0:
                            flush_evac()
                        if kt > 2:
                            last_O = emit_od(it - 3, hp, qc)
                        it += 1
                    last_O = emit_od(it - 3, hp, qc)
                    last_O = emit_od(it - 2, hp, qc)
                    last_O = emit_od(it - 1, hp, qc)
                    p = (it - 1) % 3
                    orA = P.add("v", lambda e: e.reciprocal(
                        rd_s[0:1, 0:512], PA[64:65, :]),
                        deps=[od_last[it - 1], bc_copies])
                    orB = P.add("v", lambda e: e.reciprocal(
                        rd_s[0:1, 512:1024], PB[0:1, :]),
                        deps=[od_last[it - 1]])
                    ev_recipB = orB
                    pend_ev[0] = (p, od_last[it - 1], exp_ops[it - 1],
                                  orA, orB, hp * d.TPC + qc * 512)
            flush_evac()
            att_t = att_last_t
            att_v = ev_mults
            att_s = exp_ops[it - 1]
        _qfm_cm.__exit__(None, None, None)

        with contextlib.ExitStack() as p3:
            T3 = lambda *a: p3.enter_context(nc.sbuf_tensor(*a))
            q0bd = T3("q0bd", [128, d.CH * d.H], BF16)
            kcb = [T3(f"kcb{i}", [128, d.CH * 512], BF16)
                   for i in range(4)]
            s0_s = T3("s0_s", [d.H, d.L], F32)
            p0_s = T3("p0_s", [d.H, d.L], BF16)
            p0T = T3("p0T", [128, d.KT * d.H], BF16)
            vb3 = [T3(f"vb3{i}", [128, d.KT * 128], BF16)
                   for i in range(2)]
            o0f = T3("o0f", [128, d.CH], BF16)
            m0 = T3("m0", [d.H, 4], F32)
            xp0_s = T3("xp0_s", [1, d.C], F32)
            g_scr = T3("g_scr", [1, d.C], F32)
            g_sc2 = T3("g_sc2", [1, 64], F32)
            cls_s = T3("cls_s", [128, d.CH], F32)
            wrT_s = T3("wrT_s", [128, d.CH * d.E], F32)
            gfm_s = T3("gfm_s", [d.E, 1], F32)
            wr_s = T3("wr_s", [d.E, d.C], F32)
            sel_s = T3("sel_s", [d.E, d.ERCH * 128], F32)

            CP = [p3.enter_context(nc.psum_tensor(f"CP{i}", [128, 512], F32))
                  for i in range(2)]
            CO = [p3.enter_context(nc.psum_tensor(f"CO{i}", [128, 16], F32))
                  for i in range(2)]
            CT = [p3.enter_context(nc.psum_tensor(f"CT{i}", [128, 16], BF16))
                  for i in range(2)]

            ld_wr = P.add("sd", lambda e: e.dma_start(out=wr_s[:], in_=wr),
                          deps=[att_t, att_v, att_s], stream="sonce")
            ld_sel = P.add("sd", lambda e: e.dma_start(
                out=sel_s[:], in_=sel), deps=[att_t, att_v, att_s],
                stream="sonce")
            lx0 = P.add("sd", lambda e: e.dma_start(
                out=xp0_s[:], in_=xbt[0:1, :]),
                deps=[att_t, att_v, att_s], stream="x0")
            o0prev = [None, None]
            lv3s = [None, None]
            for hp in range(2):
                for j8 in range(max(1, d.KT // 8)):
                    n8 = min(8, d.KT - j8 * 8)
                    lv3s[hp % 2] = P.add(
                        "sd", lambda e, hp=hp, j8=j8, n8=n8:
                        e.dma_start(
                            out=vb3[hp % 2][:, j8 * 1024:
                                            j8 * 1024 + n8 * 128]
                            .rearrange("p (k dd) -> p k dd", k=n8),
                            in_=vscr[hp, j8 * 8:j8 * 8 + n8].rearrange(
                                "k p dd -> p k dd")),
                        deps=[vo_lasts[0], vo_lasts[1],
                              att_t, att_v, att_s],
                        stream=f"vs{j8 % 4}")

            oz = P.add("g", lambda e: e.memset(q0bd[:], 0.0),
                       deps=[att_t, att_v, att_s])
            oqb = None
            for c in range(d.CH):
                oqb = P.add("v", lambda e, c=c: e.tensor_copy(
                    q0bd[0:64, c * d.H + 2 * c:c * d.H + 2 * c + 1],
                    q0f[0:64, c:c + 1]), deps=[oz, p1_v_tail, oqb])
                oqb = P.add("v", lambda e, c=c: e.tensor_copy(
                    q0bd[64:128, c * d.H + 2 * c + 1:c * d.H + 2 * c + 2],
                    q0f[64:128, c:c + 1]), deps=[oqb])
            os0 = []
            for ktc in range(d.GB):
                lk = P.add("sd", lambda e, ktc=ktc: e.dma_start(
                    out=kcb[ktc % 4].rearrange("p (h t) -> p h t", h=d.CH),
                    in_=kscr[:, :, ktc * 512:(ktc + 1) * 512].rearrange(
                        "h p t -> p h t")),
                    deps=[ko_lasts[0], ko_lasts[1], att_t, att_v, att_s]
                    + (os0[-4:-3] or []), stream=f"kc{ktc % 4}")
                mm = None
                for c in range(d.CH):
                    mm = P.add("t", lambda e, c=c, ktc=ktc: e.matmul(
                        CP[ktc % 2][0:d.H, :],
                        q0bd[:, c * d.H:(c + 1) * d.H],
                        kcb[ktc % 4][:, c * 512:(c + 1) * 512],
                        start=(c == 0), stop=(c == d.CH - 1)),
                        deps=[oqb, lk, att_v, os0[-2] if len(os0) >= 2
                              else None])
                oc_ = P.add("v", lambda e, ktc=ktc: e.tensor_copy(
                    s0_s[:, ktc * 512:(ktc + 1) * 512], CP[ktc % 2][0:d.H, :]),
                    deps=[mm, att_v])
                os0.append(oc_)
            om0 = P.add("v", lambda e: e.reduce_max(
                m0[:, 0:1], s0_s[:], axis=AX.X), deps=[os0[-1]])
            om0n = P.add("v", lambda e: e.tensor_scalar(
                out=m0[:, 1:2], in0=m0[:, 0:1], scalar1=-LAM, scalar2=None,
                op0=ALU.mult), deps=[om0])
            oe0 = P.add("s", lambda e: e.activation(
                out=p0_s[:], in_=s0_s[:], func=AF.Exp, bias=m0[:, 1:2],
                scale=LAM, accum_out=m0[:, 2:3]), deps=[om0n, os0[-1]])
            or0 = P.add("v", lambda e: e.reciprocal(m0[:, 3:4], m0[:, 2:3]),
                        deps=[oe0])
            op0n = P.add("v", lambda e: e.tensor_scalar(
                out=p0_s[:], in0=p0_s[:], scalar1=m0[:, 3:4], scalar2=None,
                op0=ALU.mult), deps=[or0, oe0])
            otp_prev = [None, None]
            op0T = None
            for kt in range(d.KT):
                ct_ = CT[kt % 2]
                otp = P.add("t", lambda e, kt=kt, ct_=ct_: e.transpose(
                    ct_[:, 0:d.H],
                    p0_s[:, kt * 128:(kt + 1) * 128], idn_s[0:d.H, 0:d.H]),
                    deps=[op0n, ld_id, otp_prev[kt % 2]])
                oev = P.add("v", lambda e, kt=kt, ct_=ct_: e.tensor_copy(
                    p0T[:, kt * d.H:(kt + 1) * d.H], ct_[:, 0:d.H]),
                    deps=[otp])
                otp_prev[kt % 2] = oev
                op0T = oev
            o0evs = []
            for hp in range(d.HP):
                if hp >= 2:
                    for j8 in range(max(1, d.KT // 8)):
                        n8 = min(8, d.KT - j8 * 8)
                        lv3s[hp % 2] = P.add(
                            "sd", lambda e, hp=hp, j8=j8, n8=n8:
                            e.dma_start(
                                out=vb3[hp % 2][:, j8 * 1024:
                                                j8 * 1024 + n8 * 128]
                                .rearrange("p (k dd) -> p k dd", k=n8),
                                in_=vscr[hp, j8 * 8:j8 * 8 + n8].rearrange(
                                    "k p dd -> p k dd")),
                            deps=[o0prev[hp % 2], vo_lasts[0], vo_lasts[1]],
                            stream=f"vs{j8 % 4}")
                co = CO[hp % 2]
                mm = None
                for kt in range(d.KT):
                    mm = P.add("t", lambda e, kt=kt, co=co, hp=hp: e.matmul(
                        co[:, 0:d.H], vb3[hp % 2][:, kt * 128:(kt + 1) * 128],
                        p0T[:, kt * d.H:(kt + 1) * d.H],
                        start=(kt == 0), stop=(kt == d.KT - 1)),
                        deps=[lv3s[hp % 2], op0T,
                              o0evs[-2] if len(o0evs) >= 2 else None])
                oe1 = P.add("v", lambda e, hp=hp, co=co: e.tensor_copy(
                    o0f[0:64, hp:hp + 1], co[0:64, 2 * hp:2 * hp + 1]),
                    deps=[mm])
                oe2 = P.add("v", lambda e, hp=hp, co=co: e.tensor_copy(
                    o0f[64:128, hp:hp + 1],
                    co[64:128, 2 * hp + 1:2 * hp + 2]), deps=[oe1])
                o0evs.append(oe2)
                o0prev[hp % 2] = mm
            oxp0 = []
            for fc in range(d.NFC):
                mm = None
                for c in range(d.CH):
                    mm = P.add("t", lambda e, c=c, fc=fc: e.matmul(
                        CP[0][0:1, 0:d.FC],
                        o0f[:, c:c + 1],
                        wp_sb[:, c * d.C + fc * d.FC:c * d.C + (fc + 1) * d.FC],
                        start=(c == 0), stop=False),
                        deps=[o0evs[-1], ld_wp,
                              oxp0[-1] if oxp0 else os0[-1]])
                mb = P.add("t", lambda e, fc=fc: e.matmul(
                    CP[0][0:1, 0:d.FC], ones1f[0:1, 0:1],
                    bp_s[0:1, fc * d.FC:(fc + 1) * d.FC],
                    start=False, stop=True), deps=[ld_bp, o_ones1])
                oadd = P.add("v", lambda e, fc=fc: e.tensor_tensor(
                    out=xp0_s[0:1, fc * d.FC:(fc + 1) * d.FC],
                    in0=CP[0][0:1, 0:d.FC],
                    in1=xp0_s[0:1, fc * d.FC:(fc + 1) * d.FC],
                    op=ALU.add), deps=[mb, lx0])
                oxp0.append(oadd)
            og1 = P.add("v", lambda e: e.reduce_sum(
                g_sc2[0:1, 0:1], xp0_s[:], axis=AX.X), deps=[oxp0[-1]])
            og2 = P.add("v", lambda e: e.tensor_scalar(
                out=g_sc2[0:1, 1:2], in0=g_sc2[0:1, 0:1],
                scalar1=1.0 / d.C, scalar2=None, op0=ALU.mult), deps=[og1])
            og3 = P.add("v", lambda e: e.tensor_scalar(
                out=g_scr[:], in0=xp0_s[:], scalar1=g_sc2[0:1, 1:2],
                scalar2=None, op0=ALU.subtract), deps=[og2])
            og4 = P.add("v", lambda e: e.tensor_tensor(
                out=g_scr[:], in0=g_scr[:], in1=g_scr[:], op=ALU.mult),
                deps=[og3])
            og5 = P.add("v", lambda e: e.reduce_sum(
                g_sc2[0:1, 2:3], g_scr[:], axis=AX.X), deps=[og4])
            og6 = P.add("v", lambda e: e.tensor_scalar(
                out=g_sc2[0:1, 3:4], in0=g_sc2[0:1, 2:3],
                scalar1=1.0 / d.C, scalar2=None, op0=ALU.mult), deps=[og5])
            og7 = P.add("s", lambda e: e.activation(
                out=g_sc2[0:1, 4:5], in_=g_sc2[0:1, 3:4], func=AF.Ln,
                bias=cst[0:1, 0:1], scale=1.0), deps=[og6, oc1])
            og8 = P.add("s", lambda e: e.activation(
                out=g_sc2[0:1, 5:6], in_=g_sc2[0:1, 4:5], func=AF.Exp,
                bias=cst[0:1, 1:2], scale=-0.5), deps=[og7])
            og9 = P.add("v", lambda e: e.tensor_scalar(
                out=g_scr[:], in0=xp0_s[:], scalar1=g_sc2[0:1, 1:2],
                scalar2=g_sc2[0:1, 5:6], op0=ALU.subtract, op1=ALU.mult),
                deps=[og8, og5])
            ocs = []
            for c in range(d.CH):
                ocs.append(P.add("t", lambda e, c=c: e.transpose(
                    CP[1][:, c:c + 1], g_scr[0:1, c * 128:(c + 1) * 128],
                    idf_s[0:1, 0:1]), deps=[og9, ld_if, oxp0[-1]]))
            occ = P.add("v", lambda e: e.tensor_copy(
                cls_s[:], CP[1][:, 0:d.CH]), deps=ocs)
            ows = []
            for c in range(d.CH):
                ows.append(P.add("t", lambda e, c=c: e.transpose(
                    CP[1][:, 16 + c * d.E:16 + (c + 1) * d.E],
                    wr_s[:, c * 128:(c + 1) * 128], idf_s[0:d.E, 0:d.E]),
                    deps=[ld_wr, ld_if, occ if c == 0 else None]))
            owc = P.add("v", lambda e: e.tensor_copy(
                wrT_s[:], CP[1][:, 16:16 + d.CH * d.E]), deps=ows)
            olg = None
            for c in range(d.CH):
                olg = P.add("t", lambda e, c=c: e.matmul(
                    CP[0][0:1, 16:16 + d.E], cls_s[:, c:c + 1],
                    wrT_s[:, c * d.E:(c + 1) * d.E],
                    start=(c == 0), stop=(c == d.CH - 1)),
                    deps=[occ, owc, oxp0[-1]])
            olgc = P.add("v", lambda e: e.tensor_copy(
                g_sc2[0:1, 8:8 + d.E], CP[0][0:1, 16:16 + d.E]),
                deps=[olg])
            om8 = P.add("v", lambda e: e.max(
                g_sc2[0:1, 16:24], g_sc2[0:1, 8:8 + d.E]), deps=[olgc])
            okp = P.add("v", lambda e: e.tensor_scalar(
                out=g_sc2[0:1, 24:24 + d.E], in0=g_sc2[0:1, 8:8 + d.E],
                scalar1=g_sc2[0:1, 17:18], scalar2=None, op0=ALU.is_ge),
                deps=[om8])
            om1n = P.add("v", lambda e: e.tensor_scalar(
                out=g_sc2[0:1, 32:33], in0=g_sc2[0:1, 16:17], scalar1=-1.0,
                scalar2=None, op0=ALU.mult), deps=[om8])
            oen = P.add("s", lambda e: e.activation(
                out=g_sc2[0:1, 40:40 + d.E], in_=g_sc2[0:1, 8:8 + d.E],
                func=AF.Exp, bias=g_sc2[0:1, 32:33], scale=1.0),
                deps=[om1n, okp])
            ogu = P.add("v", lambda e: e.tensor_tensor(
                out=g_sc2[0:1, 48:48 + d.E], in0=g_sc2[0:1, 24:24 + d.E],
                in1=g_sc2[0:1, 40:40 + d.E], op=ALU.mult), deps=[oen])
            odn = P.add("v", lambda e: e.reduce_sum(
                g_sc2[0:1, 56:57], g_sc2[0:1, 48:48 + d.E], axis=AX.X),
                deps=[ogu])
            ord_ = P.add("v", lambda e: e.reciprocal(
                g_sc2[0:1, 57:58], g_sc2[0:1, 56:57]), deps=[odn])
            ogt = P.add("v", lambda e: e.tensor_scalar(
                out=g_sc2[0:1, 48:48 + d.E], in0=g_sc2[0:1, 48:48 + d.E],
                scalar1=g_sc2[0:1, 57:58], scalar2=None, op0=ALU.mult),
                deps=[ord_])
            ogf = P.add("t", lambda e: e.transpose(
                CP[0][0:d.E, 32:33], g_sc2[0:1, 48:48 + d.E],
                idf_s[0:1, 0:1]), deps=[ogt, olg])
            ogfc = P.add("v", lambda e: e.tensor_copy(
                gfm_s[:], CP[0][0:d.E, 32:33]), deps=[ogf])
            oGs = []
            for ec in range(d.ERCH):
                oGs.append(P.add("t", lambda e, ec=ec: e.matmul(
                    CP[1][:, 48 + ec:48 + ec + 1],
                    sel_s[:, ec * 128:(ec + 1) * 128],
                    gfm_s[:], start=True, stop=True),
                    deps=[ogfc, ld_sel, owc]))
            oGc = P.add("v", lambda e: e.tensor_copy(
                G_s[:], CP[1][:, 48:48 + d.ERCH]), deps=oGs)
            p3_t_tail = oGs[-1]
            p3_v_tail = oGc

        _xpost_cm = nc.sbuf_tensor("xpost", [128, d.NT * d.C], F32,
                                   side="right")
        xpost = _xpost_cm.__enter__()
        with contextlib.ExitStack() as p4:
            T4 = lambda *a: p4.enter_context(nc.sbuf_tensor(*a))
            xs_t = [T4(f"xs4{i}", [128, d.C], F32) for i in range(4)]
            PP = [p4.enter_context(nc.psum_tensor(f"PP{i}", [128, 512],
                                                  F32)) for i in range(4)]
            xs_free = [None] * 4
            pslot_free = [p3_v_tail] * 4
            xpw = []
            for tt in range(d.NT):
                lx = P.add("sd", lambda e, tt=tt: e.dma_start(
                    out=xs_t[tt % 4][:], in_=xst[tt * 128:(tt + 1) * 128, :]),
                    deps=[xs_free[tt % 4], p3_t_tail, p3_v_tail],
                    stream=f"xp{tt % 2}")
                evs = []
                for fc in range(d.NFC):
                    slot = (tt * d.NFC + fc) % 4
                    ps = PP[slot]
                    mm = None
                    for c in range(d.CH):
                        mm = P.add("t", lambda e, c=c, tt=tt, fc=fc, ps=ps:
                                   e.matmul(
                            ps[:, 0:d.FC],
                            ofm[:, c * d.TPC + tt * 128:
                                c * d.TPC + (tt + 1) * 128],
                            wp_sb[:, c * d.C + fc * d.FC:
                                  c * d.C + (fc + 1) * d.FC],
                            start=(c == 0), stop=False),
                            deps=[att_v, ld_wp, pslot_free[slot],
                                  p3_t_tail])
                    mb = P.add("t", lambda e, fc=fc, ps=ps: e.matmul(
                        ps[:, 0:d.FC], ones1f[:],
                        bp_s[0:1, fc * d.FC:(fc + 1) * d.FC],
                        start=False, stop=True), deps=[ld_bp, o_ones1])
                    oev = P.add("v", lambda e, tt=tt, fc=fc, ps=ps:
                                e.tensor_tensor(
                        out=xpost[:, tt * d.C + fc * d.FC:
                                  tt * d.C + (fc + 1) * d.FC],
                        in0=ps[:, 0:d.FC],
                        in1=xs_t[tt % 4][:, fc * d.FC:(fc + 1) * d.FC],
                        op=ALU.add), deps=[mb, lx])
                    pslot_free[slot] = oev
                    evs.append(oev)
                xs_free[tt % 4] = evs[-1]
                xpw.append(evs[-1])

        _ofm_cm.__exit__(None, None, None)

        with contextlib.ExitStack() as p5:
            T5 = lambda *a: p5.enter_context(nc.sbuf_tensor(*a))
            xfm2 = T5("xfm2", [128, d.CH * d.TPC], BF16)
            xlnb2 = [T5(f"xlnb2{i}", [128, d.C], BF16) for i in range(2)]
            sc5 = {"s1": T5("s51", [128, 4], F32),
                   "jnk": T5("s5jnk", [128, d.C], BF16),
                   "s2": T5("s52", [128, 4], F32),
                   "sqv": T5("s5sqv", [128, d.C], F32),
                   "s6": T5("s56", [128, 4], F32),
                   "z": cst[0:128, 1:2], "oz": oc2}
            mean2 = T5("mean2", [128, d.NT], F32)
            var2 = T5("var2", [128, d.NT], F32)
            rstd2 = T5("rstd2", [128, d.NT], F32)
            mur2 = T5("mur2", [128, d.NT], F32)
            r1_2 = T5("r1_2", [128, d.NT], F32)
            _ab_cm = nc.sbuf_tensor("a_sb", [d.R, d.E * d.C], BF16,
                                    side="right")
            a_sb = _ab_cm.__enter__()
            _bb_cm = nc.sbuf_tensor("b_sb", [128, d.E * d.CH * d.R],
                                    BF16, side="right")
            b_sb = _bb_cm.__enter__()
            aT_s = T5("aT_s", [128, d.CH * d.ERCH * 128], BF16)
            bT_s = T5("bT_s", [128, d.ERCH * d.C], BF16)
            mid_s = T5("mid_s", [128, d.ERCH * d.TPC], BF16)
            xo = [T5(f"xo5{i}", [128, d.C], F32) for i in range(2)]
            sc1b = [T5(f"sc1b{i}", [128, d.FC], F32) for i in range(2)]
            sc2b = [T5(f"sc2b{i}", [128, d.FC], F32) for i in range(2)]

            FSG = [p5.enter_context(nc.psum_tensor(f"FSG{i}", [128, 1024],
                                                   F32)) for i in range(2)]
            PT5 = [p5.enter_context(nc.psum_tensor(f"PT5{i}", [128, 512],
                                                   BF16)) for i in range(2)]

            ld_a = P.add("gd", lambda e: e.dma_start(
                out=a_sb.rearrange("r (e h) -> r e h", e=d.E),
                in_=aw.rearrange("e r h -> r e h")),
                deps=[p3_t_tail, p3_v_tail, xpw[-1]], stream="ab")
            ld_bs = []
            for e_ in range(d.E):
                ld_bs.append(P.add("gd", lambda e, e_=e_: e.dma_start(
                    out=b_sb[:, e_ * d.CH * d.R:(e_ + 1) * d.CH * d.R]
                    .rearrange("p (c r) -> p c r", c=d.CH),
                    in_=bw[e_].rearrange("(c p) r -> p c r", p=128)),
                    stream="ab"))

            ln_done = []
            for t in range(d.NT):
                ln_done.append(ln_pass1(
                    P, d, xpost[:, t * d.C:(t + 1) * d.C], mean2, var2, t,
                    [xpw[t]] + ([p3_t_tail, xpw[-1]] if t == 0 else []),
                    sc5))
            oE2, oM2 = ln_finish(P, d, mean2, var2, rstd2, mur2, ln_done,
                                 cst, oc2)
            oR1 = P.add("v", lambda e: e.tensor_scalar(
                out=r1_2[:], in0=rstd2[:], scalar1=1.0, scalar2=None,
                op0=ALU.add), deps=[oE2])

            bank_ev = [None, None]
            xln_rd = [None, None]
            xfm_done_sv = [None, None]
            aT_done_sv = [None, None]
            bT_done_sv = [None, None]
            bsz = min(4, d.CH)
            nblk = d.CH // bsz
            bi = 0
            for t in range(d.NT):
                oxl = P.add("v", lambda e, t=t: e.tensor_scalar(
                    out=xlnb2[t % 2][:], in0=xpost[:, t * d.C:(t + 1) * d.C],
                    scalar1=rstd2[:, t:t + 1], scalar2=mur2[:, t:t + 1],
                    op0=ALU.mult, op1=ALU.subtract),
                    deps=[oE2, oM2, xln_rd[t % 2]])
                for blk in range(nblk):
                    hb = bi % 2
                    pt = PT5[hb]
                    last_tp = None
                    for j in range(bsz):
                        c = blk * bsz + j
                        last_tp = P.add("t", lambda e, t=t, c=c, j=j, pt=pt:
                                        e.transpose(
                            pt[:, j * 128:(j + 1) * 128],
                            xlnb2[t % 2][:, c * 128:(c + 1) * 128],
                            idn_s[:]),
                            deps=([oxl, ld_id, bank_ev[hb]] if j == 0
                                  else [oxl]))
                    dst = lambda t=t, blk=blk: xfm2.rearrange(
                        "p (c x) -> p c x", c=d.CH)[
                        :, blk * bsz:(blk + 1) * bsz,
                        t * 128:(t + 1) * 128]
                    src = lambda pt=pt: pt[:, 0:bsz * 128].rearrange(
                        "p (c x) -> p c x", x=128)
                    if bi % 2 == 0:
                        ev = P.add("s", lambda e, dst=dst, src=src:
                                   e.activation(
                            out=dst(), in_=src(), func=AF.Copy),
                            deps=[last_tp])
                    else:
                        ev = P.add("v", lambda e, dst=dst, src=src:
                                   e.tensor_copy(dst(), src()),
                            deps=[last_tp])
                    bank_ev[hb] = ev
                    xfm_done_sv[bi % 2] = ev
                    bi += 1
                xln_rd[t % 2] = last_tp

            for e_ in range(d.E):
                hb = bi % 2
                pt = PT5[hb]
                last_tp = None
                for c in range(d.CH):
                    last_tp = P.add("t", lambda e, e_=e_, c=c, pt=pt:
                                    e.transpose(
                        pt[:, c * d.R:(c + 1) * d.R],
                        a_sb[:, e_ * d.C + c * 128:e_ * d.C + (c + 1) * 128],
                        idn_s[0:d.R, 0:d.R]),
                        deps=[ld_a, ld_id] + ([bank_ev[hb]] if c == 0
                                              else []))
                off = (e_ // d.EPC) * 128 + (e_ % d.EPC) * d.R
                dst = lambda off=off: aT_s.rearrange(
                    "p (c x) -> p c x", x=d.ERCH * 128)[:, :, off:off + d.R]
                src = lambda pt=pt: pt[:, 0:d.CH * d.R].rearrange(
                    "p (c x) -> p c x", x=d.R)
                if bi % 2 == 0:
                    ev = P.add("s", lambda e, dst=dst, src=src: e.activation(
                        out=dst(), in_=src(), func=AF.Copy), deps=[last_tp])
                else:
                    ev = P.add("v", lambda e, dst=dst, src=src:
                               e.tensor_copy(dst(), src()), deps=[last_tp])
                bank_ev[hb] = ev
                aT_done_sv[bi % 2] = ev
                bi += 1
            for e_ in range(d.E):
                for blk in range(nblk):
                    hb = bi % 2
                    pt = PT5[hb]
                    last_tp = None
                    for j in range(bsz):
                        c = blk * bsz + j
                        last_tp = P.add("t", lambda e, e_=e_, c=c, j=j,
                                        pt=pt: e.transpose(
                            pt[0:d.R, j * 128:(j + 1) * 128],
                            b_sb[:, e_ * d.CH * d.R + c * d.R:
                                 e_ * d.CH * d.R + (c + 1) * d.R],
                            idn_s[:]),
                            deps=[ld_bs[e_], ld_id]
                            + ([bank_ev[hb]] if j == 0 else []))
                    row = (e_ % d.EPC) * d.R
                    col = (e_ // d.EPC) * d.C + blk * bsz * 128
                    w_ = bsz * 128
                    if bi % 2 == 0:
                        ev = P.add("s", lambda e, row=row, col=col, w_=w_,
                                   pt=pt: e.activation(
                            out=bT_s[row:row + d.R, col:col + w_],
                            in_=pt[0:d.R, 0:w_], func=AF.Copy),
                            deps=[last_tp])
                    else:
                        ev = P.add("v", lambda e, row=row, col=col, w_=w_,
                                   pt=pt: e.tensor_copy(
                            bT_s[row:row + d.R, col:col + w_],
                            pt[0:d.R, 0:w_]), deps=[last_tp])
                    bank_ev[hb] = ev
                    bT_done_sv[bi % 2] = ev
                    bi += 1

            _bb_cm.__exit__(None, None, None)
            _ab_cm.__exit__(None, None, None)
            slot_free = [bT_done_sv[0], aT_done_sv[0],
                         bT_done_sv[1], aT_done_sv[1]]
            mi = 0
            mid_done = None
            for ec in range(d.ERCH):
                for tc in range(d.NTC):
                    slot = mi % 4
                    ps = FSG[slot // 2][:, (slot % 2) * 512:
                                        (slot % 2) * 512 + 512]
                    mm = None
                    for c in range(d.CH):
                        mm = P.add("t", lambda e, ec=ec, tc=tc, c=c, ps=ps:
                                   e.matmul(
                            ps, aT_s[:, (c * d.ERCH + ec) * 128:
                                     (c * d.ERCH + ec + 1) * 128],
                            xfm2[:, c * d.TPC + tc * 512:
                                 c * d.TPC + (tc + 1) * 512],
                            start=(c == 0), stop=(c == d.CH - 1)),
                            deps=xfm_done_sv + aT_done_sv
                            + [slot_free[slot]])
                    oev = P.add("v", lambda e, ec=ec, tc=tc, ps=ps:
                                e.tensor_scalar(
                        out=mid_s[:, ec * d.TPC + tc * 512:
                                  ec * d.TPC + (tc + 1) * 512],
                        in0=ps, scalar1=G_s[:, ec:ec + 1], scalar2=None,
                        op0=ALU.mult), deps=[mm, p3_v_tail])
                    slot_free[slot] = oev
                    mid_done = oev
                    mi += 1

            xo_free = [None, None]
            ps_free = [mid_done] * 4
            prev_oo = [None, None]
            ei = 0
            for tt in range(d.NT):
                evs = []
                for fc in range(d.NFC):
                    slot = ei % 4
                    ps = FSG[slot // 2][:, (slot % 2) * 512:
                                        (slot % 2) * 512 + d.FC]
                    mm = None
                    for ec in range(d.ERCH):
                        mm = P.add("t", lambda e, tt=tt, fc=fc, ec=ec,
                                   ps=ps: e.matmul(
                            ps, mid_s[:, ec * d.TPC + tt * 128:
                                      ec * d.TPC + (tt + 1) * 128],
                            bT_s[:, ec * d.C + fc * d.FC:
                                 ec * d.C + (fc + 1) * d.FC],
                            start=(ec == 0), stop=(ec == d.ERCH - 1)),
                            deps=[mid_done] + bT_done_sv
                            + [ps_free[slot]])
                    oa = P.add("v", lambda e, tt=tt, ps=ps, slot=slot:
                               e.tensor_scalar(
                        out=sc1b[slot % 2][:, 0:d.FC], in0=ps,
                        scalar1=mur2[:, tt:tt + 1], scalar2=None,
                        op0=ALU.subtract),
                        deps=[mm, oM2, prev_oo[slot % 2]])
                    ps_free[slot] = oa
                    ob = P.add("v", lambda e, tt=tt, fc=fc, slot=slot:
                               e.tensor_scalar(
                        out=sc2b[slot % 2][:, 0:d.FC],
                        in0=xpost[:, tt * d.C + fc * d.FC:
                                  tt * d.C + (fc + 1) * d.FC],
                        scalar1=r1_2[:, tt:tt + 1], scalar2=None,
                        op0=ALU.mult), deps=[oR1, prev_oo[slot % 2]])
                    oo = P.add("v", lambda e, tt=tt, fc=fc, slot=slot:
                               e.tensor_tensor(
                        out=xo[tt % 2][:, fc * d.FC:(fc + 1) * d.FC],
                        in0=sc1b[slot % 2][:, 0:d.FC],
                        in1=sc2b[slot % 2][:, 0:d.FC],
                        op=ALU.add), deps=[oa, ob, xo_free[tt % 2]])
                    prev_oo[slot % 2] = oo
                    evs.append(oo)
                    ei += 1
                od = P.add("gd", lambda e, tt=tt: e.dma_start(
                    out=out[tt * 128:(tt + 1) * 128, :],
                    in_=xo[tt % 2][:]), deps=evs, stream="out")
                xo_free[tt % 2] = od

        _xpost_cm.__exit__(None, None, None)
        P.emit(nc)
    return nc


def host_consts(d):
    i32 = np.arange(0, d.D, 2) / d.D
    theta = (1.0 / (10000.0 ** i32)) * 3.0
    pos = np.arange(d.L, dtype=np.float64)
    ang = pos[:, None] * theta[None, :]
    cos = np.cos(ang)
    sin = np.sin(ang)
    p = np.arange(128)
    pair = (p % d.D) // 2
    sgn = np.where(p % 2 == 0, -1.0, 1.0)
    cosF = np.ascontiguousarray(cos[:, pair].T.astype(np.float32))
    sinF = np.ascontiguousarray((sin[:, pair] * sgn[None, :]).T
                                .astype(np.float32))
    idn = np.eye(128, dtype=BF)
    idf = np.eye(128, dtype=np.float32)
    sel = np.zeros((d.E, d.ERCH * 128), np.float32)
    for e_ in range(d.E):
        ec = e_ // d.EPC
        j0 = (e_ % d.EPC) * d.R
        sel[e_, ec * 128 + j0: ec * 128 + j0 + d.R] = 1.0
    return cosF, sinF, idn, idf, sel


def run_fused(d, x, wqkv, wproj, bproj, wr, aw, bw):
    ncores = d.NB * d.G
    cores = list(range(ncores))
    cosF, sinF, idn, idf, sel = host_consts(d)
    nc = build_fused(d)
    im = []
    for c in cores:
        b, r = c // d.G, c % d.G
        off = r * d.TPC
        im.append({
            "xb": np.ascontiguousarray(x[b]),
            "xs": np.ascontiguousarray(x[b, off:off + d.TPC]),
            "wqkv": wqkv, "wproj": wproj, "bproj": bproj,
            "cosK": cosF, "sinK": sinF,
            "cosQ": np.ascontiguousarray(cosF[:, off:off + d.TPC]),
            "sinQ": np.ascontiguousarray(sinF[:, off:off + d.TPC]),
            "wr": wr, "aw": aw, "bw": bw, "sel": sel,
            "idn": idn, "idf": idf,
        })
    r_ = run_bass_kernel_spmd(nc, im, core_ids=cores)
    outp = np.empty((d.NB, d.L, d.C), np.float32)
    for c in cores:
        outp[c // d.G, (c % d.G) * d.TPC:(c % d.G + 1) * d.TPC] = \
            np.asarray(r_.results[c]["out"], np.float32)
    return outp


def kernel(L0_attn_x, L0_attn_w_qkv, L0_attn_w_proj, L0_attn_b_proj,
           L0_ffn_Wr, L0_ffn_A, L0_ffn_Bw):
    d = dd(dims_real())
    x = np.ascontiguousarray(np.asarray(L0_attn_x, np.float32))
    wqkv = np.ascontiguousarray(np.asarray(L0_attn_w_qkv, np.float32))
    wproj = np.ascontiguousarray(np.asarray(L0_attn_w_proj, np.float32))
    bproj = np.ascontiguousarray(
        np.asarray(L0_attn_b_proj, np.float32).reshape(1, d.C))
    wr = np.ascontiguousarray(np.asarray(L0_ffn_Wr, np.float32))
    aw = np.ascontiguousarray(np.asarray(L0_ffn_A, np.float32))
    bw = np.ascontiguousarray(np.asarray(L0_ffn_Bw, np.float32))
    return run_fused(d, x, wqkv, wproj, bproj, wr, aw, bw)


# version 23 (final)
# speedup vs baseline: 1.1893x; 1.0024x over previous
import numpy as np
import ml_dtypes
from types import SimpleNamespace
import concourse.bass as bass
import concourse.mybir as mybir
from concourse.bass_utils import run_bass_kernel_spmd

F32 = mybir.dt.float32
BF16 = mybir.dt.bfloat16
AF = mybir.ActivationFunctionType
ALU = mybir.AluOpType
AX = mybir.AxisListType
BF = ml_dtypes.bfloat16

LAM = 0.125
EPS = 1e-5

KINDS = ("sd", "gd", "g", "t", "v", "s")
ENGINE_OF = {"sd": "sync", "gd": "gpsimd", "g": "gpsimd",
             "t": "tensor", "v": "vector", "s": "scalar"}
ENGINES = ("sync", "gpsimd", "tensor", "vector", "scalar")


def dims_real():
    return SimpleNamespace(C=1024, H=16, TPC=2048, G=4, NB=2, E=8, R=64)


def dd(d):
    d.D = 64
    d.HP = d.H // 2
    d.CH = d.C // 128
    d.NT = d.TPC // 128
    d.NTC = d.TPC // 512
    d.L = d.G * d.TPC
    d.KT = d.L // 128
    d.NTB = d.L // 128
    d.GB = d.L // 512
    d.F3 = 3 * d.C
    d.FC = min(512, d.C)
    d.NFC = d.C // d.FC
    d.ER = d.E * d.R
    d.ERCH = d.ER // 128
    d.EPC = 128 // d.R
    assert d.CH == d.HP, "cls q0 block-diag assumes CH == HP"
    return d


class Prog:

    want_oplog = False

    def __init__(self):
        self.ops = []
        self.cum = {}
        self.last_stream = {}
        self.oplog = []

    def add(self, kind, fn, deps=(), stream=None):
        if kind in ("sd", "gd"):
            assert stream is not None
            semkey = "D_" + stream
            inc = 16
            extra = (self.last_stream.get(stream),)
        else:
            semkey = kind
            inc = 1
            extra = ()
        self.cum[semkey] = self.cum.get(semkey, 0) + inc
        op = SimpleNamespace(kind=kind, fn=fn, inc=inc, semkey=semkey,
                             deps=tuple(x for x in (tuple(deps) + extra)
                                        if x is not None),
                             scum=self.cum[semkey],
                             line=fn.__code__.co_firstlineno)
        self.ops.append(op)
        if stream is not None:
            self.last_stream[stream] = op
        return op

    def emit(self, nc):
        Prog.last = self
        import contextlib
        with contextlib.ExitStack() as st:
            sems = {k: st.enter_context(nc.semaphore(name=f"sm_{k}"))
                    for k in self.cum}
            blk = st.enter_context(nc.Block())
            for ename in ENGINES:
                kinds_here = tuple(k for k in KINDS if ENGINE_OF[k] == ename)

                def emit_engine(eng, ename=ename, kinds_here=kinds_here):
                    hw = {}
                    for op in self.ops:
                        if op.kind not in kinds_here:
                            continue
                        need = {}
                        for dp in op.deps:
                            if dp.semkey == op.semkey and op.kind == "t" \
                                    and dp.kind == "t":
                                continue
                            need[dp.semkey] = max(need.get(dp.semkey, 0),
                                                  dp.scum)
                        for k, v in need.items():
                            if v > 0 and v > hw.get(k, 0):
                                eng.wait_ge(sems[k], v)
                                hw[k] = v
                        _inst = op.fn(eng)
                        _inst.then_inc(sems[op.semkey], op.inc)
                        if Prog.want_oplog:
                            self.oplog.append(
                                (nc.next_id(), op.line, op.kind))
                    if ename == "gpsimd":
                        for k, v in self.cum.items():
                            eng.wait_ge(sems[k], v)
                getattr(blk, ename)(emit_engine)
        return nc


def ln_pass1(P, d, src_ap, mean_all, var_all, t, dep_src, sc):
    if t % 2 == 0:
        o1 = P.add("s", lambda e, a=src_ap: e.activation(
            out=sc["jnk"][:], in_=a, func=AF.Copy, bias=0.0, scale=1.0,
            accum_out=sc["s1"][:, 0:1]),
            deps=list(dep_src) + [sc.get("prev_o2"), sc.get("prev_o3")])
        o3 = P.add("s", lambda e, a=src_ap: e.activation(
            out=sc["jnk"][:], in_=a, func=AF.Square, bias=sc["z"], scale=1.0,
            accum_out=sc["s2"][:, 0:1]),
            deps=list(dep_src) + [sc["oz"], sc.get("prev_o5"), o1])
        sc["prev_o3"] = o3
        c1, c2 = sc["s1"][:, 0:1], sc["s2"][:, 0:1]
    else:
        o1 = P.add("v", lambda e, a=src_ap: e.reduce_sum(
            sc["s1"][:, 1:2], a, axis=AX.X),
            deps=list(dep_src) + [sc.get("prev_o2v")])
        osq = P.add("v", lambda e, a=src_ap: e.tensor_tensor(
            out=sc["sqv"][:], in0=a, in1=a, op=ALU.mult),
            deps=list(dep_src) + [sc.get("prev_o5v")])
        o3 = P.add("v", lambda e: e.reduce_sum(
            sc["s2"][:, 2:3], sc["sqv"][:], axis=AX.X), deps=[osq])
        c1, c2 = sc["s1"][:, 1:2], sc["s2"][:, 2:3]
    o2 = P.add("v", lambda e, t=t, c1=c1: e.tensor_scalar(
        out=mean_all[:, t:t + 1], in0=c1, scalar1=1.0 / d.C,
        scalar2=None, op0=ALU.mult), deps=[o1])
    o5 = P.add("v", lambda e, c2=c2: e.tensor_scalar(
        out=sc["s2"][:, 1:2], in0=c2, scalar1=1.0 / d.C,
        scalar2=None, op0=ALU.mult), deps=[o3, sc.get("prev_o7")])
    o6 = P.add("v", lambda e, t=t: e.tensor_tensor(
        out=sc["s6"][:, 0:1], in0=mean_all[:, t:t + 1],
        in1=mean_all[:, t:t + 1], op=ALU.mult),
        deps=[o2, sc.get("prev_o7")])
    o7 = P.add("v", lambda e, t=t: e.tensor_tensor(
        out=var_all[:, t:t + 1], in0=sc["s2"][:, 1:2], in1=sc["s6"][:, 0:1],
        op=ALU.subtract), deps=[o5, o6])
    if t % 2 == 0:
        sc["prev_o2"] = o2
        sc["prev_o5"] = o5
    else:
        sc["prev_o2v"] = o2
        sc["prev_o5v"] = o3
    sc["prev_o7"] = o7
    return o7


def ln_finish(P, d, mean_all, var_all, rstd_all, mur_all, dep, cst, ocst):
    oL = P.add("s", lambda e: e.activation(
        out=var_all[:], in_=var_all[:], func=AF.Ln, bias=cst[:, 0:1],
        scale=1.0), deps=list(dep) + [ocst])
    oE = P.add("s", lambda e: e.activation(
        out=rstd_all[:], in_=var_all[:], func=AF.Exp, bias=cst[:, 1:2],
        scale=-0.5), deps=[oL])
    oM = P.add("v", lambda e: e.tensor_tensor(
        out=mur_all[:], in0=mean_all[:], in1=rstd_all[:], op=ALU.mult),
        deps=[oE])
    return oE, oM


def build_fused(d):
    nc = bass.Bass("TRN2", target_bir_lowering=False, debug=False)
    P = Prog()

    xbt = nc.dram_tensor("xb", [d.L, d.C], F32, kind="ExternalInput").ap()
    xst = nc.dram_tensor("xs", [d.TPC, d.C], F32, kind="ExternalInput").ap()
    wqkv = nc.dram_tensor("wqkv", [d.C, d.F3], F32, kind="ExternalInput").ap()
    cosK = nc.dram_tensor("cosK", [128, d.L], F32, kind="ExternalInput").ap()
    sinK = nc.dram_tensor("sinK", [128, d.L], F32, kind="ExternalInput").ap()
    cosQ = nc.dram_tensor("cosQ", [128, d.TPC], F32, kind="ExternalInput").ap()
    sinQ = nc.dram_tensor("sinQ", [128, d.TPC], F32, kind="ExternalInput").ap()
    wproj = nc.dram_tensor("wproj", [d.C, d.C], F32, kind="ExternalInput").ap()
    bproj = nc.dram_tensor("bproj", [1, d.C], F32, kind="ExternalInput").ap()
    wr = nc.dram_tensor("wr", [d.E, d.C], F32, kind="ExternalInput").ap()
    aw = nc.dram_tensor("aw", [d.E, d.R, d.C], F32, kind="ExternalInput").ap()
    bw = nc.dram_tensor("bw", [d.E, d.C, d.R], F32, kind="ExternalInput").ap()
    sel = nc.dram_tensor("sel", [d.E, d.ERCH * 128], F32,
                         kind="ExternalInput").ap()
    idn = nc.dram_tensor("idn", [128, 128], BF16, kind="ExternalInput").ap()
    idf = nc.dram_tensor("idf", [128, 128], F32, kind="ExternalInput").ap()
    out = nc.dram_tensor("out", [d.TPC, d.C], F32, kind="ExternalOutput").ap()

    kscr = nc.dram_tensor("kscr", [d.HP, 128, d.L], BF16,
                          kind="Internal").ap()
    vscr = nc.dram_tensor("vscr", [d.HP, d.KT, 128, 128], BF16,
                          kind="Internal").ap()

    import contextlib
    with contextlib.ExitStack() as top:
        T = lambda *a: top.enter_context(nc.sbuf_tensor(*a))
        _qfm_cm = nc.sbuf_tensor("Q_fm", [128, d.HP * d.TPC], BF16,
                                 side="right")
        Q_fm = _qfm_cm.__enter__()
        wp_sb = T("wp_sb", [128, d.CH * d.C], BF16)
        bp_s = T("bp_s", [1, d.C], F32)
        idn_s = T("idn_s", [128, 128], BF16)
        idf_s = T("idf_s", [128, 128], F32)
        ones_bf = T("ones_bf", [128, 1], BF16)
        ones1f = T("ones1f", [1, 128], F32)
        cst = T("cst", [128, 2], F32)
        kn_s = T("kn_s", [1, d.H * d.GB], F32)
        qn_s = T("qn_s", [1, d.H * d.NTC], F32)
        Mb_s = T("Mb_s", [128, d.HP * d.NTC], F32)
        G_s = T("G_s", [128, d.ERCH], F32)
        xln0 = T("xln0", [128, d.CH], BF16)
        q0f = T("q0f", [128, d.HP], BF16)

        ld_wp = P.add("gd", lambda e: e.dma_start(
            out=wp_sb.rearrange("p (c f) -> p c f", c=d.CH),
            in_=wproj.rearrange("(c p) f -> p c f", p=128)), stream="wonce")
        ld_bp = P.add("sd", lambda e: e.dma_start(out=bp_s[:], in_=bproj),
                      stream="sonce")
        ld_id = P.add("sd", lambda e: e.dma_start(out=idn_s[:], in_=idn),
                      stream="sonce")
        ld_if = P.add("sd", lambda e: e.dma_start(out=idf_s[:], in_=idf),
                      stream="sonce")
        o_ones = P.add("g", lambda e: e.memset(ones_bf[:], 1.0))
        o_ones1 = P.add("g", lambda e: e.memset(ones1f[:], 1.0))
        oc1 = P.add("g", lambda e: e.memset(cst[:, 0:1], EPS))
        oc2 = P.add("g", lambda e: e.memset(cst[:, 1:2], 0.0))

        with contextlib.ExitStack() as p1:
            T1 = lambda *a: p1.enter_context(nc.sbuf_tensor(*a))
            wsb = T1("wsb", [128, d.CH * d.F3], BF16)
            x_t = [T1(f"x{i}", [128, d.C], F32) for i in range(3)]
            sc = {"s1": T1("s1", [128, 4], F32),
                  "jnk": T1("jnk", [128, d.C], BF16),
                  "s2": T1("s2", [128, 4], F32),
                  "sqv": T1("sqv", [128, d.C], F32),
                  "s6": T1("s6", [128, 4], F32),
                  "z": cst[0:128, 1:2], "oz": oc2}
            meanb = T1("meanb", [128, d.NTB], F32)
            varb = T1("varb", [128, d.NTB], F32)
            rstdb = T1("rstdb", [128, d.NTB], F32)
            murb = T1("murb", [128, d.NTB], F32)
            mean_o = T1("mean_o", [128, d.NT], F32)
            var_o = T1("var_o", [128, d.NT], F32)
            rstd_o = T1("rstd_o", [128, d.NT], F32)
            mur_o = T1("mur_o", [128, d.NT], F32)
            xlnb = [T1(f"xlnb{i}", [128, d.C], BF16) for i in range(4)]
            xfg = [T1(f"xfg{i}", [128, d.CH * 512], BF16) for i in range(3)]
            ctb = [T1(f"ctb{i}", [128, 512], F32) for i in range(4)]
            stb = [T1(f"stb{i}", [128, 512], F32) for i in range(4)]
            sh_b = [T1(f"sh_b{i}", [128, 512], F32) for i in range(4)]
            t1_b = [T1(f"t1_b{i}", [128, 512], F32) for i in range(4)]
            t2_b = [T1(f"t2_b{i}", [128, 512], F32) for i in range(4)]
            stage = [T1(f"stage{i}", [128, 512], BF16) for i in range(4)]
            qsq_b = [T1(f"qsq_b{i}", [128, 512], BF16) for i in range(4)]
            vstage = [T1(f"vstage{i}", [128, d.FC], BF16) for i in range(4)]

            SG0 = p1.enter_context(nc.psum_tensor("SG0", [128, 1024], F32))
            SG1 = p1.enter_context(nc.psum_tensor("SG1", [128, 1024], F32))
            PO = p1.enter_context(nc.psum_tensor("PO", [128, 512], F32))
            PD = p1.enter_context(nc.psum_tensor("PD", [64, 512], F32))
            PTX = [p1.enter_context(nc.psum_tensor(f"PTX{i}",
                                                   [128, 512], BF16))
                   for i in range(2)]

            ld_w = P.add("gd", lambda e: e.dma_start(
                out=wsb.rearrange("p (c f) -> p c f", c=d.CH),
                in_=wqkv.rearrange("(c p) f -> p c f", p=128)),
                stream="wonce")

            lnb_done = []
            for t in range(d.NTB):
                lx = P.add("sd", lambda e, t=t: e.dma_start(
                    out=x_t[t % 2][:], in_=xbt[t * 128:(t + 1) * 128, :]),
                    deps=[lnb_done[t - 2]] if t >= 2 else [],
                    stream=f"x1{t % 2}")
                lnb_done.append(ln_pass1(P, d, x_t[t % 2][:], meanb, varb,
                                         t, [lx], sc))
            oEb, oMb_ = ln_finish(P, d, meanb, varb, rstdb, murb,
                                  lnb_done, cst, oc2)

            lno_done = []
            for t in range(d.NT):
                lx = P.add("sd", lambda e, t=t: e.dma_start(
                    out=x_t[t % 2][:], in_=xst[t * 128:(t + 1) * 128, :]),
                    deps=[lno_done[t - 2] if t >= 2 else lnb_done[-1]],
                    stream=f"x1{t % 2}")
                lno_done.append(ln_pass1(P, d, x_t[t % 2][:], mean_o, var_o,
                                         t, [lx], sc))
            oEo, oMo = ln_finish(P, d, mean_o, var_o, rstd_o, mur_o,
                                 lno_done, cst, oc2)

            st8 = dict(pslot=[None] * 4, psi=0,
                       t12_reader=[None] * 4, t2_free=[None] * 4,
                       qsq_reader=[None] * 4,
                       normPD=None, normPO=None,
                       stage_readers=[[] for _ in range(4)],
                       vst_free=[[] for _ in range(4)],
                       xfg_users=[[], [], []], norm_pending=[],
                       xln_rd=[lno_done[-1]] * 4,
                       tp_evac=[None, None], tpb=0, si=0, vi=0,
                       gseq=0, ct_user=[None] * 4)

            def take_pslot():
                i = st8["psi"] % 4
                st8["psi"] += 1
                ten = (SG0, SG1)[i // 2]
                return i, ten[:, (i % 2) * 512:(i % 2) * 512 + 512]
            x0ops = []

            def emit_group(src_x, g, rstd_a, mur_a, oE_, oM_, save0):
                last = None
                gp = st8["gseq"] % 3
                users = st8["xfg_users"][gp]
                for tl in range(4):
                    t = g * 4 + tl
                    ti = (g * 4 + tl) % 3
                    lx = P.add("sd", lambda e, t=t, ti=ti, src_x=src_x:
                               e.dma_start(
                        out=x_t[ti][:],
                        in_=src_x[t * 128:(t + 1) * 128, :]),
                        deps=[st8["xln_rd"][ti]], stream=f"x2{ti % 2}")
                    oxl = P.add("v", lambda e, t=t, ti=ti, rstd_a=rstd_a,
                                mur_a=mur_a: e.tensor_scalar(
                        out=xlnb[ti][:], in0=x_t[ti][:],
                        scalar1=rstd_a[:, t:t + 1],
                        scalar2=mur_a[:, t:t + 1],
                        op0=ALU.mult, op1=ALU.subtract),
                        deps=[lx, oE_, oM_])
                    for b4 in range((d.CH + 3) // 4):
                        bank = st8["tpb"] % 2
                        st8["tpb"] += 1
                        otps = []
                        for j4 in range(min(4, d.CH - b4 * 4)):
                            c = b4 * 4 + j4
                            pt = PTX[bank][:, j4 * 128:(j4 + 1) * 128]
                            otp = P.add("t", lambda e, ti=ti, c=c, pt=pt:
                                        e.transpose(
                                pt, xlnb[ti][:, c * 128:(c + 1) * 128],
                                idn_s[:]),
                                deps=[oxl, ld_id,
                                      st8["tp_evac"][bank]] + users)
                            otps.append(otp)
                            st8["xln_rd"][ti] = otp
                        for j4 in range(min(4, d.CH - b4 * 4)):
                            c = b4 * 4 + j4
                            pt = PTX[bank][:, j4 * 128:(j4 + 1) * 128]
                            oev = P.add("s", lambda e, tl=tl, c=c, pt=pt,
                                        gp=gp: e.copy(
                                xfg[gp][:, c * 512 + tl * 128:
                                        c * 512 + (tl + 1) * 128], pt),
                                deps=[otps[-1]])
                            last = oev
                        st8["tp_evac"][bank] = last
                        if save0 and tl == 0:
                            for c in range(b4 * 4,
                                           min(d.CH, b4 * 4 + 4)):
                                osv = P.add("v", lambda e, c=c:
                                            e.tensor_copy(
                                    xln0[:, c:c + 1],
                                    xfg[0][:, c * 512:c * 512 + 1]),
                                    deps=[last])
                                x0ops.append(osv)
                                last = osv
                            st8["tp_evac"][bank] = last
                st8["xfg_users"][gp] = []
                st8["gseq"] += 1
                return last, gp

            def rope_block(ps, mm, ct, st_, outsink, nbuf, ncol):
                si = st8["si"]
                pz = si % 4
                osh = P.add("v", lambda e, ps=ps, pz=pz: e.stream_shuffle(
                    sh_b[pz][:], ps, [i ^ 1 for i in range(32)]),
                    deps=[mm, st8["t2_free"][pz]])
                ot1 = P.add("v", lambda e, ps=ps, ct=ct, pz=pz:
                            e.tensor_tensor(
                    out=t1_b[pz][:], in0=ps, in1=ct, op=ALU.mult),
                    deps=[mm, st8["t12_reader"][pz]])
                ot2 = P.add("g", lambda e, st_=st_, pz=pz: e.tensor_tensor(
                    out=t2_b[pz][:], in0=sh_b[pz][:], in1=st_, op=ALU.mult),
                    deps=[osh, st8["t12_reader"][pz]])
                st8["t2_free"][pz] = ot2
                ost = P.add("v", lambda e, pz=pz: e.tensor_tensor(
                    out=stage[pz][:], in0=t1_b[pz][:], in1=t2_b[pz][:],
                    op=ALU.add),
                    deps=[ot1, ot2] + st8["stage_readers"][pz])
                st8["t12_reader"][pz] = ost
                osink = outsink(stage[pz][:], [ost])
                osq = P.add("v", lambda e, pz=pz: e.tensor_tensor(
                    out=qsq_b[pz][:], in0=stage[pz][:],
                    in1=stage[pz][:], op=ALU.mult),
                    deps=[ost, st8["qsq_reader"][pz]])
                st8["stage_readers"][pz] = [osq, osink]
                st8["norm_pending"].append((pz, osq, nbuf, ncol))
                flush_norm(3)
                st8["si"] = si + 1
                return osink

            def flush_norm(n_keep):
                while len(st8["norm_pending"]) > n_keep:
                    pz, osq, nbuf, ncol = st8["norm_pending"].pop(0)
                    on1 = P.add("t", lambda e, pz=pz: e.matmul(
                        PD[0:1, :], ones_bf[0:64, :], qsq_b[pz][0:64, :],
                        start=True, stop=True, tile_position=(0, 0)),
                        deps=[osq, o_ones, st8["normPD"]])
                    on2 = P.add("t", lambda e, pz=pz: e.matmul(
                        PO[0:1, :], ones_bf[64:128, :], qsq_b[pz][64:128, :],
                        start=True, stop=True, tile_position=(64, 0)),
                        deps=[osq, o_ones, st8["normPO"]])
                    st8["qsq_reader"][pz] = on2
                    or1 = P.add("v", lambda e, nbuf=nbuf, n0=ncol[0]:
                                e.reduce_max(nbuf[0:1, n0:n0 + 1],
                                             PD[0:1, :], axis=AX.X),
                                deps=[on1])
                    or2 = P.add("v", lambda e, nbuf=nbuf, n1=ncol[1]:
                                e.reduce_max(nbuf[0:1, n1:n1 + 1],
                                             PO[0:1, :], axis=AX.X),
                                deps=[on2])
                    st8["normPD"] = or1
                    st8["normPO"] = or2

            ko_last = vo_last = None
            vo_lasts = [None, None]
            ko_lasts = [None, None]
            for g in range(d.GB):
                ogrp, gp = emit_group(xbt, g, rstdb, murb, oEb, oMb_, g == 0)
                lct = P.add("sd", lambda e, g=g: e.dma_start(
                    out=ctb[g % 4][:], in_=cosK[:, g * 512:(g + 1) * 512]),
                    deps=[st8["ct_user"][g % 4]], stream="ct")
                lst = P.add("sd", lambda e, g=g: e.dma_start(
                    out=stb[g % 4][:], in_=sinK[:, g * 512:(g + 1) * 512]),
                    deps=[st8["ct_user"][g % 4]], stream="st")
                for hp in range(d.HP):
                    slot, ps = take_pslot()
                    mm = None
                    for c in range(d.CH):
                        mm = P.add("t", lambda e, c=c, hp=hp, g=g, ps=ps,
                                   gp=gp: e.matmul(
                            ps, wsb[:, c * d.F3 + d.C + hp * 128:
                                    c * d.F3 + d.C + (hp + 1) * 128],
                            xfg[gp][:, c * 512:(c + 1) * 512],
                            start=(c == 0), stop=(c == d.CH - 1)),
                            deps=[ogrp, ld_w, st8["pslot"][slot], lct, lst])
                    st8["xfg_users"][gp].append(mm)

                    def ksink(stage_ap, dps, hp=hp, g=g,
                              kp=st8["si"] % 2):
                        return P.add("gd", lambda e: e.dma_start(
                            out=kscr[hp, :, g * 512:(g + 1) * 512],
                            in_=stage_ap), deps=dps, stream=f"ko{kp}")
                    ncol = ((hp * 2 + 0) * d.GB + g, (hp * 2 + 1) * d.GB + g)
                    osink = rope_block(ps, mm, ctb[g % 4][:], stb[g % 4][:],
                                       ksink, kn_s, ncol)
                    ko_lasts[(st8["si"] - 1) % 2] = osink
                    ko_last = osink
                    st8["pslot"][slot] = st8["t12_reader"][(st8["si"] - 1) % 4]
                    st8["ct_user"][g % 4] = st8["t12_reader"][(st8["si"] - 1) % 4]
                for tl in range(4):
                    vsl = [take_pslot() for _ in range(d.NFC)]
                    mmvs = [None] * d.NFC
                    for c in range(d.CH):
                        for fc in range(d.NFC):
                            vslot, psv0 = vsl[fc]
                            psv = psv0[:, 0:d.FC]
                            mmvs[fc] = P.add("t", lambda e, c=c, tl=tl,
                                             fc=fc, g=g, psv=psv, gp=gp:
                                             e.matmul(
                                psv,
                                xfg[gp][:, c * 512 + tl * 128:
                                        c * 512 + (tl + 1) * 128],
                                wsb[:, c * d.F3 + 2 * d.C + fc * d.FC:
                                    c * d.F3 + 2 * d.C + (fc + 1) * d.FC],
                                start=(c == 0), stop=(c == d.CH - 1)),
                                deps=[ogrp, ld_w,
                                      st8["pslot"][vsl[fc][0]]]
                                if c == 0 else [ogrp, ld_w])
                    for fc in range(d.NFC):
                        vslot, psv0 = vsl[fc]
                        psv = psv0[:, 0:d.FC]
                        mmv = mmvs[fc]
                        st8["xfg_users"][gp].append(mmv)
                        vi = st8["vi"]
                        oev = P.add("s", lambda e, psv=psv, vi=vi:
                                    e.copy(vstage[vi % 4][:], psv),
                                    deps=[mmv] + st8["vst_free"][vi % 4])
                        st8["pslot"][vslot] = oev
                        hp0 = (fc * d.FC) // 128
                        nh = d.FC // 128
                        kt = g * 4 + tl
                        od = P.add("gd", lambda e, hp0=hp0, nh=nh, kt=kt,
                                   vi=vi: e.dma_start(
                            out=vscr[hp0:hp0 + nh, kt, :, :].rearrange(
                                "h p dd -> p h dd"),
                            in_=vstage[vi % 4].rearrange(
                                "p (h dd) -> p h dd", h=nh)),
                            deps=[oev], stream=f"vo{vi % 2}")
                        st8["vst_free"][vi % 4] = [od]
                        vo_lasts[vi % 2] = od
                        vo_last = od
                        st8["vi"] += 1

            q_done = []
            for qg in range(d.NTC):
                ogrp, gp = emit_group(xst, qg, rstd_o, mur_o, oEo, oMo, False)
                gpar = (d.GB + qg) % 4
                lct = P.add("sd", lambda e, qg=qg, gpar=gpar: e.dma_start(
                    out=ctb[gpar][:], in_=cosQ[:, qg * 512:(qg + 1) * 512]),
                    deps=[st8["ct_user"][gpar]], stream="ct")
                lst = P.add("sd", lambda e, qg=qg, gpar=gpar: e.dma_start(
                    out=stb[gpar][:], in_=sinQ[:, qg * 512:(qg + 1) * 512]),
                    deps=[st8["ct_user"][gpar]], stream="st")
                for hp in range(d.HP):
                    slot, ps = take_pslot()
                    mm = None
                    for c in range(d.CH):
                        mm = P.add("t", lambda e, c=c, hp=hp, gpar=gpar,
                                   ps=ps, gp=gp: e.matmul(
                            ps, wsb[:, c * d.F3 + hp * 128:
                                    c * d.F3 + (hp + 1) * 128],
                            xfg[gp][:, c * 512:(c + 1) * 512],
                            start=(c == 0), stop=(c == d.CH - 1)),
                            deps=[ogrp, ld_w, st8["pslot"][slot], lct, lst])
                    st8["xfg_users"][gp].append(mm)

                    def qsink(stage_ap, dps, hp=hp, qg=qg):
                        return P.add("s", lambda e: e.copy(
                            Q_fm[:, hp * d.TPC + qg * 512:
                                 hp * d.TPC + (qg + 1) * 512], stage_ap),
                            deps=dps)
                    ncol = ((2 * hp) * d.NTC + qg, (2 * hp + 1) * d.NTC + qg)
                    osink = rope_block(ps, mm, ctb[gpar][:], stb[gpar][:],
                                       qsink, qn_s, ncol)
                    q_done.append(osink)
                    st8["pslot"][slot] = st8["t12_reader"][(st8["si"] - 1) % 4]
                    st8["ct_user"][gpar] = st8["t12_reader"][(st8["si"] - 1) % 4]

            flush_norm(0)
            oq0m = None
            for hp in range(d.HP):
                for c in range(d.CH):
                    oq0m = P.add("t", lambda e, hp=hp, c=c: e.matmul(
                        SG1[:, hp:hp + 1],
                        wsb[:, c * d.F3 + hp * 128:c * d.F3 + (hp + 1) * 128],
                        xln0[:, c:c + 1],
                        start=(c == 0), stop=(c == d.CH - 1)),
                        deps=[x0ops[-1], ld_w] + list(st8["pslot"]))
            oq0c = P.add("v", lambda e: e.tensor_copy(
                q0f[:], SG1[:, 0:d.HP]), deps=[oq0m])
            p1_t_tail = oq0m
            p1_v_tail = oq0c
            or_last = st8["normPO"]

        _ofm_cm = nc.sbuf_tensor("ofm", [128, d.HP * d.TPC], BF16)
        ofm = _ofm_cm.__enter__()
        with contextlib.ExitStack() as p2:
            T2 = lambda *a: p2.enter_context(nc.sbuf_tensor(*a))
            kbuf = [T2(f"kbuf{i}", [128, d.L], BF16) for i in range(2)]
            vbuf = T2("vbuf", [128, d.KT * 128], BF16)
            PTb = [T2(f"PTb{i}", [128, 1024], BF16) for i in range(4)]
            rd_s = T2("rd_s", [1, 1024], F32)
            rb_s = T2("rb_s", [128, 1024], F32)
            kn8 = T2("kn8", [1, d.HP], F32)
            qn32 = T2("qn32", [1, d.HP * d.NTC], F32)
            t32 = T2("t32", [1, d.HP * d.NTC], F32)

            ASG = [p2.enter_context(nc.psum_tensor(f"ASG{i}", [128, 1024],
                                                   F32)) for i in range(3)]
            PA = p2.enter_context(nc.psum_tensor("PA", [128, 512], F32))
            PB = p2.enter_context(nc.psum_tensor("PB", [128, 512], F32))

            okn = P.add("v", lambda e: e.reduce_max(
                kn8.rearrange("p (a b) -> p a b", b=1),
                kn_s.rearrange("p (a b) -> p a b", b=2 * d.GB),
                axis=AX.X), deps=[or_last, p1_t_tail])
            oqn = None
            for hp in range(d.HP):
                for qc in range(d.NTC):
                    oqn = P.add("v", lambda e, hp=hp, qc=qc: e.tensor_tensor(
                        out=qn32[0:1, hp * d.NTC + qc:hp * d.NTC + qc + 1],
                        in0=qn_s[0:1, (2 * hp) * d.NTC + qc:
                                 (2 * hp) * d.NTC + qc + 1],
                        in1=qn_s[0:1, (2 * hp + 1) * d.NTC + qc:
                                 (2 * hp + 1) * d.NTC + qc + 1],
                        op=ALU.max), deps=[or_last, oqn])
            ot32 = None
            for hp in range(d.HP):
                ot32 = P.add("v", lambda e, hp=hp: e.tensor_scalar(
                    out=t32[0:1, hp * d.NTC:(hp + 1) * d.NTC],
                    in0=qn32[0:1, hp * d.NTC:(hp + 1) * d.NTC],
                    scalar1=kn8[0:1, hp:hp + 1], scalar2=None,
                    op0=ALU.mult), deps=[okn, oqn, ot32])
            oln = P.add("s", lambda e: e.activation(
                out=t32[:], in_=t32[:], func=AF.Ln, bias=cst[0:1, 1:2],
                scale=1.0), deps=[ot32, oc2])
            oex = P.add("s", lambda e: e.activation(
                out=t32[:], in_=t32[:], func=AF.Exp, bias=cst[0:1, 1:2],
                scale=0.5), deps=[oln])
            omu = P.add("s", lambda e: e.mul(t32[:], t32[:], -LAM),
                        deps=[oex])
            obc = P.add("t", lambda e: e.matmul(
                ASG[0][:, 0:d.HP * d.NTC], ones1f[:], t32[:],
                start=True, stop=True),
                deps=[omu, o_ones1, p1_t_tail, p1_v_tail])
            oMb = P.add("v", lambda e: e.tensor_copy(
                Mb_s[:], ASG[0][:, 0:d.HP * d.NTC]), deps=[obc])

            exp_ops = {}
            od_last = {}
            vb_loads = {}
            bc_par = [0]
            ev_mults = None
            ev_recipB = None
            bc_copies = oMb
            att_last_t = None
            pend_ev = [None]

            def flush_evac():
                nonlocal ev_mults, bc_copies, att_last_t
                if pend_ev[0] is None:
                    return
                p, odl, expl, orA, orB, col = pend_ev[0]
                pend_ev[0] = None
                obA = P.add("t", lambda e, p=p: e.matmul(
                    ASG[p][:, 0:512], ones1f[:], rd_s[0:1, 0:512],
                    start=True, stop=True),
                    deps=[orA, expl, o_ones1])
                obB = P.add("t", lambda e, p=p: e.matmul(
                    ASG[p][:, 512:1024], ones1f[:],
                    rd_s[0:1, 512:1024],
                    start=True, stop=True),
                    deps=[orB, expl])
                ocA = P.add("v", lambda e, p=p: e.tensor_copy(
                    rb_s[:, 0:512], ASG[p][:, 0:512]),
                    deps=[obA, ev_mults])
                ocB = P.add("v", lambda e, p=p: e.tensor_copy(
                    rb_s[:, 512:1024], ASG[p][:, 512:1024]),
                    deps=[obB, ev_mults])
                bc_par[0] = p
                bc_copies = ocB
                omA = P.add("v", lambda e, col=col: e.tensor_tensor(
                    out=ofm[0:64, col:col + 512], in0=PA[0:64, :],
                    in1=rb_s[0:64, 0:512], op=ALU.mult),
                    deps=[ocA, odl])
                omB = P.add("v", lambda e, col=col: e.tensor_tensor(
                    out=ofm[64:128, col:col + 512], in0=PB[64:128, :],
                    in1=rb_s[64:128, 512:1024], op=ALU.mult),
                    deps=[ocB])
                ev_mults = omB
                att_last_t = obB
            last_S = obc
            last_O = None
            it = 0
            prev_s = [obc, obc]
            for hp in range(d.HP):
                kload = P.add("sd", lambda e, hp=hp: e.dma_start(
                    out=kbuf[hp % 2][:], in_=kscr[hp]),
                    deps=[prev_s[hp % 2], ko_lasts[0], ko_lasts[1],
                          p1_v_tail],
                    stream="kb")
                vls = []
                for j8 in range(max(1, d.KT // 8)):
                    n8 = min(8, d.KT - j8 * 8)
                    vls.append(P.add("sd", lambda e, hp=hp, j8=j8, n8=n8:
                                     e.dma_start(
                        out=vbuf[:, j8 * 1024:j8 * 1024 + n8 * 128]
                        .rearrange("p (k dd) -> p k dd", k=n8),
                        in_=vscr[hp, j8 * 8:j8 * 8 + n8].rearrange(
                            "k p dd -> p k dd")),
                        deps=[last_O, vo_lasts[0], vo_lasts[1], obc,
                              p1_v_tail], stream="vb"))
                vb_loads[hp] = vls[-1]
                def emit_od(jt, hp2, qc2):
                    kt2 = jt % d.KT
                    oex2 = exp_ops[jt]
                    first, last = kt2 == 0, kt2 == d.KT - 1
                    ops = []
                    if first:
                        ops = ["oA", "oB", "dA", "dB"]
                    else:
                        ops = ["dA", "dB", "oA", "oB"]
                    res = {}
                    for nm in ops:
                        if nm == "oA":
                            res[nm] = P.add("t", lambda e, jt=jt, kt2=kt2:
                                            e.matmul(
                                PA[0:64, :],
                                vbuf[:, kt2 * 128:kt2 * 128 + 64],
                                PTb[jt % 4][:, 0:512],
                                start=first, stop=last,
                                tile_position=(0, 0)),
                                deps=[oex2, vb_loads[hp2],
                                      ev_mults if first else None])
                        elif nm == "oB":
                            res[nm] = P.add("t", lambda e, jt=jt, kt2=kt2:
                                            e.matmul(
                                PB[64:128, :],
                                vbuf[:, kt2 * 128 + 64:(kt2 + 1) * 128],
                                PTb[jt % 4][:, 512:1024],
                                start=first, stop=last,
                                tile_position=(0, 64)),
                                deps=[oex2, vb_loads[hp2],
                                      ev_mults if first else None])
                        elif nm == "dA":
                            res[nm] = P.add("t", lambda e, jt=jt, kt2=kt2:
                                            e.matmul(
                                PA[64:65, :], ones_bf[:],
                                PTb[jt % 4][:, 0:512],
                                start=first, stop=False,
                                tile_position=(0, 64),
                                skip_group_check=True),
                                deps=[oex2, o_ones,
                                      ev_recipB if first else None])
                        else:
                            res[nm] = P.add("t", lambda e, jt=jt, kt2=kt2:
                                            e.matmul(
                                PB[0:1, :], ones_bf[:],
                                PTb[jt % 4][:, 512:1024],
                                start=first, stop=False,
                                tile_position=(0, 0),
                                skip_group_check=True),
                                deps=[oex2, o_ones,
                                      ev_recipB if first else None])
                    od_last[jt] = res[ops[-1]]
                    return res["oB"] if not first else res["dB"]

                for qc in range(d.NTC):
                    for kt in range(d.KT):
                        sg = ASG[it % 3]
                        extra = bc_copies if (
                            kt < 3 and (it % 3) == bc_par[0]) or it == 0 \
                            else None
                        sA = P.add("t", lambda e, hp=hp, qc=qc, kt=kt, sg=sg:
                                   e.matmul(
                            sg[:, 0:512],
                            kbuf[hp % 2][0:64, kt * 128:(kt + 1) * 128],
                            Q_fm[0:64, hp * d.TPC + qc * 512:
                                 hp * d.TPC + (qc + 1) * 512],
                            start=True, stop=True, tile_position=(0, 0)),
                            deps=[kload, exp_ops.get(it - 3), extra]
                            + ([q_done[-1], p1_v_tail] if it == 0 else []))
                        sB = P.add("t", lambda e, hp=hp, qc=qc, kt=kt, sg=sg:
                                   e.matmul(
                            sg[:, 512:1024],
                            kbuf[hp % 2][64:128, kt * 128:(kt + 1) * 128],
                            Q_fm[64:128, hp * d.TPC + qc * 512:
                                 hp * d.TPC + (qc + 1) * 512],
                            start=True, stop=True, tile_position=(64, 0)),
                            deps=[kload, exp_ops.get(it - 3), extra])
                        last_S = sB
                        midx = hp * d.NTC + qc
                        oex2 = P.add("s", lambda e, it=it, midx=midx, sg=sg:
                                     e.activation(
                            out=PTb[it % 4][:], in_=sg[:], func=AF.Exp,
                            bias=Mb_s[:, midx:midx + 1], scale=LAM),
                            deps=[sB, oMb, od_last.get(it - 4)]
                            + ([ko_lasts[0], ko_lasts[1], vo_lasts[0],
                                vo_lasts[1]] if it == 0 else []))
                        exp_ops[it] = oex2
                        if kt == 0:
                            flush_evac()
                        if kt > 2:
                            last_O = emit_od(it - 3, hp, qc)
                        it += 1
                    last_O = emit_od(it - 3, hp, qc)
                    last_O = emit_od(it - 2, hp, qc)
                    last_O = emit_od(it - 1, hp, qc)
                    p = (it - 1) % 3
                    orA = P.add("v", lambda e: e.reciprocal(
                        rd_s[0:1, 0:512], PA[64:65, :]),
                        deps=[od_last[it - 1], bc_copies])
                    orB = P.add("v", lambda e: e.reciprocal(
                        rd_s[0:1, 512:1024], PB[0:1, :]),
                        deps=[od_last[it - 1]])
                    ev_recipB = orB
                    pend_ev[0] = (p, od_last[it - 1], exp_ops[it - 1],
                                  orA, orB, hp * d.TPC + qc * 512)
                prev_s[hp % 2] = last_S
            flush_evac()
            att_t = att_last_t
            att_v = ev_mults
            att_s = exp_ops[it - 1]
        _qfm_cm.__exit__(None, None, None)

        with contextlib.ExitStack() as p3:
            T3 = lambda *a: p3.enter_context(nc.sbuf_tensor(*a))
            q0bd = T3("q0bd", [128, d.CH * d.H], BF16)
            kcb = [T3(f"kcb{i}", [128, d.CH * 512], BF16)
                   for i in range(2)]
            s0_s = T3("s0_s", [d.H, d.L], F32)
            p0_s = T3("p0_s", [d.H, d.L], BF16)
            p0T = T3("p0T", [128, d.KT * d.H], BF16)
            vb3 = [T3(f"vb3{i}", [128, d.KT * 128], BF16)
                   for i in range(2)]
            o0f = T3("o0f", [128, d.CH], BF16)
            m0 = T3("m0", [d.H, 4], F32)
            xp0_s = T3("xp0_s", [1, d.C], F32)
            g_scr = T3("g_scr", [1, d.C], F32)
            g_sc2 = T3("g_sc2", [1, 64], F32)
            cls_s = T3("cls_s", [128, d.CH], F32)
            wrT_s = T3("wrT_s", [128, d.CH * d.E], F32)
            gfm_s = T3("gfm_s", [d.E, 1], F32)
            wr_s = T3("wr_s", [d.E, d.C], F32)
            sel_s = T3("sel_s", [d.E, d.ERCH * 128], F32)

            CP = [p3.enter_context(nc.psum_tensor(f"CP{i}", [128, 512], F32))
                  for i in range(2)]
            CO = [p3.enter_context(nc.psum_tensor(f"CO{i}", [128, 16], F32))
                  for i in range(2)]
            CT = [p3.enter_context(nc.psum_tensor(f"CT{i}", [128, 16], BF16))
                  for i in range(2)]

            ld_wr = P.add("sd", lambda e: e.dma_start(out=wr_s[:], in_=wr),
                          deps=[att_t, att_v, att_s], stream="sonce")
            ld_sel = P.add("sd", lambda e: e.dma_start(
                out=sel_s[:], in_=sel), deps=[att_t, att_v, att_s],
                stream="sonce")
            lx0 = P.add("sd", lambda e: e.dma_start(
                out=xp0_s[:], in_=xbt[0:1, :]),
                deps=[att_t, att_v, att_s], stream="x0")
            o0prev = [None, None]
            lv3s = [None, None]
            for hp in range(2):
                for j8 in range(max(1, d.KT // 8)):
                    n8 = min(8, d.KT - j8 * 8)
                    lv3s[hp % 2] = P.add(
                        "sd", lambda e, hp=hp, j8=j8, n8=n8:
                        e.dma_start(
                            out=vb3[hp % 2][:, j8 * 1024:
                                            j8 * 1024 + n8 * 128]
                            .rearrange("p (k dd) -> p k dd", k=n8),
                            in_=vscr[hp, j8 * 8:j8 * 8 + n8].rearrange(
                                "k p dd -> p k dd")),
                        deps=[vo_lasts[0], vo_lasts[1],
                              att_t, att_v, att_s],
                        stream=f"vs{j8 % 4}")

            oz = P.add("g", lambda e: e.memset(q0bd[:], 0.0),
                       deps=[att_t, att_v, att_s])
            oqb = None
            for c in range(d.CH):
                oqb = P.add("v", lambda e, c=c: e.tensor_copy(
                    q0bd[0:64, c * d.H + 2 * c:c * d.H + 2 * c + 1],
                    q0f[0:64, c:c + 1]), deps=[oz, p1_v_tail, oqb])
                oqb = P.add("v", lambda e, c=c: e.tensor_copy(
                    q0bd[64:128, c * d.H + 2 * c + 1:c * d.H + 2 * c + 2],
                    q0f[64:128, c:c + 1]), deps=[oqb])
            os0 = []
            for ktc in range(d.GB):
                lk = P.add("sd", lambda e, ktc=ktc: e.dma_start(
                    out=kcb[ktc % 2].rearrange("p (h t) -> p h t", h=d.CH),
                    in_=kscr[:, :, ktc * 512:(ktc + 1) * 512].rearrange(
                        "h p t -> p h t")),
                    deps=[ko_lasts[0], ko_lasts[1], att_t, att_v, att_s]
                    + (os0[-2:-1] or []), stream=f"kc{ktc % 2}")
                mm = None
                for c in range(d.CH):
                    mm = P.add("t", lambda e, c=c, ktc=ktc: e.matmul(
                        CP[ktc % 2][0:d.H, :],
                        q0bd[:, c * d.H:(c + 1) * d.H],
                        kcb[ktc % 2][:, c * 512:(c + 1) * 512],
                        start=(c == 0), stop=(c == d.CH - 1)),
                        deps=[oqb, lk, att_v, os0[-2] if len(os0) >= 2
                              else None])
                oc_ = P.add("v", lambda e, ktc=ktc: e.tensor_copy(
                    s0_s[:, ktc * 512:(ktc + 1) * 512], CP[ktc % 2][0:d.H, :]),
                    deps=[mm, att_v])
                os0.append(oc_)
            om0 = P.add("v", lambda e: e.reduce_max(
                m0[:, 0:1], s0_s[:], axis=AX.X), deps=[os0[-1]])
            om0n = P.add("v", lambda e: e.tensor_scalar(
                out=m0[:, 1:2], in0=m0[:, 0:1], scalar1=-LAM, scalar2=None,
                op0=ALU.mult), deps=[om0])
            oe0 = P.add("s", lambda e: e.activation(
                out=p0_s[:], in_=s0_s[:], func=AF.Exp, bias=m0[:, 1:2],
                scale=LAM, accum_out=m0[:, 2:3]), deps=[om0n, os0[-1]])
            or0 = P.add("v", lambda e: e.reciprocal(m0[:, 3:4], m0[:, 2:3]),
                        deps=[oe0])
            op0n = P.add("v", lambda e: e.tensor_scalar(
                out=p0_s[:], in0=p0_s[:], scalar1=m0[:, 3:4], scalar2=None,
                op0=ALU.mult), deps=[or0, oe0])
            otp_prev = [None, None]
            op0T = None
            for kt in range(d.KT):
                ct_ = CT[kt % 2]
                otp = P.add("t", lambda e, kt=kt, ct_=ct_: e.transpose(
                    ct_[:, 0:d.H],
                    p0_s[:, kt * 128:(kt + 1) * 128], idn_s[0:d.H, 0:d.H]),
                    deps=[op0n, ld_id, otp_prev[kt % 2]])
                oev = P.add("v", lambda e, kt=kt, ct_=ct_: e.tensor_copy(
                    p0T[:, kt * d.H:(kt + 1) * d.H], ct_[:, 0:d.H]),
                    deps=[otp])
                otp_prev[kt % 2] = oev
                op0T = oev
            o0evs = []
            for hp in range(d.HP):
                if hp >= 2:
                    for j8 in range(max(1, d.KT // 8)):
                        n8 = min(8, d.KT - j8 * 8)
                        lv3s[hp % 2] = P.add(
                            "sd", lambda e, hp=hp, j8=j8, n8=n8:
                            e.dma_start(
                                out=vb3[hp % 2][:, j8 * 1024:
                                                j8 * 1024 + n8 * 128]
                                .rearrange("p (k dd) -> p k dd", k=n8),
                                in_=vscr[hp, j8 * 8:j8 * 8 + n8].rearrange(
                                    "k p dd -> p k dd")),
                            deps=[o0prev[hp % 2], vo_lasts[0], vo_lasts[1]],
                            stream=f"vs{j8 % 4}")
                co = CO[hp % 2]
                mm = None
                for kt in range(d.KT):
                    mm = P.add("t", lambda e, kt=kt, co=co, hp=hp: e.matmul(
                        co[:, 0:d.H], vb3[hp % 2][:, kt * 128:(kt + 1) * 128],
                        p0T[:, kt * d.H:(kt + 1) * d.H],
                        start=(kt == 0), stop=(kt == d.KT - 1)),
                        deps=[lv3s[hp % 2], op0T,
                              o0evs[-2] if len(o0evs) >= 2 else None])
                oe1 = P.add("v", lambda e, hp=hp, co=co: e.tensor_copy(
                    o0f[0:64, hp:hp + 1], co[0:64, 2 * hp:2 * hp + 1]),
                    deps=[mm])
                oe2 = P.add("v", lambda e, hp=hp, co=co: e.tensor_copy(
                    o0f[64:128, hp:hp + 1],
                    co[64:128, 2 * hp + 1:2 * hp + 2]), deps=[oe1])
                o0evs.append(oe2)
                o0prev[hp % 2] = mm
            oxp0 = []
            for fc in range(d.NFC):
                mm = None
                for c in range(d.CH):
                    mm = P.add("t", lambda e, c=c, fc=fc: e.matmul(
                        CP[0][0:1, 0:d.FC],
                        o0f[:, c:c + 1],
                        wp_sb[:, c * d.C + fc * d.FC:c * d.C + (fc + 1) * d.FC],
                        start=(c == 0), stop=False),
                        deps=[o0evs[-1], ld_wp,
                              oxp0[-1] if oxp0 else os0[-1]])
                mb = P.add("t", lambda e, fc=fc: e.matmul(
                    CP[0][0:1, 0:d.FC], ones1f[0:1, 0:1],
                    bp_s[0:1, fc * d.FC:(fc + 1) * d.FC],
                    start=False, stop=True), deps=[ld_bp, o_ones1])
                oadd = P.add("v", lambda e, fc=fc: e.tensor_tensor(
                    out=xp0_s[0:1, fc * d.FC:(fc + 1) * d.FC],
                    in0=CP[0][0:1, 0:d.FC],
                    in1=xp0_s[0:1, fc * d.FC:(fc + 1) * d.FC],
                    op=ALU.add), deps=[mb, lx0])
                oxp0.append(oadd)
            og1 = P.add("v", lambda e: e.reduce_sum(
                g_sc2[0:1, 0:1], xp0_s[:], axis=AX.X), deps=[oxp0[-1]])
            og2 = P.add("v", lambda e: e.tensor_scalar(
                out=g_sc2[0:1, 1:2], in0=g_sc2[0:1, 0:1],
                scalar1=1.0 / d.C, scalar2=None, op0=ALU.mult), deps=[og1])
            og3 = P.add("v", lambda e: e.tensor_scalar(
                out=g_scr[:], in0=xp0_s[:], scalar1=g_sc2[0:1, 1:2],
                scalar2=None, op0=ALU.subtract), deps=[og2])
            og4 = P.add("v", lambda e: e.tensor_tensor(
                out=g_scr[:], in0=g_scr[:], in1=g_scr[:], op=ALU.mult),
                deps=[og3])
            og5 = P.add("v", lambda e: e.reduce_sum(
                g_sc2[0:1, 2:3], g_scr[:], axis=AX.X), deps=[og4])
            og6 = P.add("v", lambda e: e.tensor_scalar(
                out=g_sc2[0:1, 3:4], in0=g_sc2[0:1, 2:3],
                scalar1=1.0 / d.C, scalar2=None, op0=ALU.mult), deps=[og5])
            og7 = P.add("s", lambda e: e.activation(
                out=g_sc2[0:1, 4:5], in_=g_sc2[0:1, 3:4], func=AF.Ln,
                bias=cst[0:1, 0:1], scale=1.0), deps=[og6, oc1])
            og8 = P.add("s", lambda e: e.activation(
                out=g_sc2[0:1, 5:6], in_=g_sc2[0:1, 4:5], func=AF.Exp,
                bias=cst[0:1, 1:2], scale=-0.5), deps=[og7])
            og9 = P.add("v", lambda e: e.tensor_scalar(
                out=g_scr[:], in0=xp0_s[:], scalar1=g_sc2[0:1, 1:2],
                scalar2=g_sc2[0:1, 5:6], op0=ALU.subtract, op1=ALU.mult),
                deps=[og8, og5])
            ocs = []
            for c in range(d.CH):
                ocs.append(P.add("t", lambda e, c=c: e.transpose(
                    CP[1][:, c:c + 1], g_scr[0:1, c * 128:(c + 1) * 128],
                    idf_s[0:1, 0:1]), deps=[og9, ld_if, oxp0[-1]]))
            occ = P.add("v", lambda e: e.tensor_copy(
                cls_s[:], CP[1][:, 0:d.CH]), deps=ocs)
            ows = []
            for c in range(d.CH):
                ows.append(P.add("t", lambda e, c=c: e.transpose(
                    CP[1][:, 16 + c * d.E:16 + (c + 1) * d.E],
                    wr_s[:, c * 128:(c + 1) * 128], idf_s[0:d.E, 0:d.E]),
                    deps=[ld_wr, ld_if, occ if c == 0 else None]))
            owc = P.add("v", lambda e: e.tensor_copy(
                wrT_s[:], CP[1][:, 16:16 + d.CH * d.E]), deps=ows)
            olg = None
            for c in range(d.CH):
                olg = P.add("t", lambda e, c=c: e.matmul(
                    CP[0][0:1, 16:16 + d.E], cls_s[:, c:c + 1],
                    wrT_s[:, c * d.E:(c + 1) * d.E],
                    start=(c == 0), stop=(c == d.CH - 1)),
                    deps=[occ, owc, oxp0[-1]])
            olgc = P.add("v", lambda e: e.tensor_copy(
                g_sc2[0:1, 8:8 + d.E], CP[0][0:1, 16:16 + d.E]),
                deps=[olg])
            om8 = P.add("v", lambda e: e.max(
                g_sc2[0:1, 16:24], g_sc2[0:1, 8:8 + d.E]), deps=[olgc])
            okp = P.add("v", lambda e: e.tensor_scalar(
                out=g_sc2[0:1, 24:24 + d.E], in0=g_sc2[0:1, 8:8 + d.E],
                scalar1=g_sc2[0:1, 17:18], scalar2=None, op0=ALU.is_ge),
                deps=[om8])
            om1n = P.add("v", lambda e: e.tensor_scalar(
                out=g_sc2[0:1, 32:33], in0=g_sc2[0:1, 16:17], scalar1=-1.0,
                scalar2=None, op0=ALU.mult), deps=[om8])
            oen = P.add("s", lambda e: e.activation(
                out=g_sc2[0:1, 40:40 + d.E], in_=g_sc2[0:1, 8:8 + d.E],
                func=AF.Exp, bias=g_sc2[0:1, 32:33], scale=1.0),
                deps=[om1n, okp])
            ogu = P.add("v", lambda e: e.tensor_tensor(
                out=g_sc2[0:1, 48:48 + d.E], in0=g_sc2[0:1, 24:24 + d.E],
                in1=g_sc2[0:1, 40:40 + d.E], op=ALU.mult), deps=[oen])
            odn = P.add("v", lambda e: e.reduce_sum(
                g_sc2[0:1, 56:57], g_sc2[0:1, 48:48 + d.E], axis=AX.X),
                deps=[ogu])
            ord_ = P.add("v", lambda e: e.reciprocal(
                g_sc2[0:1, 57:58], g_sc2[0:1, 56:57]), deps=[odn])
            ogt = P.add("v", lambda e: e.tensor_scalar(
                out=g_sc2[0:1, 48:48 + d.E], in0=g_sc2[0:1, 48:48 + d.E],
                scalar1=g_sc2[0:1, 57:58], scalar2=None, op0=ALU.mult),
                deps=[ord_])
            ogf = P.add("t", lambda e: e.transpose(
                CP[0][0:d.E, 32:33], g_sc2[0:1, 48:48 + d.E],
                idf_s[0:1, 0:1]), deps=[ogt, olg])
            ogfc = P.add("v", lambda e: e.tensor_copy(
                gfm_s[:], CP[0][0:d.E, 32:33]), deps=[ogf])
            oGs = []
            for ec in range(d.ERCH):
                oGs.append(P.add("t", lambda e, ec=ec: e.matmul(
                    CP[1][:, 48 + ec:48 + ec + 1],
                    sel_s[:, ec * 128:(ec + 1) * 128],
                    gfm_s[:], start=True, stop=True),
                    deps=[ogfc, ld_sel, owc]))
            oGc = P.add("v", lambda e: e.tensor_copy(
                G_s[:], CP[1][:, 48:48 + d.ERCH]), deps=oGs)
            p3_t_tail = oGs[-1]
            p3_v_tail = oGc

        _xpost_cm = nc.sbuf_tensor("xpost", [128, d.NT * d.C], F32,
                                   side="right")
        xpost = _xpost_cm.__enter__()
        with contextlib.ExitStack() as p4:
            T4 = lambda *a: p4.enter_context(nc.sbuf_tensor(*a))
            xs_t = [T4(f"xs4{i}", [128, d.C], F32) for i in range(4)]
            PP = [p4.enter_context(nc.psum_tensor(f"PP{i}", [128, 512],
                                                  F32)) for i in range(4)]
            xs_free = [None] * 4
            pslot_free = [p3_v_tail] * 4
            xpw = []
            for tt in range(d.NT):
                lx = P.add("sd", lambda e, tt=tt: e.dma_start(
                    out=xs_t[tt % 4][:], in_=xst[tt * 128:(tt + 1) * 128, :]),
                    deps=[xs_free[tt % 4], p3_t_tail, p3_v_tail],
                    stream=f"xp{tt % 2}")
                evs = []
                for fc in range(d.NFC):
                    slot = (tt * d.NFC + fc) % 4
                    ps = PP[slot]
                    mm = None
                    for c in range(d.CH):
                        mm = P.add("t", lambda e, c=c, tt=tt, fc=fc, ps=ps:
                                   e.matmul(
                            ps[:, 0:d.FC],
                            ofm[:, c * d.TPC + tt * 128:
                                c * d.TPC + (tt + 1) * 128],
                            wp_sb[:, c * d.C + fc * d.FC:
                                  c * d.C + (fc + 1) * d.FC],
                            start=(c == 0), stop=False),
                            deps=[att_v, ld_wp, pslot_free[slot],
                                  p3_t_tail])
                    mb = P.add("t", lambda e, fc=fc, ps=ps: e.matmul(
                        ps[:, 0:d.FC], ones1f[:],
                        bp_s[0:1, fc * d.FC:(fc + 1) * d.FC],
                        start=False, stop=True), deps=[ld_bp, o_ones1])
                    oev = P.add("v", lambda e, tt=tt, fc=fc, ps=ps:
                                e.tensor_tensor(
                        out=xpost[:, tt * d.C + fc * d.FC:
                                  tt * d.C + (fc + 1) * d.FC],
                        in0=ps[:, 0:d.FC],
                        in1=xs_t[tt % 4][:, fc * d.FC:(fc + 1) * d.FC],
                        op=ALU.add), deps=[mb, lx])
                    pslot_free[slot] = oev
                    evs.append(oev)
                xs_free[tt % 4] = evs[-1]
                xpw.append(evs[-1])

        _ofm_cm.__exit__(None, None, None)

        with contextlib.ExitStack() as p5:
            T5 = lambda *a: p5.enter_context(nc.sbuf_tensor(*a))
            xfm2 = T5("xfm2", [128, d.CH * d.TPC], BF16)
            xlnb2 = [T5(f"xlnb2{i}", [128, d.C], BF16) for i in range(2)]
            sc5 = {"s1": T5("s51", [128, 4], F32),
                   "jnk": T5("s5jnk", [128, d.C], BF16),
                   "s2": T5("s52", [128, 4], F32),
                   "sqv": T5("s5sqv", [128, d.C], F32),
                   "s6": T5("s56", [128, 4], F32),
                   "z": cst[0:128, 1:2], "oz": oc2}
            mean2 = T5("mean2", [128, d.NT], F32)
            var2 = T5("var2", [128, d.NT], F32)
            rstd2 = T5("rstd2", [128, d.NT], F32)
            mur2 = T5("mur2", [128, d.NT], F32)
            r1_2 = T5("r1_2", [128, d.NT], F32)
            _ab_cm = nc.sbuf_tensor("a_sb", [d.R, d.E * d.C], BF16,
                                    side="right")
            a_sb = _ab_cm.__enter__()
            _bb_cm = nc.sbuf_tensor("b_sb", [128, d.E * d.CH * d.R],
                                    BF16, side="right")
            b_sb = _bb_cm.__enter__()
            aT_s = T5("aT_s", [128, d.CH * d.ERCH * 128], BF16)
            bT_s = T5("bT_s", [128, d.ERCH * d.C], BF16)
            mid_s = T5("mid_s", [128, d.ERCH * d.TPC], BF16)
            xo = [T5(f"xo5{i}", [128, d.C], F32) for i in range(2)]
            sc1b = [T5(f"sc1b{i}", [128, d.FC], F32) for i in range(2)]
            sc2b = [T5(f"sc2b{i}", [128, d.FC], F32) for i in range(2)]

            FSG = [p5.enter_context(nc.psum_tensor(f"FSG{i}", [128, 1024],
                                                   F32)) for i in range(2)]
            PT5 = [p5.enter_context(nc.psum_tensor(f"PT5{i}", [128, 512],
                                                   BF16)) for i in range(2)]

            ld_a = P.add("gd", lambda e: e.dma_start(
                out=a_sb.rearrange("r (e h) -> r e h", e=d.E),
                in_=aw.rearrange("e r h -> r e h")),
                deps=[p3_t_tail, p3_v_tail, xpw[-1]], stream="ab")
            ld_bs = []
            for e_ in range(d.E):
                ld_bs.append(P.add("gd", lambda e, e_=e_: e.dma_start(
                    out=b_sb[:, e_ * d.CH * d.R:(e_ + 1) * d.CH * d.R]
                    .rearrange("p (c r) -> p c r", c=d.CH),
                    in_=bw[e_].rearrange("(c p) r -> p c r", p=128)),
                    stream="ab"))

            ln_done = []
            for t in range(d.NT):
                ln_done.append(ln_pass1(
                    P, d, xpost[:, t * d.C:(t + 1) * d.C], mean2, var2, t,
                    [xpw[t]] + ([p3_t_tail, xpw[-1]] if t == 0 else []),
                    sc5))
            oE2, oM2 = ln_finish(P, d, mean2, var2, rstd2, mur2, ln_done,
                                 cst, oc2)
            oR1 = P.add("v", lambda e: e.tensor_scalar(
                out=r1_2[:], in0=rstd2[:], scalar1=1.0, scalar2=None,
                op0=ALU.add), deps=[oE2])

            bank_ev = [None, None]
            xln_rd = [None, None]
            xfm_done_sv = [None, None]
            aT_done_sv = [None, None]
            bT_done_sv = [None, None]
            bsz = min(4, d.CH)
            nblk = d.CH // bsz
            bi = 0
            for t in range(d.NT):
                oxl = P.add("v", lambda e, t=t: e.tensor_scalar(
                    out=xlnb2[t % 2][:], in0=xpost[:, t * d.C:(t + 1) * d.C],
                    scalar1=rstd2[:, t:t + 1], scalar2=mur2[:, t:t + 1],
                    op0=ALU.mult, op1=ALU.subtract),
                    deps=[oE2, oM2, xln_rd[t % 2]])
                for blk in range(nblk):
                    hb = bi % 2
                    pt = PT5[hb]
                    last_tp = None
                    for j in range(bsz):
                        c = blk * bsz + j
                        last_tp = P.add("t", lambda e, t=t, c=c, j=j, pt=pt:
                                        e.transpose(
                            pt[:, j * 128:(j + 1) * 128],
                            xlnb2[t % 2][:, c * 128:(c + 1) * 128],
                            idn_s[:]),
                            deps=([oxl, ld_id, bank_ev[hb]] if j == 0
                                  else [oxl]))
                    dst = lambda t=t, blk=blk: xfm2.rearrange(
                        "p (c x) -> p c x", c=d.CH)[
                        :, blk * bsz:(blk + 1) * bsz,
                        t * 128:(t + 1) * 128]
                    src = lambda pt=pt: pt[:, 0:bsz * 128].rearrange(
                        "p (c x) -> p c x", x=128)
                    if bi % 2 == 0:
                        ev = P.add("s", lambda e, dst=dst, src=src:
                                   e.activation(
                            out=dst(), in_=src(), func=AF.Copy),
                            deps=[last_tp])
                    else:
                        ev = P.add("v", lambda e, dst=dst, src=src:
                                   e.tensor_copy(dst(), src()),
                            deps=[last_tp])
                    bank_ev[hb] = ev
                    xfm_done_sv[bi % 2] = ev
                    bi += 1
                xln_rd[t % 2] = last_tp

            for e_ in range(d.E):
                hb = bi % 2
                pt = PT5[hb]
                last_tp = None
                for c in range(d.CH):
                    last_tp = P.add("t", lambda e, e_=e_, c=c, pt=pt:
                                    e.transpose(
                        pt[:, c * d.R:(c + 1) * d.R],
                        a_sb[:, e_ * d.C + c * 128:e_ * d.C + (c + 1) * 128],
                        idn_s[0:d.R, 0:d.R]),
                        deps=[ld_a, ld_id] + ([bank_ev[hb]] if c == 0
                                              else []))
                off = (e_ // d.EPC) * 128 + (e_ % d.EPC) * d.R
                dst = lambda off=off: aT_s.rearrange(
                    "p (c x) -> p c x", x=d.ERCH * 128)[:, :, off:off + d.R]
                src = lambda pt=pt: pt[:, 0:d.CH * d.R].rearrange(
                    "p (c x) -> p c x", x=d.R)
                if bi % 2 == 0:
                    ev = P.add("s", lambda e, dst=dst, src=src: e.activation(
                        out=dst(), in_=src(), func=AF.Copy), deps=[last_tp])
                else:
                    ev = P.add("v", lambda e, dst=dst, src=src:
                               e.tensor_copy(dst(), src()), deps=[last_tp])
                bank_ev[hb] = ev
                aT_done_sv[bi % 2] = ev
                bi += 1
            for e_ in range(d.E):
                for blk in range(nblk):
                    hb = bi % 2
                    pt = PT5[hb]
                    last_tp = None
                    for j in range(bsz):
                        c = blk * bsz + j
                        last_tp = P.add("t", lambda e, e_=e_, c=c, j=j,
                                        pt=pt: e.transpose(
                            pt[0:d.R, j * 128:(j + 1) * 128],
                            b_sb[:, e_ * d.CH * d.R + c * d.R:
                                 e_ * d.CH * d.R + (c + 1) * d.R],
                            idn_s[:]),
                            deps=[ld_bs[e_], ld_id]
                            + ([bank_ev[hb]] if j == 0 else []))
                    row = (e_ % d.EPC) * d.R
                    col = (e_ // d.EPC) * d.C + blk * bsz * 128
                    w_ = bsz * 128
                    if bi % 2 == 0:
                        ev = P.add("s", lambda e, row=row, col=col, w_=w_,
                                   pt=pt: e.activation(
                            out=bT_s[row:row + d.R, col:col + w_],
                            in_=pt[0:d.R, 0:w_], func=AF.Copy),
                            deps=[last_tp])
                    else:
                        ev = P.add("v", lambda e, row=row, col=col, w_=w_,
                                   pt=pt: e.tensor_copy(
                            bT_s[row:row + d.R, col:col + w_],
                            pt[0:d.R, 0:w_]), deps=[last_tp])
                    bank_ev[hb] = ev
                    bT_done_sv[bi % 2] = ev
                    bi += 1

            _bb_cm.__exit__(None, None, None)
            _ab_cm.__exit__(None, None, None)
            slot_free = [bT_done_sv[0], aT_done_sv[0],
                         bT_done_sv[1], aT_done_sv[1]]
            mi = 0
            mid_done = None
            for ec in range(d.ERCH):
                for tc in range(d.NTC):
                    slot = mi % 4
                    ps = FSG[slot // 2][:, (slot % 2) * 512:
                                        (slot % 2) * 512 + 512]
                    mm = None
                    for c in range(d.CH):
                        mm = P.add("t", lambda e, ec=ec, tc=tc, c=c, ps=ps:
                                   e.matmul(
                            ps, aT_s[:, (c * d.ERCH + ec) * 128:
                                     (c * d.ERCH + ec + 1) * 128],
                            xfm2[:, c * d.TPC + tc * 512:
                                 c * d.TPC + (tc + 1) * 512],
                            start=(c == 0), stop=(c == d.CH - 1)),
                            deps=xfm_done_sv + aT_done_sv
                            + [slot_free[slot]])
                    oev = P.add("v", lambda e, ec=ec, tc=tc, ps=ps:
                                e.tensor_scalar(
                        out=mid_s[:, ec * d.TPC + tc * 512:
                                  ec * d.TPC + (tc + 1) * 512],
                        in0=ps, scalar1=G_s[:, ec:ec + 1], scalar2=None,
                        op0=ALU.mult), deps=[mm, p3_v_tail])
                    slot_free[slot] = oev
                    mid_done = oev
                    mi += 1

            xo_free = [None, None]
            ps_free = [mid_done] * 4
            prev_oo = [None, None]
            ei = 0
            for tt in range(d.NT):
                evs = []
                for fc in range(d.NFC):
                    slot = ei % 4
                    ps = FSG[slot // 2][:, (slot % 2) * 512:
                                        (slot % 2) * 512 + d.FC]
                    mm = None
                    for ec in range(d.ERCH):
                        mm = P.add("t", lambda e, tt=tt, fc=fc, ec=ec,
                                   ps=ps: e.matmul(
                            ps, mid_s[:, ec * d.TPC + tt * 128:
                                      ec * d.TPC + (tt + 1) * 128],
                            bT_s[:, ec * d.C + fc * d.FC:
                                 ec * d.C + (fc + 1) * d.FC],
                            start=(ec == 0), stop=(ec == d.ERCH - 1)),
                            deps=[mid_done] + bT_done_sv
                            + [ps_free[slot]])
                    oa = P.add("v", lambda e, tt=tt, ps=ps, slot=slot:
                               e.tensor_scalar(
                        out=sc1b[slot % 2][:, 0:d.FC], in0=ps,
                        scalar1=mur2[:, tt:tt + 1], scalar2=None,
                        op0=ALU.subtract),
                        deps=[mm, oM2, prev_oo[slot % 2]])
                    ps_free[slot] = oa
                    ob = P.add("v", lambda e, tt=tt, fc=fc, slot=slot:
                               e.tensor_scalar(
                        out=sc2b[slot % 2][:, 0:d.FC],
                        in0=xpost[:, tt * d.C + fc * d.FC:
                                  tt * d.C + (fc + 1) * d.FC],
                        scalar1=r1_2[:, tt:tt + 1], scalar2=None,
                        op0=ALU.mult), deps=[oR1, prev_oo[slot % 2]])
                    oo = P.add("v", lambda e, tt=tt, fc=fc, slot=slot:
                               e.tensor_tensor(
                        out=xo[tt % 2][:, fc * d.FC:(fc + 1) * d.FC],
                        in0=sc1b[slot % 2][:, 0:d.FC],
                        in1=sc2b[slot % 2][:, 0:d.FC],
                        op=ALU.add), deps=[oa, ob, xo_free[tt % 2]])
                    prev_oo[slot % 2] = oo
                    evs.append(oo)
                    ei += 1
                od = P.add("gd", lambda e, tt=tt: e.dma_start(
                    out=out[tt * 128:(tt + 1) * 128, :],
                    in_=xo[tt % 2][:]), deps=evs, stream="out")
                xo_free[tt % 2] = od

        _xpost_cm.__exit__(None, None, None)
        P.emit(nc)
    return nc


def host_consts(d):
    i32 = np.arange(0, d.D, 2) / d.D
    theta = (1.0 / (10000.0 ** i32)) * 3.0
    pos = np.arange(d.L, dtype=np.float64)
    ang = pos[:, None] * theta[None, :]
    cos = np.cos(ang)
    sin = np.sin(ang)
    p = np.arange(128)
    pair = (p % d.D) // 2
    sgn = np.where(p % 2 == 0, -1.0, 1.0)
    cosF = np.ascontiguousarray(cos[:, pair].T.astype(np.float32))
    sinF = np.ascontiguousarray((sin[:, pair] * sgn[None, :]).T
                                .astype(np.float32))
    idn = np.eye(128, dtype=BF)
    idf = np.eye(128, dtype=np.float32)
    sel = np.zeros((d.E, d.ERCH * 128), np.float32)
    for e_ in range(d.E):
        ec = e_ // d.EPC
        j0 = (e_ % d.EPC) * d.R
        sel[e_, ec * 128 + j0: ec * 128 + j0 + d.R] = 1.0
    return cosF, sinF, idn, idf, sel


def run_fused(d, x, wqkv, wproj, bproj, wr, aw, bw):
    ncores = d.NB * d.G
    cores = list(range(ncores))
    cosF, sinF, idn, idf, sel = host_consts(d)
    nc = build_fused(d)
    im = []
    for c in cores:
        b, r = c // d.G, c % d.G
        off = r * d.TPC
        im.append({
            "xb": np.ascontiguousarray(x[b]),
            "xs": np.ascontiguousarray(x[b, off:off + d.TPC]),
            "wqkv": wqkv, "wproj": wproj, "bproj": bproj,
            "cosK": cosF, "sinK": sinF,
            "cosQ": np.ascontiguousarray(cosF[:, off:off + d.TPC]),
            "sinQ": np.ascontiguousarray(sinF[:, off:off + d.TPC]),
            "wr": wr, "aw": aw, "bw": bw, "sel": sel,
            "idn": idn, "idf": idf,
        })
    r_ = run_bass_kernel_spmd(nc, im, core_ids=cores)
    outp = np.empty((d.NB, d.L, d.C), np.float32)
    for c in cores:
        outp[c // d.G, (c % d.G) * d.TPC:(c % d.G + 1) * d.TPC] = \
            np.asarray(r_.results[c]["out"], np.float32)
    return outp


def kernel(L0_attn_x, L0_attn_w_qkv, L0_attn_w_proj, L0_attn_b_proj,
           L0_ffn_Wr, L0_ffn_A, L0_ffn_Bw):
    d = dd(dims_real())
    x = np.ascontiguousarray(np.asarray(L0_attn_x, np.float32))
    wqkv = np.ascontiguousarray(np.asarray(L0_attn_w_qkv, np.float32))
    wproj = np.ascontiguousarray(np.asarray(L0_attn_w_proj, np.float32))
    bproj = np.ascontiguousarray(
        np.asarray(L0_attn_b_proj, np.float32).reshape(1, d.C))
    wr = np.ascontiguousarray(np.asarray(L0_ffn_Wr, np.float32))
    aw = np.ascontiguousarray(np.asarray(L0_ffn_A, np.float32))
    bw = np.ascontiguousarray(np.asarray(L0_ffn_Bw, np.float32))
    return run_fused(d, x, wqkv, wproj, bproj, wr, aw, bw)
